# version 01
# baseline (speedup 1.0000x reference)
import numpy as np
import ml_dtypes
import concourse.bass as bass
import concourse.mybir as mybir
from concourse.bass_utils import run_bass_kernel_spmd
from contextlib import ExitStack

F32 = mybir.dt.float32
BF16 = mybir.dt.bfloat16
AF = mybir.ActivationFunctionType
ALU = mybir.AluOpType
AX = mybir.AxisListType

D = 2048
T = 2048
O = 1024
KD = 16
EPS = 1e-5
KSP = 204
NHA = 32
NHB = 16
INTER = 8192
LAM_INIT = 0.2
NBIS = 26
RSQ2 = float(1.0 / np.sqrt(2.0))

DEBUG_STOP = None
DEBUG_TAPS = ()


def build_program():
    nc = bass.Bass("TRN2", target_bir_lowering=False)

    def I(name, shape, dtype=F32):
        return nc.dram_tensor(name, list(shape), dtype, kind="ExternalInput")

    xT_own = I("xT_own", [D, O])
    xT_oth = I("xT_oth", [D, O])
    aW1T = I("aW1T", [D, 128])
    ab1 = I("ab1", [128, 1])
    aW2T = I("aW2T", [128, 64])
    ab2d = I("ab2d", [128, 1])
    aW3c = I("aW3c", [64, 1])
    ab3 = I("ab3", [1, 1])
    aWqkvT = I("aWqkvT", [D, 192])
    abqkv = I("abqkv", [128, 2])
    aWoutT = I("aWoutT", [64, D])
    about_pk = I("about_pk", [128, KD])
    agam_pk = I("agam_pk", [128, KD])
    abet_pk = I("abet_pk", [128, KD])
    bWattn = I("bWattn", [D, 3 * D])
    bWproj = I("bWproj", [D, D])
    blq1 = I("blq1", [1, D]); blk1 = I("blk1", [1, D])
    blq2 = I("blq2", [1, D]); blk2 = I("blk2", [1, D])
    fgupT = I("fgupT", [D, 2 * INTER])
    fdnT = I("fdnT", [INTER, D])

    y = nc.dram_tensor("y", [D, O], F32, kind="ExternalOutput")

    taps = {}

    def tap(name, shape, dtype=F32):
        if name in DEBUG_TAPS:
            taps[name] = nc.dram_tensor("tap_" + name, list(shape), dtype,
                                        kind="ExternalOutput")
        return taps.get(name)

    gup16 = nc.dram_tensor("gup16", [D, 2 * INTER], BF16, kind="Internal")
    dn16 = nc.dram_tensor("dn16", [INTER, D], BF16, kind="Internal")
    x1of = nc.dram_tensor("x1of", [D, O], F32, kind="Internal")
    x1pb = nc.dram_tensor("x1pb", [D, O], BF16, kind="Internal")
    x2of = nc.dram_tensor("x2of", [D, O], F32, kind="Internal")
    x2pb = nc.dram_tensor("x2pb", [D, O], BF16, kind="Internal")
    x3of = nc.dram_tensor("x3of", [D, O], F32, kind="Internal")
    kvs = nc.dram_tensor("kvs", [4096, T], BF16, kind="Internal")

    ctx = ExitStack()
    E = ctx.enter_context
    with ctx:
        rows = E(nc.sbuf_tensor("rows", [1, 4 * T], F32))

        def ROW(i, t0=0, w=T):
            return rows[0:1, i * T + t0: i * T + t0 + w]
        srow = E(nc.sbuf_tensor("srow", [1, 4 * 512], F32))

        def SR(i, w=512):
            return srow[0:1, 512 * i: 512 * i + w]
        onescol = E(nc.sbuf_tensor("onescol", [128, 1], F32))
        onescol_b = E(nc.sbuf_tensor("onescol_b", [128, 1], BF16))
        onesrow = E(nc.sbuf_tensor("onesrow", [1, 128], F32))
        onesrow_b = E(nc.sbuf_tensor("onesrow_b", [1, 128], BF16))
        onesTrow = E(nc.sbuf_tensor("onesTrow", [1, T], BF16))
        idn = E(nc.sbuf_tensor("idn", [128, 128], BF16))
        noti = E(nc.sbuf_tensor("noti", [128, 128], BF16))
        dupm = E(nc.sbuf_tensor("dupm", [32, 64], F32))
        sel64 = E(nc.sbuf_tensor("sel64", [3, 32, 64], BF16))
        basis = E(nc.sbuf_tensor("basis", [1, 32, 32], BF16))
        smalls = E(nc.sbuf_tensor("smalls", [128, 16], F32))
        bdca = E(nc.sbuf_tensor("bdca", [128, 32], F32))
        w2t = E(nc.sbuf_tensor("w2t", [128, 2, 64], BF16))
        w3s = E(nc.sbuf_tensor("w3s", [128, 64, 128], BF16))
        wot = E(nc.sbuf_tensor("wot", [64, D], BF16))
        gpk = E(nc.sbuf_tensor("gpk", [128, KD], F32))
        bpkf = E(nc.sbuf_tensor("bpkf", [128, KD], F32))
        bpk16 = E(nc.sbuf_tensor("bpk16", [128, KD], BF16))
        outpk = E(nc.sbuf_tensor("outpk", [128, KD], F32))

        block = E(nc.Block())

        CHAINED = ("ve", "sc", "gp")
        NOCHAIN = {"dma_start", "wait_ge", "dma_start_transpose"}
        chain_counts = {"ve": 0, "sc": 0, "gp": 0}

        class InstHandle:
            __slots__ = ("incs", "engine", "chain_val", "mark")

            def __init__(self, engine, chain_val):
                self.incs = []
                self.engine = engine
                self.chain_val = chain_val
                self.mark = None

            def then_inc(self, sem, k=1):
                self.incs.append((sem, k))
                return self

        class EngQ:
            def __init__(self, name):
                self.name = name
                self.q = []

            def __getattr__(self, attr):
                def call(*a, **kw):
                    cv = None
                    if self.name in CHAINED and attr not in NOCHAIN:
                        chain_counts[self.name] += 1
                        cv = chain_counts[self.name]
                    h = InstHandle(self.name, cv)
                    self.q.append((attr, a, kw, h))
                    return h
                return call

        sy, gp, ve, sc, te = EngQ("sy"), EngQ("gp"), EngQ("ve"), EngQ("sc"), EngQ("te")
        ALLE = (sy, gp, ve, sc, te)
        chain_sems = {n: E(nc.semaphore("ch_" + n)) for n in CHAINED}

        def flush():
            regs = ((sy, block.sync), (gp, block.gpsimd), (ve, block.vector),
                    (sc, block.scalar), (te, block.tensor))
            for proxy, reg in regs:
                q = proxy.q
                csem = chain_sems.get(proxy.name)

                def runner(e, q=q, csem=csem):
                    for attr, a, kw, h in q:
                        if h.chain_val is not None and h.chain_val > 1:
                            e.wait_ge(csem, h.chain_val - 1)
                        inst = getattr(e, attr)(*a, **kw)
                        if h.chain_val is not None:
                            inst.then_inc(csem, 1)
                        for sem, k in h.incs:
                            inst.then_inc(sem, k)
                reg(runner)
                proxy.q = []

        sem_pool = [E(nc.semaphore(f"sp{i}")) for i in range(88)]
        pool_state = {}
        pool_i = [0]

        class Ctr:
            def __init__(self):
                assert pool_i[0] < len(sem_pool), "semaphore pool exhausted"
                self.sem = sem_pool[pool_i[0]]
                pool_i[0] += 1
                self.base = pool_state.get(id(self.sem), 0)
                self.n = self.base
                self.mk = {}

            def inc(self, h, k=1):
                self.n += k
                if getattr(h, "chain_val", None) is not None:
                    self.mk[self.n] = (chain_sems[h.engine], h.chain_val)
                else:
                    h.then_inc(self.sem, k)
                    h.mark = (self.sem, self.n)
                    pool_state[id(self.sem)] = max(pool_state.get(id(self.sem), 0), self.n)
                return self.n

            def tinc(self, k=1):
                assert te.q, "tinc with empty tensor queue"
                h = te.q[-1][3]
                if h.mark is not None:
                    self.n += k
                    self.mk[self.n] = h.mark
                    return self.n
                return self.inc(h, k)

            def dma(self, h):
                return self.inc(h, 16)

            def wait(self, eng, n=None):
                n = self.n if n is None else n
                if n <= self.base:
                    return
                if n in self.mk:
                    sem, val = self.mk[n]
                    eng.wait_ge(sem, val)
                else:
                    eng.wait_ge(self.sem, n)

        sB = Ctr()
        bar_gen = [0]
        bar_set = {}
        bar_seen = {id(e): -1 for e in ALLE}

        sTbar = Ctr()

        def barrier(*workers):
            for e in workers:
                if e is te:
                    if not te.q:
                        continue
                    h = te.q[-1][3]
                    if h.mark is not None:
                        bar_set[("te",)] = h.mark
                    else:
                        v = sTbar.inc(h)
                        bar_set[("te",)] = (sTbar.sem, v)
                    continue
                h = e.nop()
                if h.chain_val is not None:
                    sem = chain_sems[h.engine]
                    bar_set[id(sem)] = (sem, h.chain_val)
                else:
                    h.then_inc(sB.sem, 1)
                    sB.n += 1
                    bar_set[id(sB.sem)] = (sB.sem, sB.n)
            bar_gen[0] += 1

        def bwait(*engs):
            for e in engs:
                if bar_seen[id(e)] < bar_gen[0]:
                    for sem, val in bar_set.values():
                        e.wait_ge(sem, val)
                    bar_seen[id(e)] = bar_gen[0]

        def emit_zero_output_and_stop():
            sE = Ctr()
            bwait(sy, ve)
            with nc.sbuf_tensor("zz", [128, O], F32) as zz:
                sVz = Ctr()
                sVz.inc(ve.memset(zz[:, :], 0.0))
                sVz.wait(sy)
                for k in range(KD):
                    sE.dma(sy.dma_start(out=y[128 * k:128 * k + 128, :], in_=zz[:, :]))
                sE.wait(sy)

        sIn = Ctr(); sCa = Ctr(); sV0 = Ctr(); sT0 = Ctr(); sG0 = Ctr(); sS0 = Ctr()
        ve.memset(smalls[:, :], 0.0)
        ve.memset(rows[:, :], 0.0)
        ve.memset(srow[:, :], 0.0)
        sV0.inc(ve.memset(bdca[:, :], 0.0))
        sV0.wait(sy)
        sIn.dma(sy.dma_start(out=gpk[:, :], in_=agam_pk[:, :]))
        sIn.dma(sy.dma_start(out=bpkf[:, :], in_=abet_pk[:, :]))
        sIn.dma(sy.dma_start(out=outpk[:, :], in_=about_pk[:, :]))
        sIn.dma(sy.dma_start(out=smalls[:, 0:1], in_=ab1[:, :]))
        sIn.dma(sy.dma_start(out=smalls[:, 1:2], in_=ab2d[:, :]))
        sIn.dma(sy.dma_start(out=smalls[:, 2:4], in_=abqkv[:, :]))
        sIn.dma(sy.dma_start(out=smalls[0:1, 4:5], in_=ab3[:, :]))
        sIn.dma(sy.dma_start(out=smalls[0:64, 5:6], in_=aW3c[:, :]))
        sIn.dma(sy.dma_start(out=ROW(0, 0, D), in_=blq1[:, :]))
        sIn.dma(sy.dma_start(out=ROW(1, 0, D), in_=blk1[:, :]))
        sIn.dma(sy.dma_start(out=ROW(2, 0, D), in_=blq2[:, :]))
        sIn.dma(sy.dma_start(out=ROW(3, 0, D), in_=blk2[:, :]))

        NCB = 3
        cstack = ExitStack()
        cbuf = cstack.enter_context(nc.sbuf_tensor("cbuf", [64, NCB, 4096], BF16, side="right"))
        wstack = ExitStack()
        w1t = wstack.enter_context(nc.sbuf_tensor("w1t", [128, KD, 128], BF16, side="right"))
        wqt = wstack.enter_context(nc.sbuf_tensor("wqt", [128, KD, 192], BF16, side="right"))
        for k in range(KD):
            sCa.dma(gp.dma_start(out=w1t[:, k, :], in_=aW1T[128 * k:128 * k + 128, :]))
            sCa.dma(gp.dma_start(out=wqt[:, k, :], in_=aWqkvT[128 * k:128 * k + 128, :]))
        sCa.dma(gp.dma_start(out=w2t[:, 0, :], in_=aW2T[:, :]))
        sCa.dma(gp.dma_start(out=wot[:, :], in_=aWoutT[:, :]))

        ve.memset(smalls[:, 13:14], float(EPS))
        ve.memset(onescol[:, :], 1.0)
        ve.memset(onescol_b[:, :], 1.0)
        ve.memset(onesrow[:, :], 1.0)
        ve.memset(onesrow_b[:, :], 1.0)
        ve.memset(onesTrow[:, :], 1.0)
        sV0.inc(ve.memset(w3s[:, :, :], 0.0))
        MEMSET_DONE = sV0.n
        gp.memset(idn[:, :], 1.0)
        sG0.inc(gp.affine_select(idn[:, :], idn[:, :], [[1, 128]], ALU.is_equal, 0.0,
                                 base=0, channel_multiplier=-1))
        gp.memset(noti[:, :], 1.0)
        sG0.inc(gp.affine_select(noti[:, :], noti[:, :], [[1, 128]], ALU.not_equal, 0.0,
                                 base=0, channel_multiplier=-1))
        gp.memset(dupm[:, :], 1.0)
        gp.affine_select(dupm[:, :], dupm[:, :], [[1, 64]], ALU.is_ge, 0.0,
                         base=0, channel_multiplier=-2)
        sG0.inc(gp.affine_select(dupm[:, :], dupm[:, :], [[-1, 64]], ALU.is_ge, 0.0,
                                 base=1, channel_multiplier=2))
        gp.memset(sel64[:, :, :], 1.0)
        gp.affine_select(sel64[:, :, :], sel64[:, :, :], [[-2, 32], [1, 64]],
                         ALU.is_equal, 0.0, base=1, channel_multiplier=-1)
        sG0.inc(gp.affine_select(sel64[:, :, :], sel64[:, :, :], [[0, 32], [0, 64]],
                                 ALU.is_ge, 0.0, base=-1, channel_multiplier=1))
        gp.memset(basis[:, :, :], 1.0)
        sG0.inc(gp.affine_select(basis[:, :, :], basis[:, :, :], [[-1, 32], [1, 32]],
                                 ALU.is_equal, 0.0, base=0, channel_multiplier=0))

        if DEBUG_STOP == "ld":
            with nc.sbuf_tensor("zzl", [128, O], F32) as zz:
                sVz = Ctr()
                sVz.inc(ve.memset(zz[:, :], 0.0))
                sVz.wait(sy)
                sIn.wait(sy); sCa.wait(sy)
                sV0.wait(sy); sG0.wait(sy)
                sE = Ctr()
                tg = tap("gpk", [128, 16], F32)
                if tg is not None:
                    sE.dma(sy.dma_start(out=tg[:, :], in_=gpk[:, :]))
                tp = tap("p0", [128, 16], F32)
                if tp is not None:
                    sE.dma(sy.dma_start(out=tp[:, :], in_=smalls[:, :]))
                for k in range(KD):
                    sE.dma(sy.dma_start(out=y[128 * k:128 * k + 128, :], in_=zz[:, :]))
                sE.wait(sy)
            wstack.close()
            cstack.close()
            flush()
            return nc, taps

        sIn.wait(ve)
        lr = ROW(0, 0, D)
        ve.tensor_tensor(lr, lr, ROW(1, 0, D), ALU.mult)
        ve.tensor_reduce(smalls[0:1, 7:8], lr, AX.X, ALU.add)
        ve.tensor_tensor(ROW(2, 0, D), ROW(2, 0, D), ROW(3, 0, D), ALU.mult)
        sV0.inc(ve.tensor_reduce(smalls[0:1, 8:9], ROW(2, 0, D), AX.X, ALU.add))
        sV0.wait(sc)
        sc.activation(smalls[0:1, 9:10], smalls[0:1, 7:8], AF.Exp)
        sS0.inc(sc.activation(smalls[0:1, 10:11], smalls[0:1, 8:9], AF.Exp))
        sS0.wait(ve)
        ve.tensor_tensor(smalls[0:1, 11:12], smalls[0:1, 9:10], smalls[0:1, 10:11],
                         ALU.subtract)
        ve.tensor_scalar(smalls[0:1, 11:12], smalls[0:1, 11:12], float(LAM_INIT),
                         None, ALU.add)
        sV0.inc(ve.tensor_scalar(smalls[0:1, 12:13], smalls[0:1, 11:12], -1.0,
                                 None, ALU.mult))

        with nc.psum_tensor("p0a", [128, 4], F32) as p0a:
            sCa.wait(te); sIn.wait(te)
            sV0.wait(te, MEMSET_DONE)
            sG0.wait(te); sG0.wait(ve); sG0.wait(sc)
            sV0.inc(ve.tensor_copy(bpk16[:, :], bpkf[:, :]))
            sV0.wait(te)
            for k in range(KD):
                te.matmul(p0a[0:128, 0:1], wqt[:, k, 0:128], bpk16[:, k:k + 1],
                          start=(k == 0), stop=(k == KD - 1))
            for k in range(KD):
                te.matmul(p0a[0:64, 1:2], wqt[:, k, 128:192], bpk16[:, k:k + 1],
                          start=(k == 0), stop=(k == KD - 1))
            sV0.wait(te)
            te.matmul(p0a[:, 2:3], onesrow[0:1, :], smalls[0:1, 4:5],
                      start=True, stop=True)
            te.matmul(p0a[0:64, 3:4], onesrow[0:1, 0:64], smalls[0:1, 12:13],
                      start=True, stop=True)
            sT0.tinc()
            sT0.wait(ve)
            ve.tensor_tensor(smalls[:, 2:3], smalls[:, 2:3], p0a[0:128, 0:1], ALU.add)
            ve.tensor_tensor(smalls[0:64, 3:4], smalls[0:64, 3:4], p0a[0:64, 1:2], ALU.add)
            ve.tensor_copy(smalls[:, 4:5], p0a[:, 2:3])
            ve.tensor_copy(smalls[0:64, 6:7], p0a[0:64, 3:4])
            for k in range(KD):
                ve.tensor_scalar(w1t[:, k, :], w1t[:, k, :], gpk[:, k:k + 1], None, ALU.mult)
                ve.tensor_scalar(wqt[:, k, :], wqt[:, k, :], gpk[:, k:k + 1], None, ALU.mult)
            ve.tensor_scalar(w2t[:, 1, :], w2t[:, 0, :], -1.0, None, ALU.mult)
            sV0.wait(ve)
            ve.tensor_copy(w3s[0:64, 0, 0:1], smalls[0:64, 5:6])
            for t in range(64):
                if t > 0:
                    ve.tensor_copy(w3s[0:64, t, 2 * t:2 * t + 1], w3s[0:64, 0, 0:1])
                ve.tensor_copy(w3s[64:128, t, 2 * t + 1:2 * t + 2], w3s[0:64, 0, 0:1])
            sV0.inc(ve.nop())

        if DEBUG_STOP == "p0b":
            with nc.sbuf_tensor("zzl2", [128, O], F32) as zz:
                sVz = Ctr()
                sVz.inc(ve.memset(zz[:, :], 0.0))
                sVz.wait(sy)
                sV0.wait(sy); sT0.wait(sy); sS0.wait(sy)
                sE = Ctr()
                tp = tap("p0", [128, 16], F32)
                if tp is not None:
                    sE.dma(sy.dma_start(out=tp[:, :], in_=smalls[:, :]))
                for k in range(KD):
                    sE.dma(sy.dma_start(out=y[128 * k:128 * k + 128, :], in_=zz[:, :]))
                sE.wait(sy)
            wstack.close()
            cstack.close()
            flush()
            return nc, taps

        sFW = Ctr()
        sFL = Ctr()
        segs = []
        for r in range(0, D, 64):
            for cc in range(4):
                segs.append((fgupT, gup16, r, 4096 * cc, 4096))
        for r in range(0, INTER, 64):
            segs.append((fdnT, dn16, r, 0, 2048))
        marks = []
        for si, (srcd, dstd, r, c0, w) in enumerate(segs):
            bi = si % NCB
            if si >= NCB and si % NCB == 0:
                sFW.wait(gp, marks[si - 1])
            sFL.dma(gp.dma_start(out=cbuf[:, bi, 0:w], in_=srcd[r:r + 64, c0:c0 + w]))
            sFL.wait(gp)
            sFW.dma(gp.dma_start(out=dstd[r:r + 64, c0:c0 + w], in_=cbuf[:, bi, 0:w]))
            marks.append(sFW.n)
        FW_DONE = sFW.n

        barrier(ve, te, sc, sy)

        t_ = tap("p0", [128, 16], F32)
        if t_ is not None:
            s_ = Ctr(); bwait(sy)
            s_.dma(sy.dma_start(out=t_[:, :], in_=smalls[:, :]))
            s_.wait(sy)
        t_ = tap("gpk", [128, 16], F32)
        if t_ is not None:
            s_ = Ctr(); bwait(sy)
            s_.dma(sy.dma_start(out=t_[:, :], in_=gpk[:, :]))
            s_.wait(sy)
        t_ = tap("rows01", [1, 4 * T], F32)
        if t_ is not None:
            s_ = Ctr(); bwait(sy)
            s_.dma(sy.dma_start(out=t_[:, :], in_=rows[:, :]))
            s_.wait(sy)
        if DEBUG_STOP == "p0":
            with nc.sbuf_tensor("zzp", [128, O], F32) as zz:
                sVz = Ctr()
                sVz.inc(ve.memset(zz[:, :], 0.0))
                sVz.wait(sy)
                sE = Ctr()
                for k in range(KD):
                    sE.dma(sy.dma_start(out=y[128 * k:128 * k + 128, :], in_=zz[:, :]))
                sE.wait(sy)
            wstack.close()
            cstack.close()
            flush()
            return nc, taps

        xna = wstack.enter_context(nc.sbuf_tensor("xna", [128, KD, T], BF16, side="right"))
        sl1a = ExitStack()
        with sl1a:
            x0c = sl1a.enter_context(nc.sbuf_tensor("x0c", [128, KD, 512], F32))
            sqc = sl1a.enter_context(nc.sbuf_tensor("sqc", [128, 2, 512], F32))
            psS1 = sl1a.enter_context(nc.psum_tensor("psS1", [1, 512], F32))
            psS2 = sl1a.enter_context(nc.psum_tensor("psS2", [1, 512], F32))
            psBC = sl1a.enter_context(nc.psum_tensor("psBC", [128, 1024], F32))
            sX = Ctr(); sVa = Ctr(); sTa = Ctr(); sSa = Ctr()
            bwait(sy, te, sc, ve)
            for c in range(4):
                src = xT_own if c < 2 else xT_oth
                col0 = (c % 2) * 512
                t0 = 512 * c
                for k in range(KD):
                    sX.dma(sy.dma_start(out=x0c[:, k, :],
                                        in_=src[128 * k:128 * k + 128, col0:col0 + 512]))
                sX.wait(sc); sX.wait(te)
                temark = sTa.n
                for k in range(KD):
                    if k >= 2:
                        sTa.wait(sc, temark + k - 1)
                    sSa.inc(sc.activation(sqc[:, k % 2, :], x0c[:, k, :], AF.Square))
                    te.matmul(psS1[:, :], onescol[:, :], x0c[:, k, :],
                              start=(k == 0), stop=(k == KD - 1))
                    sSa.wait(te)
                    te.matmul(psS2[:, :], onescol[:, :], sqc[:, k % 2, :],
                              start=(k == 0), stop=(k == KD - 1))
                    sTa.tinc()
                sTa.wait(ve)
                m, v, t1, v2 = SR(0), SR(1), SR(2), SR(3)
                ve.tensor_scalar(m, psS1[:, :], 1.0 / D, None, ALU.mult)
                ve.tensor_scalar(v, psS2[:, :], 1.0 / D, None, ALU.mult)
                ve.tensor_tensor(t1, m, m, ALU.mult)
                sVa.inc(ve.tensor_tensor(v, v, t1, ALU.subtract))
                sVa.wait(sc)
                sSa.inc(sc.activation(t1, v, AF.Sqrt, bias=smalls[0:1, 13:14]))
                sSa.wait(ve)
                ve.reciprocal(ROW(2, t0, 512), t1)
                ve.tensor_tensor(ROW(3, t0, 512), m, ROW(2, t0, 512), ALU.mult)
                ve.tensor_tensor(v2, v, ROW(2, t0, 512), ALU.mult)
                sVa.inc(ve.tensor_tensor(v2, v2, ROW(2, t0, 512), ALU.mult))
                sVa.wait(sc)
                sSa.inc(sc.activation(v2, v2, AF.Sqrt, bias=smalls[0:1, 13:14]))
                sSa.wait(ve)
                ve.reciprocal(v2, v2)
                ve.tensor_tensor(ROW(0, t0, 512), ROW(2, t0, 512), v2, ALU.mult)
                ve.tensor_tensor(ROW(1, t0, 512), ROW(3, t0, 512), v2, ALU.mult)
                sVa.inc(ve.nop())
                sVa.wait(te)
                te.matmul(psBC[:, 0:512], onesrow[:, :], ROW(0, t0, 512),
                          start=True, stop=True)
                te.matmul(psBC[:, 512:1024], onesrow[:, :], ROW(1, t0, 512),
                          start=True, stop=True)
                sTa.tinc()
                sTa.wait(ve)
                for k in range(KD):
                    ve.tensor_tensor(sqc[:, k % 2, :], x0c[:, k, :], psBC[:, 0:512],
                                     ALU.mult)
                    ve.tensor_tensor(xna[:, k, t0:t0 + 512], sqc[:, k % 2, :],
                                     psBC[:, 512:1024], ALU.subtract)
                sVa.inc(ve.nop())
                sVa.wait(sy); sVa.wait(te); sVa.wait(sc)
        barrier(ve, te, sc, sy)

        t_ = tap("xna", [D, T], BF16)
        if t_ is not None:
            s_ = Ctr(); bwait(sy)
            for k in range(KD):
                s_.dma(sy.dma_start(out=t_[128 * k:128 * k + 128, :], in_=xna[:, k, :]))
            s_.wait(sy)
        t_ = tap("rows_a", [1, 4 * T], F32)
        if t_ is not None:
            s_ = Ctr(); bwait(sy)
            s_.dma(sy.dma_start(out=t_[:, :], in_=rows[:, :]))
            s_.wait(sy)
        if DEBUG_STOP == "sl1a":
            emit_zero_output_and_stop()
            wstack.close()
            cstack.close()
            flush()
            return nc, taps

        sl1stack = ExitStack()
        hT = sl1stack.enter_context(nc.sbuf_tensor("hT", [128, T], BF16))
        Hb = sl1stack.enter_context(nc.sbuf_tensor("Hb", [128, T], F32))
        kqv = sl1stack.enter_context(nc.sbuf_tensor("kqv", [128, 2, T], BF16))
        vhat = sl1stack.enter_context(nc.sbuf_tensor("vhat", [128, KD, 96], BF16))
        out_aT = sl1stack.enter_context(nc.sbuf_tensor("out_aT", [64, T], BF16))
        sl1b = ExitStack()
        with sl1b:
            psH = sl1b.enter_context(nc.psum_tensor("psH", [128, 1024], F32))
            psT2 = sl1b.enter_context(nc.psum_tensor("psT2", [128, 128], BF16))
            psMQ = sl1b.enter_context(nc.psum_tensor("psMQ", [128, 128], F32))
            sVb = Ctr(); sTb = Ctr(); sCb = Ctr(); sSb = Ctr()
            bwait(te, ve, sc, gp)
            sV0.wait(te)
            for half in range(2):
                cols = slice(1024 * half, 1024 * half + 1024)
                for nn2 in range(2):
                    n0 = 1024 * half + 512 * nn2
                    for k in range(KD):
                        te.matmul(psH[:, 512 * nn2:512 * nn2 + 512], w1t[:, k, :],
                                  xna[:, k, n0:n0 + 512], start=(k == 0), stop=(k == KD - 1))
                sTb.tinc()
                sTb.wait(ve)
                sVb.inc(ve.tensor_copy(hT[:, cols], psH[:, :]))
                sVb.wait(te)
            for half in range(2):
                cols = slice(1024 * half, 1024 * half + 1024)
                for nn2 in range(2):
                    n0 = 1024 * half + 512 * nn2
                    for k in range(KD):
                        te.matmul(psH[:, 512 * nn2:512 * nn2 + 512], wqt[:, k, 0:128],
                                  xna[:, k, n0:n0 + 512], start=(k == 0), stop=(k == KD - 1))
                sTb.tinc()
                sTb.wait(ve)
                sVb.inc(ve.tensor_scalar(kqv[:, 0, cols], psH[:, :], smalls[:, 2:3],
                                         None, ALU.add))
                sVb.wait(te)
                for nn2 in range(2):
                    n0 = 1024 * half + 512 * nn2
                    for k in range(KD):
                        te.matmul(psH[0:64, 512 * nn2:512 * nn2 + 512], wqt[:, k, 128:192],
                                  xna[:, k, n0:n0 + 512], start=(k == 0), stop=(k == KD - 1))
                sTb.tinc()
                sTb.wait(ve)
                sVb.inc(ve.tensor_scalar(kqv[0:64, 1, cols], psH[0:64, :],
                                         smalls[0:64, 3:4], None, ALU.add))
                sVb.wait(te)
            sVb.inc(ve.tensor_scalar(Hb[:, :], hT[:, :], smalls[:, 0:1], None, ALU.add))
            for jc in range(KD):
                sTb.inc(te.transpose(psT2[:, 0:64], kqv[0:64, 1, 128 * jc:128 * jc + 128],
                                     idn[0:64, 0:64]))
                sTb.wait(ve)
                dst = vhat[:, jc, :].rearrange("p (h t) -> p h t", t=3)
                ve.tensor_copy(dst[:, :, 1:3],
                               psT2[:, 0:64].rearrange("p (h t) -> p h t", t=2))
                sVb.inc(ve.memset(dst[:, :, 0:1], 1.0))
                sVb.wait(te)
            with nc.sbuf_tensor("mqk", [128, 4], F32) as mqk, \
                 nc.sbuf_tensor("mrow", [1, 128], F32) as mrow, \
                 nc.sbuf_tensor("mtile", [128, 128], F32) as mtile:
                sVb.inc(ve.tensor_reduce(mqk[:, 0:1], kqv[:, 0, :], AX.X, ALU.max,
                                         apply_absolute_value=True))
                sVb.wait(sy)
                sKmv = Ctr()
                sKmv.dma(sy.dma_start(out=mqk[0:64, 2:3], in_=mqk[64:128, 0:1]))
                sKmv.wait(ve)
                ve.tensor_tensor(mqk[0:64, 1:2], mqk[0:64, 0:1], mqk[0:64, 2:3],
                                 ALU.mult)
                ve.memset(mtile[:, :], 0.0)
                sVb.inc(ve.tensor_copy(mtile[0:64, 0:1], mqk[0:64, 1:2]))
                with nc.sbuf_tensor("mtb", [128, 128], BF16) as mtb:
                    sVb.wait(ve)
                    ve.tensor_copy(mtb[:, :], mtile[:, :])
                    sVb.inc(ve.nop())
                    sVb.wait(te)
                    sTb.inc(te.transpose(psT2[:, :], mtb[:, :], idn[:, :]))
                    sTb.wait(ve)
                    ve.tensor_copy(mrow[0:1, :], psT2[0:1, :])
                    sVb.inc(ve.tensor_reduce(
                        SR(0, 32),
                        mrow[0:1, 0:64].rearrange("a (h t) -> a h t", t=2),
                        AX.X, ALU.add))
                    ve.tensor_scalar(SR(0, 32), SR(0, 32), -RSQ2, None, ALU.mult)
                    sVb.inc(ve.nop())
                    sVb.wait(te)
                    te.matmul(psMQ[:, 0:32], onesrow[0:1, :], SR(0, 32),
                              start=True, stop=True)
                    sTb.tinc()
                    sTb.wait(ve)
                    sVb.inc(ve.tensor_copy(bdca[:, :], psMQ[:, 0:32]))
                    sVb.wait(te)
        barrier(ve, te, sc, gp, sy)
        wstack.close()

        if DEBUG_STOP == "sl1b":
            emit_zero_output_and_stop()
            sl1stack.close()
            wstack.close()
            cstack.close()
            flush()
            return nc, taps

        t_ = tap("hT", [128, T], BF16)
        if t_ is not None:
            s_ = Ctr(); bwait(sy)
            s_.dma(sy.dma_start(out=t_[:, :], in_=hT[:, :]))
            s_.wait(sy)
        t_ = tap("kqv", [256, T], BF16)
        if t_ is not None:
            s_ = Ctr(); bwait(sy)
            s_.dma(sy.dma_start(out=t_[0:128, :], in_=kqv[:, 0, :]))
            s_.dma(sy.dma_start(out=t_[128:256, :], in_=kqv[:, 1, :]))
            s_.wait(sy)

        sl1c = ExitStack()
        with sl1c:
            EC = sl1c.enter_context
            p1a = EC(nc.sbuf_tensor("p1a", [128, 2, T], BF16))
            p1b = EC(nc.sbuf_tensor("p1b", [128, 2, T], BF16))
            rp2 = EC(nc.sbuf_tensor("rp2", [128, 2, 1024], BF16))
            simb = EC(nc.sbuf_tensor("simb", [128, T], F32))
            bst = EC(nc.sbuf_tensor("bst", [128, 8], F32))
            junk = EC(nc.sbuf_tensor("junkb", [128, T], BF16))
            sel = EC(nc.sbuf_tensor("selb", [128, T], BF16))
            selT = EC(nc.sbuf_tensor("selT", [128, KD, 512], BF16))
            ksl = EC(nc.sbuf_tensor("ksl", [2, 2, T], BF16))
            qsl = EC(nc.sbuf_tensor("qsl", [2, 2, 512], BF16))
            wts = EC(nc.sbuf_tensor("wts", [128, 2, 512], BF16))
            aou = EC(nc.sbuf_tensor("aou", [64, 512], BF16))
            zinv = EC(nc.sbuf_tensor("zinv", [32, 512], F32))
            ztmp = EC(nc.sbuf_tensor("ztmp", [1, 512], BF16))
            aor = EC(nc.sbuf_tensor("aor", [3, 512], BF16))
            sSc = Ctr(); sVc = Ctr(); sTc = Ctr(); sKc2 = [Ctr(), Ctr()]
            bwait(te, ve, sc, sy)
            for sb in range(4):
                icols = slice(512 * sb, 512 * sb + 512)
                pstk = ExitStack()
                psPP = pstk.enter_context(nc.psum_tensor(f"psPP{sb}", [128, 1024], F32))
                psSM = pstk.enter_context(nc.psum_tensor(f"psSM{sb}", [128, 2048], F32))
                for bl in range(4):
                    blk = 4 * sb + bl
                    i0 = 128 * blk
                    relu_marks = []
                    for t in range(64):
                        ia, ib = i0 + 2 * t, i0 + 2 * t + 1
                        pa, pb_ = p1a[:, t % 2, :], p1b[:, t % 2, :]
                        if t >= 2:
                            sTc.wait(sc, relu_marks[t - 2][1])
                            sTc.wait(ve, relu_marks[t - 2][1])
                        sSc.inc(sc.activation(pa, hT[:, :], AF.Relu,
                                              bias=Hb[:, ia:ia + 1], scale=-1.0))
                        sVc.inc(ve.tensor_scalar(pb_, hT[:, :], Hb[:, ib:ib + 1],
                                                 0.0, ALU.subtract, ALU.min))
                        sSc.wait(te); sVc.wait(te)
                        w2m = None
                        for jh in range(2):
                            pp = psPP[:, :]
                            j0 = 1024 * jh
                            for nn2 in range(2):
                                js = slice(512 * nn2, 512 * nn2 + 512)
                                jsrc = slice(j0 + 512 * nn2, j0 + 512 * nn2 + 512)
                                te.matmul(pp[0:64, js], w2t[:, 0, :], pa[:, jsrc],
                                          start=True, stop=True, tile_position=(0, 0))
                                te.matmul(pp[64:128, js], w2t[:, 1, :], pb_[:, jsrc],
                                          start=True, stop=True, tile_position=(0, 64))
                            m_ = sTc.tinc()
                            if jh == 0:
                                w2m = m_
                            sTc.wait(sc)
                            sSc.inc(sc.activation(rp2[:, jh, :], pp[:, :], AF.Relu,
                                                  bias=smalls[:, 1:2]))
                            sSc.wait(te)
                            for nn2 in range(2):
                                js = slice(512 * nn2, 512 * nn2 + 512)
                                jsim = slice(j0 + 512 * nn2, j0 + 512 * nn2 + 512)
                                te.matmul(psSM[:, jsim], w3s[:, t, :], rp2[:, jh, js],
                                          start=(t == 0), stop=(t == 63))
                        relu_marks.append((w2m, sTc.tinc()))
                    sTc.wait(ve)
                    ve.tensor_scalar(simb[:, :], psSM[:, :], smalls[:, 4:5], None, ALU.add)
                    sVc.inc(ve.tensor_tensor(simb[:, i0:i0 + 128], simb[:, i0:i0 + 128],
                                             noti[:, :], ALU.mult))
                    if blk == 0:
                        tsim = tap("sim0", [128, T], F32)
                        if tsim is not None:
                            sVc.wait(sy)
                            s_ = Ctr()
                            s_.dma(sy.dma_start(out=tsim[:, :], in_=simb[:, :]))
                            s_.wait(ve)
                    ve.tensor_reduce(bst[:, 1:2], simb[:, :], AX.X, ALU.max)
                    ve.tensor_reduce(bst[:, 0:1], simb[:, :], AX.X, ALU.min)
                    ve.tensor_scalar(bst[:, 5:6], bst[:, 1:2], -1.0, None, ALU.mult)
                    ve.tensor_tensor(bst[:, 5:6], bst[:, 5:6], bst[:, 1:2], ALU.max)
                    ve.tensor_scalar(bst[:, 5:6], bst[:, 5:6], 1e-5, 1e-20,
                                     ALU.mult, ALU.add)
                    ve.tensor_tensor(bst[:, 1:2], bst[:, 1:2], bst[:, 5:6], ALU.add)
                    jview = junk[:, :]
                    for it in range(NBIS):
                        ve.tensor_tensor(bst[:, 2:3], bst[:, 0:1], bst[:, 1:2], ALU.add)
                        ve.tensor_scalar(bst[:, 2:3], bst[:, 2:3], 0.5, None, ALU.mult)
                        ve.tensor_scalar(jview, simb[:, :], bst[:, 2:3], 0.0,
                                         ALU.is_ge, ALU.add, accum_out=bst[:, 3:4])
                        ve.tensor_scalar(bst[:, 4:5], bst[:, 3:4], float(KSP) - 0.5,
                                         None, ALU.is_ge)
                        ve.tensor_tensor(bst[:, 5:6], bst[:, 2:3], bst[:, 0:1], ALU.subtract)
                        ve.tensor_tensor(bst[:, 5:6], bst[:, 5:6], bst[:, 4:5], ALU.mult)
                        ve.tensor_tensor(bst[:, 0:1], bst[:, 0:1], bst[:, 5:6], ALU.add)
                        ve.tensor_tensor(bst[:, 5:6], bst[:, 2:3], bst[:, 1:2], ALU.subtract)
                        ve.tensor_tensor(bst[:, 6:7], bst[:, 5:6], bst[:, 4:5], ALU.mult)
                        ve.tensor_tensor(bst[:, 5:6], bst[:, 5:6], bst[:, 6:7], ALU.subtract)
                        ve.tensor_tensor(bst[:, 1:2], bst[:, 1:2], bst[:, 5:6], ALU.add)
                    ve.tensor_scalar(sel[:, :], simb[:, :], bst[:, 0:1], None, ALU.is_ge)
                    sVc.inc(ve.nop())
                    sVc.wait(te)
                    with nc.psum_tensor(f"psSL_{sb}_{bl}", [128, 128], BF16) as psSL:
                        for jc in range(KD):
                            sTc.inc(te.transpose(psSL[:, :],
                                                 sel[:, 128 * jc:128 * jc + 128], idn[:, :]))
                            sTc.wait(ve)
                            sVc.inc(ve.tensor_copy(selT[:, jc, 128 * bl:128 * bl + 128],
                                                   psSL[:, :]))
                            sVc.wait(te)
                pstk.close()
                if sb == 0:
                    tsel = tap("sel", [128, KD * 512], BF16)
                    if tsel is not None:
                        sVc.wait(sy)
                        s_ = Ctr()
                        s_.dma(sy.dma_start(
                            out=tsel[:, :],
                            in_=selT[:, :, :].rearrange("p k n -> p (k n)")))
                        s_.wait(sy)
                with nc.psum_tensor(f"psSC_{sb}", [128, 2, 512], F32) as psSC, \
                     nc.psum_tensor(f"psAO_{sb}", [128, 512], F32) as psAO, \
                     nc.psum_tensor(f"psZP_{sb}", [32, 512], F32) as psZP, \
                     nc.psum_tensor(f"psOA_{sb}", [64, 512], F32) as psOA, \
                     nc.psum_tensor(f"psZD_{sb}", [64, 512], F32) as psZD:
                    kq_marks = {}
                    def fetch_head(h):
                        bi = h % 2
                        c = sKc2[bi]
                        c.dma(sy.dma_start(out=ksl[0:2, bi, :],
                                           in_=kqv[64 + 2 * h:64 + 2 * h + 2, 0, :]))
                        c.dma(sy.dma_start(out=qsl[0:2, bi, :],
                                           in_=kqv[2 * h:2 * h + 2, 0, icols]))
                        kq_marks[h] = c.n
                    sTc.wait(sy)
                    fetch_head(0)
                    fetch_head(1)
                    for h in range(NHA):
                        bi = h % 2
                        sKc2[bi].wait(te, kq_marks[h])
                        for jt in range(KD):
                            te.matmul(psSC[:, jt % 2, :], ksl[0:2, bi, 128 * jt:128 * jt + 128],
                                      qsl[0:2, bi, :], start=True, stop=True)
                            sTc.tinc()
                            sTc.wait(sc)
                            sSc.inc(sc.activation(wts[:, jt % 2, :], psSC[:, jt % 2, :],
                                                  AF.Exp, scale=RSQ2,
                                                  bias=bdca[:, h:h + 1]))
                            sSc.wait(ve)
                            sVc.inc(ve.tensor_tensor(wts[:, jt % 2, :], wts[:, jt % 2, :],
                                                     selT[:, jt, :], ALU.mult))
                            sVc.wait(te)
                            te.matmul(psAO[0:3, :], vhat[:, jt, 3 * h:3 * h + 3],
                                      wts[:, jt % 2, :], start=(jt == 0), stop=(jt == KD - 1))
                        aomark = sTc.tinc()
                        if h + 2 < NHA:
                            sTc.wait(sy, aomark)
                            fetch_head(h + 2)
                        sTc.wait(ve)
                        ve.tensor_copy(ztmp[:, :], psAO[0:1, :])
                        sVc.inc(ve.tensor_copy(aor[:, :], psAO[0:3, :]))
                        sVc.wait(te)
                        te.matmul(psZP[:, :], basis[0:1, h, :], ztmp[:, :],
                                  start=(h == 0), stop=(h == NHA - 1))
                        te.matmul(psOA[:, :], sel64[:, h, :], aor[:, :],
                                  start=(h == 0), stop=(h == NHA - 1))
                        sTc.tinc()
                        sTc.wait(ve)
                    sTc.wait(ve)
                    ve.reciprocal(zinv[:, :], psZP[:, :])
                    ve.tensor_copy(aou[:, :], psOA[:, :])
                    sVc.inc(ve.nop())
                    sVc.wait(te)
                    te.matmul(psZD[:, :], dupm[:, :], zinv[:, :], start=True, stop=True)
                    sTc.tinc()
                    sTc.wait(ve)
                    sVc.inc(ve.tensor_tensor(out_aT[:, icols], aou[:, :], psZD[:, :],
                                             ALU.mult))
                    sVc.wait(te)
        barrier(ve, te, sc, sy)

        if DEBUG_STOP == "sl1c":
            emit_zero_output_and_stop()
            sl1stack.close()
            cstack.close()
            flush()
            return nc, taps

        sl1d = ExitStack()
        with sl1d:
            ED = sl1d.enter_context
            x0c2 = ED(nc.sbuf_tensor("x0c2", [128, 512], F32))
            x1c = ED(nc.sbuf_tensor("x1c", [128, 512], F32))
            x1cb = ED(nc.sbuf_tensor("x1cb", [128, 512], BF16))
            sq2 = ED(nc.sbuf_tensor("sq2", [128, 512], F32))
            psO = ED(nc.psum_tensor("psO", [128, 2, 512], F32))
            psBC2 = ED(nc.psum_tensor("psBC2", [128, 1024], F32))
            psS1b = ED(nc.psum_tensor("psS1b", [1, 512], F32))
            psS2b = ED(nc.psum_tensor("psS2b", [1, 512], F32))
            sVd = Ctr(); sTd = Ctr(); sSd = Ctr(); sXd = Ctr(); sStd = Ctr()
            t_x1 = tap("x1", [D, O], F32)
            bwait(te, ve, sc, sy)
            for tc in range(4):
                t0 = 512 * tc
                own = tc < 2
                src = xT_own if own else xT_oth
                col0 = (tc % 2) * 512
                sVd.wait(te); sSd.wait(te)
                te.matmul(psBC2[:, 0:512], onesrow[:, :], ROW(2, t0, 512),
                          start=True, stop=True)
                te.matmul(psBC2[:, 512:1024], onesrow[:, :], ROW(3, t0, 512),
                          start=True, stop=True)
                sTd.tinc()
                for dc in range(KD):
                    sXd.dma(sy.dma_start(out=x0c2[:, :],
                                         in_=src[128 * dc:128 * dc + 128, col0:col0 + 512]))
                    te.matmul(psO[:, dc % 2, :], wot[:, 128 * dc:128 * dc + 128],
                              out_aT[:, t0:t0 + 512], start=True, stop=True)
                    tdm = sTd.tinc()
                    sXd.wait(ve); sTd.wait(ve, tdm)
                    if dc > 0 or tc > 0:
                        sStd.wait(ve, sStd.n)
                        sSd.wait(ve)
                    ve.tensor_tensor(x1c[:, :], x0c2[:, :], psBC2[:, 0:512], ALU.mult)
                    ve.tensor_tensor(x1c[:, :], x1c[:, :], psBC2[:, 512:1024], ALU.subtract)
                    ve.tensor_tensor(x1c[:, :], x1c[:, :], x0c2[:, :], ALU.add)
                    ve.tensor_tensor(x1c[:, :], x1c[:, :], psO[:, dc % 2, :], ALU.add)
                    ve.tensor_scalar(x1c[:, :], x1c[:, :], outpk[:, dc:dc + 1], None, ALU.add)
                    if not own:
                        ve.tensor_copy(x1cb[:, :], x1c[:, :])
                    sVd.inc(ve.nop())
                    sVd.wait(sy); sVd.wait(sc)
                    if own:
                        sStd.dma(sy.dma_start(out=x1of[128 * dc:128 * dc + 128, col0:col0 + 512],
                                              in_=x1c[:, :]))
                        if t_x1 is not None:
                            sStd.dma(sy.dma_start(
                                out=t_x1[128 * dc:128 * dc + 128, col0:col0 + 512],
                                in_=x1c[:, :]))
                    else:
                        sStd.dma(sy.dma_start(out=x1pb[128 * dc:128 * dc + 128, col0:col0 + 512],
                                              in_=x1cb[:, :]))
                    sSd.inc(sc.activation(sq2[:, :], x1c[:, :], AF.Square))
                    sVd.wait(te)
                    te.matmul(psS1b[:, :], onescol[:, :], x1c[:, :],
                              start=(dc == 0), stop=(dc == KD - 1))
                    sSd.wait(te)
                    te.matmul(psS2b[:, :], onescol[:, :], sq2[:, :],
                              start=(dc == 0), stop=(dc == KD - 1))
                    sTd.tinc()
                sTd.wait(ve)
                m, v, t1 = SR(0), SR(1), SR(2)
                ve.tensor_scalar(m, psS1b[:, :], 1.0 / D, None, ALU.mult)
                ve.tensor_scalar(v, psS2b[:, :], 1.0 / D, None, ALU.mult)
                ve.tensor_tensor(t1, m, m, ALU.mult)
                sVd.inc(ve.tensor_tensor(v, v, t1, ALU.subtract))
                sVd.wait(sc)
                sSd.inc(sc.activation(t1, v, AF.Sqrt, bias=smalls[0:1, 13:14]))
                sSd.wait(ve)
                ve.reciprocal(ROW(0, t0, 512), t1)
                ve.tensor_tensor(ROW(1, t0, 512), m, ROW(0, t0, 512), ALU.mult)
                sVd.inc(ve.nop())
            sStd.wait(sy)
        barrier(ve, te, sc, sy)



        if DEBUG_STOP == "sl1":
            emit_zero_output_and_stop()
            sl1stack.close()
            cstack.close()
            flush()
            return nc, taps

        def emit_ffn(tag, tokens, src_f32, src_b16, rowA, rowB, want_stats,
                     dst_f32, dst_b16, to_output):
            HT = 512
            sTf = Ctr(); sVf = Ctr(); sSf = Ctr(); sWf2 = [Ctr(), Ctr()]; sXf = Ctr()
            sDf2 = [Ctr(), Ctr()]; sOf = Ctr()
            es = ExitStack()
            with es:
                EF = es.enter_context
                xnf = EF(nc.sbuf_tensor(f"xnf{tag}", [128, KD, HT], BF16))
                hbuf = EF(nc.sbuf_tensor(f"hbuf{tag}", [128, 64, HT], BF16))
                wgs = EF(nc.sbuf_tensor(f"wgs{tag}", [128, 2, KD, 256], BF16))
                wus = EF(nc.sbuf_tensor(f"wus{tag}", [128, 2, KD, 256], BF16))
                xrc = EF(nc.sbuf_tensor(f"xrc{tag}", [128, 512], F32))
                xrcb = EF(nc.sbuf_tensor(f"xrcb{tag}", [128, 512], BF16))
                stmp = EF(nc.sbuf_tensor(f"stmp{tag}", [128, HT], BF16))
                sqf = EF(nc.sbuf_tensor(f"sqf{tag}", [128, 512], F32))
                dts = EF(nc.sbuf_tensor(f"dts{tag}", [128, 2, 512], BF16))
                for hf in range(tokens // HT):
                    tok0 = HT * hf
                    own = (tok0 < O)
                    gcol0 = tok0 % O
                    with nc.psum_tensor(f"psBCf{tag}{hf}", [128, 1024], F32) as psBCf:
                        bwait(te, ve, sy, sc, gp)
                        sVf.wait(te)
                        te.matmul(psBCf[:, 0:512], onesrow[:, :], ROW(rowA, tok0, 512),
                                  start=True, stop=True)
                        te.matmul(psBCf[:, 512:1024], onesrow[:, :], ROW(rowB, tok0, 512),
                                  start=True, stop=True)
                        sTf.tinc()
                        sTf.wait(ve)
                        for k in range(KD):
                            src = src_f32 if own else src_b16
                            sXf.dma(sy.dma_start(
                                out=(xrc[:, :] if own else xrcb[:, :]),
                                in_=src[128 * k:128 * k + 128, gcol0:gcol0 + 512]))
                            sXf.wait(ve)
                            xin = xrc[:, :] if own else xrcb[:, :]
                            ve.tensor_tensor(sqf[:, :], xin, psBCf[:, 0:512], ALU.mult)
                            ve.tensor_tensor(xnf[:, k, :], sqf[:, :],
                                             psBCf[:, 512:1024], ALU.subtract)
                            sVf.inc(ve.nop())
                            sVf.wait(sy)
                        sVf.wait(te)
                    with nc.psum_tensor(f"psG{tag}{hf}", [128, 512], F32) as psG, \
                         nc.psum_tensor(f"psU{tag}{hf}", [128, 512], F32) as psU:
                        gu_marks = []
                        w_marks = {}

                        def prefetch(si, buf):
                            c0 = 256 * si
                            c = sWf2[buf]
                            for k in range(KD):
                                c.dma(sy.dma_start(out=wgs[:, buf, k, :],
                                                   in_=gup16[128 * k:128 * k + 128,
                                                             c0:c0 + 256]))
                                c.dma(sy.dma_start(out=wus[:, buf, k, :],
                                                   in_=gup16[128 * k:128 * k + 128,
                                                             INTER + c0:INTER + c0 + 256]))
                            w_marks[si] = c.n
                        if hf == 0 and tag == "f1":
                            sy.wait_ge(sFW.sem, FW_DONE)
                        prefetch(0, 0)
                        for si in range(32):
                            if si + 1 < 32:
                                if si >= 1:
                                    sTf.wait(sy, gu_marks[si - 1])
                                prefetch(si + 1, (si + 1) % 2)
                            sWf2[si % 2].wait(te, w_marks[si])
                            for fi in range(2):
                                fc = 2 * si + fi
                                fcol = 128 * fi
                                for k in range(KD):
                                    te.matmul(psG[:, :], wgs[:, si % 2, k, fcol:fcol + 128],
                                              xnf[:, k, :], start=(k == 0), stop=(k == KD - 1))
                                for k in range(KD):
                                    te.matmul(psU[:, :], wus[:, si % 2, k, fcol:fcol + 128],
                                              xnf[:, k, :], start=(k == 0), stop=(k == KD - 1))
                                sTf.tinc()
                                sTf.wait(sc)
                                sSf.inc(sc.activation(stmp[:, :], psG[:, :], AF.Silu))
                                sSf.wait(ve)
                                sVf.inc(ve.tensor_tensor(hbuf[:, fc, :], stmp[:, :],
                                                         psU[:, :], ALU.mult))
                                sVf.wait(te)
                            gu_marks.append(sTf.n)
                    for g in range(4):
                        with nc.psum_tensor(f"psD{tag}{hf}{g}", [128, 4, 512], F32) as psD:
                            dmarks = []
                            dl_marks = {}
                            sDf2[0].dma(sy.dma_start(out=dts[:, 0, :],
                                                     in_=dn16[0:128, 512 * g:512 * g + 512]))
                            dl_marks[0] = sDf2[0].n
                            for ic in range(64):
                                if ic + 1 < 64:
                                    if ic >= 1:
                                        sTf.wait(sy, dmarks[ic - 1])
                                    c = sDf2[(ic + 1) % 2]
                                    c.dma(sy.dma_start(
                                        out=dts[:, (ic + 1) % 2, :],
                                        in_=dn16[128 * (ic + 1):128 * (ic + 1) + 128,
                                                 512 * g:512 * g + 512]))
                                    dl_marks[ic + 1] = c.n
                                sDf2[ic % 2].wait(te, dl_marks[ic])
                                for dci in range(4):
                                    te.matmul(psD[:, dci, :],
                                              dts[:, ic % 2, 128 * dci:128 * dci + 128],
                                              hbuf[:, ic, :], start=(ic == 0), stop=(ic == 63))
                                dmarks.append(sTf.tinc())
                            for dci in range(4):
                                dc = 4 * g + dci
                                src = src_f32 if own else src_b16
                                sXf.dma(sy.dma_start(
                                    out=(xrc[:, :] if own else xrcb[:, :]),
                                    in_=src[128 * dc:128 * dc + 128, gcol0:gcol0 + 512]))
                                sXf.wait(ve)
                                sTf.wait(ve)
                                if dci > 0:
                                    sOf.wait(ve, sOf.n)
                                    sSf.wait(ve)
                                xin = xrc[:, :] if own else xrcb[:, :]
                                ve.tensor_tensor(xrc[:, :], xin, psD[:, dci, :], ALU.add)
                                if (not own) and (dst_b16 is not None):
                                    ve.tensor_copy(xrcb[:, :], xrc[:, :])
                                sVf.inc(ve.nop())
                                sVf.wait(sy); sVf.wait(sc)
                                if to_output:
                                    sOf.dma(sy.dma_start(
                                        out=y[128 * dc:128 * dc + 128, gcol0:gcol0 + 512],
                                        in_=xrc[:, :]))
                                elif own:
                                    sOf.dma(sy.dma_start(
                                        out=dst_f32[128 * dc:128 * dc + 128, gcol0:gcol0 + 512],
                                        in_=xrc[:, :]))
                                else:
                                    sOf.dma(sy.dma_start(
                                        out=dst_b16[128 * dc:128 * dc + 128, gcol0:gcol0 + 512],
                                        in_=xrcb[:, :]))
                    sOf.wait(sy)
                barrier(ve, te, sc, sy)

        sl1stack.close()
        cstack.close()
        emit_ffn("f1", 2048, x1of, x1pb, 0, 1, False, x2of, x2pb, False)

        sl2s = ExitStack()
        with sl2s:
            ES2 = sl2s.enter_context
            xc3 = ES2(nc.sbuf_tensor("xc3", [128, 512], F32))
            xc3b = ES2(nc.sbuf_tensor("xc3b", [128, 512], BF16))
            sq3 = ES2(nc.sbuf_tensor("sq3", [128, 2, 512], F32))
            psS13 = ES2(nc.psum_tensor("psS13", [1, 512], F32))
            psS23 = ES2(nc.psum_tensor("psS23", [1, 512], F32))
            sX3 = Ctr(); sT3 = Ctr(); sS3 = Ctr(); sV3 = Ctr(); sTp = Ctr()
            t_x2 = tap("x2", [D, O], F32)
            bwait(te, ve, sc, sy)
            for tc in range(4):
                t0 = 512 * tc
                own = tc < 2
                src = x2of if own else x2pb
                col0 = (tc % 2) * 512
                tbase = sT3.n
                for k in range(KD):
                    sTp.wait(sy)
                    sX3.dma(sy.dma_start(out=(xc3[:, :] if own else xc3b[:, :]),
                                         in_=src[128 * k:128 * k + 128, col0:col0 + 512]))
                    if own and t_x2 is not None:
                        sX3.wait(sy)
                        sTp.dma(sy.dma_start(
                            out=t_x2[128 * k:128 * k + 128, col0:col0 + 512],
                            in_=xc3[:, :]))
                    sX3.wait(sc); sX3.wait(te)
                    xin = xc3[:, :] if own else xc3b[:, :]
                    if k >= 2:
                        sT3.wait(sc, tbase + k - 1)
                    sS3.inc(sc.activation(sq3[:, k % 2, :], xin, AF.Square))
                    if own:
                        te.matmul(psS13[:, :], onescol[:, :], xin,
                                  start=(k == 0), stop=(k == KD - 1))
                    else:
                        te.matmul(psS13[:, :], onescol_b[:, :], xin,
                                  start=(k == 0), stop=(k == KD - 1))
                    sS3.wait(te)
                    te.matmul(psS23[:, :], onescol[:, :], sq3[:, k % 2, :],
                              start=(k == 0), stop=(k == KD - 1))
                    sT3.tinc()
                    sT3.wait(sy)
                sT3.wait(ve)
                m, v, t1 = SR(0), SR(1), SR(2)
                ve.tensor_scalar(m, psS13[:, :], 1.0 / D, None, ALU.mult)
                ve.tensor_scalar(v, psS23[:, :], 1.0 / D, None, ALU.mult)
                ve.tensor_tensor(t1, m, m, ALU.mult)
                sV3.inc(ve.tensor_tensor(v, v, t1, ALU.subtract))
                sV3.wait(sc)
                sS3.inc(sc.activation(t1, v, AF.Sqrt, bias=smalls[0:1, 13:14]))
                sS3.wait(ve)
                ve.reciprocal(ROW(2, t0, 512), t1)
                ve.tensor_tensor(ROW(3, t0, 512), m, ROW(2, t0, 512), ALU.mult)
                sV3.inc(ve.nop())
                sV3.wait(te)
        barrier(ve, te, sc, sy)


        if DEBUG_STOP == "sl2":
            emit_zero_output_and_stop()
            flush()
            return nc, taps

        sl3stack = ExitStack()
        qb = sl3stack.enter_context(nc.sbuf_tensor("qb", [128, 16, O], BF16))
        sl3a = ExitStack()
        with sl3a:
            E3 = sl3a.enter_context
            xnb = E3(nc.sbuf_tensor("xnb", [128, KD, T], BF16))
            xrc3 = E3(nc.sbuf_tensor("xrc3a", [128, 512], F32))
            xrc3b = E3(nc.sbuf_tensor("xrc3b", [128, 512], BF16))
            was = E3(nc.sbuf_tensor("was", [128, 2, KD, 512], BF16))
            kvst = E3(nc.sbuf_tensor("kvst", [128, T], BF16))
            sTg = Ctr(); sVg = Ctr(); sSg = Ctr(); sXg = Ctr(); sWg2 = [Ctr(), Ctr()]; sKg = Ctr()
            bwait(te, ve, sc, sy, gp)
            with nc.psum_tensor("psBC3", [128, 1024], F32) as psBC3:
                for tc in range(4):
                    t0 = 512 * tc
                    own = tc < 2
                    src = x2of if own else x2pb
                    col0 = (tc % 2) * 512
                    sVg.wait(te)
                    te.matmul(psBC3[:, 0:512], onesrow[:, :], ROW(2, t0, 512),
                              start=True, stop=True)
                    te.matmul(psBC3[:, 512:1024], onesrow[:, :], ROW(3, t0, 512),
                              start=True, stop=True)
                    sTg.tinc()
                    sTg.wait(ve)
                    for k in range(KD):
                        sXg.dma(sy.dma_start(out=(xrc3[:, :] if own else xrc3b[:, :]),
                                             in_=src[128 * k:128 * k + 128, col0:col0 + 512]))
                        sXg.wait(ve)
                        xin = xrc3[:, :] if own else xrc3b[:, :]
                        ve.tensor_tensor(xrc3[:, :], xin, psBC3[:, 0:512], ALU.mult)
                        ve.tensor_tensor(xnb[:, k, t0:t0 + 512], xrc3[:, :],
                                         psBC3[:, 512:1024], ALU.subtract)
                        sVg.inc(ve.nop())
                        sVg.wait(sy)
                    sVg.wait(te)
            with nc.psum_tensor("psQ", [128, 2, 512], F32) as psQ:
                aw_marks = {}
                def prefetch_a(si, buf):
                    c0 = 512 * si
                    c = sWg2[buf]
                    for k in range(KD):
                        c.dma(gp.dma_start(out=was[:, buf, k, :],
                                           in_=bWattn[128 * k:128 * k + 128, c0:c0 + 512]))
                    aw_marks[si] = c.n
                a_marks = []
                prefetch_a(0, 0)
                for si in range(12):
                    if si + 1 < 12:
                        if si >= 1:
                            sTg.wait(gp, a_marks[si - 1])
                        prefetch_a(si + 1, (si + 1) % 2)
                    sWg2[si % 2].wait(te, aw_marks[si])
                    for oi in range(4):
                        oc = 4 * si + oi
                        is_q = oc < 8 or (16 <= oc < 24)
                        ntok = O if is_q else T
                        for q in range(ntok // 512):
                            qs = slice(512 * q, 512 * q + 512)
                            for k in range(KD):
                                te.matmul(psQ[:, q % 2, :],
                                          was[:, si % 2, k, 128 * oi:128 * oi + 128],
                                          xnb[:, k, qs], start=(k == 0), stop=(k == KD - 1))
                            sTg.tinc()
                            sTg.wait(ve)
                            if is_q:
                                qi = oc if oc < 8 else (oc - 8)
                                sVg.inc(ve.tensor_copy(qb[:, qi, qs], psQ[:, q % 2, :]))
                            else:
                                sVg.inc(ve.tensor_copy(kvst[:, qs], psQ[:, q % 2, :]))
                            sVg.wait(te)
                        if not is_q:
                            if 8 <= oc < 16:
                                r0 = 128 * (oc - 8)
                            elif 24 <= oc < 32:
                                r0 = 1024 + 128 * (oc - 24)
                            else:
                                r0 = 2048 + 128 * (oc - 32)
                            sVg.wait(sy)
                            sKg.dma(sy.dma_start(out=kvs[r0:r0 + 128, :], in_=kvst[:, :]))
                            sKg.wait(ve)
                    a_marks.append(sTg.n)
            sKg.wait(sy)
        barrier(ve, te, sc, sy, gp)

        obT = sl3stack.enter_context(nc.sbuf_tensor("obT", [128, KD, O], BF16))
        sl3b = ExitStack()
        with sl3b:
            E4 = sl3b.enter_context
            kh1 = E4(nc.sbuf_tensor("kh1", [65, T], BF16))
            kh2 = E4(nc.sbuf_tensor("kh2", [65, T], BF16))
            qh1 = E4(nc.sbuf_tensor("qh1", [65, O], BF16))
            qh2 = E4(nc.sbuf_tensor("qh2", [65, O], BF16))
            vtok = E4(nc.sbuf_tensor("vtok", [128, KD, 128], BF16))
            wtb = E4(nc.sbuf_tensor("wtb", [128, 2, 1024], BF16))
            aqb = E4(nc.sbuf_tensor("aqb", [64, O], BF16))
            mk64 = E4(nc.sbuf_tensor("mk64", [64, 2], BF16))
            mk64f = E4(nc.sbuf_tensor("mk64f", [64, 2], F32))
            bdr = E4(nc.sbuf_tensor("bdr", [1, O], F32))
            zrow = E4(nc.sbuf_tensor("zrow", [1, O], F32))
            zb = E4(nc.sbuf_tensor("zb", [128, O], F32))
            sTh = Ctr(); sVh = Ctr(); sSh = Ctr(); sKh = Ctr()
            bwait(te, ve, sc, sy, gp)
            for h in range(NHB):
                q0 = 64 * (h % 2)
                sVh.wait(te)
                ve.tensor_copy(qh1[0:64, :], qb[q0:q0 + 64, h // 2, :])
                ve.tensor_scalar(qh2[0:64, :], qb[q0:q0 + 64, 8 + h // 2, :],
                                 smalls[0:64, 6:7], None, ALU.mult)
                ve.memset(qh2[64:65, :], 0.0)
                sVh.inc(ve.nop())
                sVh.wait(sy)
                sKh.dma(sy.dma_start(out=kh1[0:64, :], in_=kvs[64 * h:64 * h + 64, :]))
                sKh.dma(sy.dma_start(out=kh2[0:64, :],
                                     in_=kvs[1024 + 64 * h:1024 + 64 * h + 64, :]))
                for jc in range(KD):
                    sKh.dma(sy.dma_start(out=vtok[:, jc, :],
                                         in_=kvs[2048 + 128 * h:2048 + 128 * h + 128,
                                                 128 * jc:128 * jc + 128],
                                         transpose=True))
                sKh.wait(ve)
                ve.memset(kh1[64:65, :], 1.0)
                ve.memset(kh2[64:65, :], 0.0)
                sVh.inc(ve.nop())
                with nc.psum_tensor(f"psB3_{h}", [1, 512], F32) as psB3:
                    sVh.wait(ve)
                    ve.tensor_reduce(mk64f[:, 0:1], kh1[0:64, :], AX.X, ALU.max,
                                     apply_absolute_value=True)
                    ve.tensor_reduce(mk64f[:, 1:2], kh2[0:64, :], AX.X, ALU.max,
                                     apply_absolute_value=True)
                    sVh.inc(ve.tensor_copy(mk64[:, :], mk64f[:, :]))
                    sVh.wait(sc)
                    sSh.inc(sc.activation(aqb[:, :], qh1[0:64, :], AF.Abs))
                    sSh.wait(te); sVh.wait(te)
                    for q in range(2):
                        te.matmul(psB3[:, :], mk64[:, 0:1], aqb[:, 512 * q:512 * q + 512],
                                  start=True, stop=True)
                        sTh.tinc()
                        sTh.wait(ve)
                        sVh.inc(ve.tensor_copy(bdr[0:1, 512 * q:512 * q + 512], psB3[:, :]))
                        sVh.wait(te)
                    sVh.wait(sc)
                    sSh.inc(sc.activation(aqb[:, :], qh2[0:64, :], AF.Abs))
                    sSh.wait(te)
                    for q in range(2):
                        te.matmul(psB3[:, :], mk64[:, 1:2], aqb[:, 512 * q:512 * q + 512],
                                  start=True, stop=True)
                        sTh.tinc()
                        sTh.wait(ve)
                        ve.tensor_tensor(bdr[0:1, 512 * q:512 * q + 512],
                                         bdr[0:1, 512 * q:512 * q + 512], psB3[:, :], ALU.add)
                        sVh.inc(ve.tensor_scalar(qh1[64:65, 512 * q:512 * q + 512],
                                                 bdr[0:1, 512 * q:512 * q + 512],
                                                 -1.0, None, ALU.mult))
                        sVh.wait(te)
                with nc.psum_tensor(f"psSC3_{h}", [128, 2, 512], F32) as psSC3, \
                     nc.psum_tensor(f"psZ3_{h}", [1, 1024], F32) as psZ3, \
                     nc.psum_tensor(f"psAO3_{h}", [128, 1024], F32) as psAO3, \
                     nc.psum_tensor(f"psZB_{h}", [128, 1024], F32) as psZB:
                    for jt in range(KD):
                        jsl = slice(128 * jt, 128 * jt + 128)
                        for q in range(2):
                            qs = slice(512 * q, 512 * q + 512)
                            te.matmul(psSC3[:, q, :], kh1[:, jsl], qh1[:, qs],
                                      start=True, stop=False)
                            te.matmul(psSC3[:, q, :], kh2[:, jsl], qh2[:, qs],
                                      start=False, stop=True)
                        sTh.tinc()
                        sTh.wait(sc)
                        sSh.inc(sc.activation(wtb[:, jt % 2, 0:512], psSC3[:, 0, :],
                                              AF.Exp, scale=0.125))
                        sSh.inc(sc.activation(wtb[:, jt % 2, 512:1024], psSC3[:, 1, :],
                                              AF.Exp, scale=0.125))
                        sSh.wait(te)
                        for q in range(2):
                            qs = slice(512 * q, 512 * q + 512)
                            te.matmul(psZ3[:, qs], onescol_b[:, :], wtb[:, jt % 2, qs],
                                      start=(jt == 0), stop=(jt == KD - 1))
                        for q in range(2):
                            qs = slice(512 * q, 512 * q + 512)
                            te.matmul(psAO3[:, qs], vtok[:, jt, :], wtb[:, jt % 2, qs],
                                      start=(jt == 0), stop=(jt == KD - 1))
                        sTh.tinc()
                    sTh.wait(ve)
                    ve.tensor_copy(zrow[:, :], psZ3[:, :])
                    sVh.inc(ve.tensor_scalar(zrow[:, :], zrow[:, :], 1.25, None, ALU.mult))
                    sVh.wait(te)
                    for q in range(2):
                        qs = slice(512 * q, 512 * q + 512)
                        te.matmul(psZB[:, qs], onesrow[:, :], zrow[0:1, qs],
                                  start=True, stop=True)
                    sTh.tinc()
                    sTh.wait(ve)
                    ve.reciprocal(zb[:, :], psZB[:, :])
                    sVh.inc(ve.tensor_tensor(obT[:, h, :], psAO3[:, :], zb[:, :], ALU.mult))
                    sVh.wait(te)
        barrier(ve, te, sc, sy, gp)

        t_ = tap("ob", [D, O], BF16)
        if t_ is not None:
            s_ = Ctr(); bwait(sy)
            for k in range(KD):
                s_.dma(sy.dma_start(out=t_[128 * k:128 * k + 128, :], in_=obT[:, k, :]))
            s_.wait(sy)

        sl3c = ExitStack()
        with sl3c:
            E5 = sl3c.enter_context
            wps = E5(nc.sbuf_tensor("wps", [128, 2, KD, 512], BF16))
            xrc4 = E5(nc.sbuf_tensor("xrc4", [128, 512], F32))
            sq4 = E5(nc.sbuf_tensor("sq4", [128, 512], F32))
            x3c = E5(nc.sbuf_tensor("x3c", [128, 512], F32))
            psP = E5(nc.psum_tensor("psP", [128, 2, 512], F32))
            psS1g = E5(nc.psum_tensor("psS1g", [1, 1024], F32))
            psS2g = E5(nc.psum_tensor("psS2g", [1, 1024], F32))
            sTi = Ctr(); sVi = Ctr(); sSi = Ctr(); sXi = Ctr(); sWi2 = [Ctr(), Ctr()]; sSti = Ctr()
            t_x3 = tap("x3", [D, O], F32)
            bwait(te, ve, sc, sy, gp)
            p_marks = []
            pw_marks = {}

            def prefetch_p(si, buf):
                c0 = 512 * si
                c = sWi2[buf]
                for k in range(KD):
                    c.dma(gp.dma_start(out=wps[:, buf, k, :],
                                       in_=bWproj[128 * k:128 * k + 128, c0:c0 + 512]))
                pw_marks[si] = c.n
            prefetch_p(0, 0)
            for si in range(4):
                if si + 1 < 4:
                    if si >= 1:
                        sTi.wait(gp, p_marks[si - 1])
                    prefetch_p(si + 1, (si + 1) % 2)
                sWi2[si % 2].wait(te, pw_marks[si])
                for di in range(4):
                    dc = 4 * si + di
                    for q in range(2):
                        qs = slice(512 * q, 512 * q + 512)
                        for k in range(KD):
                            te.matmul(psP[:, q % 2, :],
                                      wps[:, si % 2, k, 128 * di:128 * di + 128],
                                      obT[:, k, qs], start=(k == 0), stop=(k == KD - 1))
                        tm = sTi.tinc()
                        sXi.dma(sy.dma_start(out=xrc4[:, :],
                                             in_=x2of[128 * dc:128 * dc + 128, qs]))
                        sXi.wait(ve); sTi.wait(ve, tm)
                        if not (si == 0 and di == 0 and q == 0):
                            sSti.wait(ve, sSti.n)
                            sSi.wait(ve)
                        ve.tensor_tensor(x3c[:, :], xrc4[:, :], psP[:, q % 2, :], ALU.add)
                        sVi.inc(ve.nop())
                        sVi.wait(sy); sVi.wait(sc)
                        sSti.dma(sy.dma_start(out=x3of[128 * dc:128 * dc + 128, qs],
                                              in_=x3c[:, :]))
                        if t_x3 is not None:
                            sSti.dma(sy.dma_start(out=t_x3[128 * dc:128 * dc + 128, qs],
                                                  in_=x3c[:, :]))
                        sSi.inc(sc.activation(sq4[:, :], x3c[:, :], AF.Square))
                        sVi.wait(te)
                        te.matmul(psS1g[:, qs], onescol[:, :], x3c[:, :],
                                  start=(dc == 0), stop=(dc == KD - 1))
                        sSi.wait(te)
                        te.matmul(psS2g[:, qs], onescol[:, :], sq4[:, :],
                                  start=(dc == 0), stop=(dc == KD - 1))
                        sTi.tinc()
                p_marks.append(sTi.n)
            sTi.wait(ve)
            for q in range(2):
                qs = slice(512 * q, 512 * q + 512)
                t0 = 512 * q
                m, v, t1 = SR(0), SR(1), SR(2)
                ve.tensor_scalar(m, psS1g[:, qs], 1.0 / D, None, ALU.mult)
                ve.tensor_scalar(v, psS2g[:, qs], 1.0 / D, None, ALU.mult)
                ve.tensor_tensor(t1, m, m, ALU.mult)
                sVi.inc(ve.tensor_tensor(v, v, t1, ALU.subtract))
                sVi.wait(sc)
                sSi.inc(sc.activation(t1, v, AF.Sqrt, bias=smalls[0:1, 13:14]))
                sSi.wait(ve)
                ve.reciprocal(ROW(0, t0, 512), t1)
                ve.tensor_tensor(ROW(1, t0, 512), m, ROW(0, t0, 512), ALU.mult)
                sVi.inc(ve.nop())
            sSti.wait(sy)
        barrier(ve, te, sc, sy, gp)


        if DEBUG_STOP == "sl3":
            emit_zero_output_and_stop()
            sl3stack.close()
            flush()
            return nc, taps

        sl3stack.close()
        emit_ffn("f2", 1024, x3of, None, 0, 1, False, None, None, True)
        flush()

    return nc, taps


_NC_CACHE = {}


def _host_maps(inputs):
    f32 = np.float32
    T_ = np.ascontiguousarray
    gup = T_(inputs["L0_ffn_w_gate_up"].astype(f32, copy=False).T)
    dnT = T_(inputs["L0_ffn_w_down"].astype(f32, copy=False).T)
    W1T = T_(inputs["L0_attn_a_W1"].astype(f32, copy=False).T)
    W2T = T_(inputs["L0_attn_a_W2"].astype(f32, copy=False).T)
    W3c = T_(inputs["L0_attn_a_W3"].astype(f32, copy=False).reshape(64, 1))
    WqT = T_(inputs["L0_attn_a_Wqkv"].astype(f32, copy=False).T)
    WoT = T_(inputs["L0_attn_a_Wout"].astype(f32, copy=False).T)
    gam = inputs["L0_attn_a_gamma"].astype(f32, copy=False)
    bet = inputs["L0_attn_a_beta"].astype(f32, copy=False)
    bout = inputs["L0_attn_a_bout"].astype(f32, copy=False)
    b1 = T_(inputs["L0_attn_a_b1"].astype(f32, copy=False).reshape(128, 1))
    b2 = inputs["L0_attn_a_b2"].astype(f32, copy=False)
    b2d = T_(np.concatenate([b2, b2]).reshape(128, 1))
    b3 = inputs["L0_attn_a_b3"].astype(f32, copy=False).reshape(1, 1)
    bqkv = inputs["L0_attn_a_bqkv"].astype(f32, copy=False)
    bqkv_p = np.zeros((128, 2), f32)
    bqkv_p[:, 0] = bqkv[0:128]
    bqkv_p[0:64, 1] = bqkv[128:192]

    def pk(vec):
        return T_(vec.reshape(KD, 128).T)
    x = inputs["L0_attn_a_x"].astype(f32, copy=False)
    com = {
        "aW1T": W1T, "ab1": b1, "aW2T": W2T, "ab2d": b2d, "aW3c": W3c, "ab3": b3,
        "aWqkvT": WqT, "abqkv": bqkv_p, "aWoutT": WoT,
        "about_pk": pk(bout), "agam_pk": pk(gam), "abet_pk": pk(bet),
        "bWattn": T_(inputs["L0_attn_b_W_attn"].astype(f32, copy=False)),
        "bWproj": T_(inputs["L0_attn_b_W_proj"].astype(f32, copy=False)),
        "blq1": inputs["L0_attn_b_lambda_q1"].astype(f32, copy=False).reshape(1, D),
        "blk1": inputs["L0_attn_b_lambda_k1"].astype(f32, copy=False).reshape(1, D),
        "blq2": inputs["L0_attn_b_lambda_q2"].astype(f32, copy=False).reshape(1, D),
        "blk2": inputs["L0_attn_b_lambda_k2"].astype(f32, copy=False).reshape(1, D),
        "fgupT": gup, "fdnT": dnT,
    }
    in_maps = []
    for c in range(8):
        b, half = c // 2, c % 2
        xb = x[b]
        m = dict(com)
        m["xT_own"] = T_(xb[1024 * half:1024 * half + 1024].T)
        m["xT_oth"] = T_(xb[1024 * (1 - half):1024 * (1 - half) + 1024].T)
        in_maps.append(m)
    return in_maps


def kernel(**inputs):
    if "std" not in _NC_CACHE:
        _NC_CACHE["std"] = build_program()
    nc, taps = _NC_CACHE["std"]
    in_maps = _host_maps(inputs)
    res = run_bass_kernel_spmd(nc, in_maps, core_ids=list(range(8)))
    out = np.empty((4, 2048, 2048), np.float32)
    for c in range(8):
        b, half = c // 2, c % 2
        out[b, 1024 * half:1024 * half + 1024, :] = res.results[c]["y"].T
    return out


# version 02
# speedup vs baseline: 1.0855x; 1.0855x over previous
import numpy as np
import ml_dtypes
import concourse.bass as bass
import concourse.mybir as mybir
from concourse.bass_utils import run_bass_kernel_spmd
from contextlib import ExitStack

F32 = mybir.dt.float32
BF16 = mybir.dt.bfloat16
AF = mybir.ActivationFunctionType
ALU = mybir.AluOpType
AX = mybir.AxisListType

D = 2048
T = 2048
O = 1024
KD = 16
EPS = 1e-5
KSP = 204
NHA = 32
NHB = 16
INTER = 8192
LAM_INIT = 0.2
NBIS = 20
RSQ2 = float(1.0 / np.sqrt(2.0))

DEBUG_STOP = None
DEBUG_TAPS = ()
CHAIN_WAITS = True


def build_program():
    nc = bass.Bass("TRN2", target_bir_lowering=False)

    def I(name, shape, dtype=F32):
        return nc.dram_tensor(name, list(shape), dtype, kind="ExternalInput")

    xT_own = I("xT_own", [D, O])
    xT_oth = I("xT_oth", [D, O])
    aW1T = I("aW1T", [D, 128])
    ab1 = I("ab1", [128, 1])
    aW2T = I("aW2T", [128, 64])
    ab2d = I("ab2d", [128, 1])
    aW3c = I("aW3c", [64, 1])
    ab3 = I("ab3", [1, 1])
    aWqkvT = I("aWqkvT", [D, 192])
    abqkv = I("abqkv", [128, 2])
    aWoutT = I("aWoutT", [64, D])
    about_pk = I("about_pk", [128, KD])
    agam_pk = I("agam_pk", [128, KD])
    abet_pk = I("abet_pk", [128, KD])
    bWattn = I("bWattn", [D, 3 * D])
    bWproj = I("bWproj", [D, D])
    blq1 = I("blq1", [1, D]); blk1 = I("blk1", [1, D])
    blq2 = I("blq2", [1, D]); blk2 = I("blk2", [1, D])
    fgupT = I("fgupT", [D, 2 * INTER])
    fdnT = I("fdnT", [INTER, D])

    y = nc.dram_tensor("y", [D, O], F32, kind="ExternalOutput")

    taps = {}

    def tap(name, shape, dtype=F32):
        if name in DEBUG_TAPS:
            taps[name] = nc.dram_tensor("tap_" + name, list(shape), dtype,
                                        kind="ExternalOutput")
        return taps.get(name)

    gup16 = nc.dram_tensor("gup16", [D, 2 * INTER], BF16, kind="Internal")
    dn16 = nc.dram_tensor("dn16", [INTER, D], BF16, kind="Internal")
    x1of = nc.dram_tensor("x1of", [D, O], F32, kind="Internal")
    x1pb = nc.dram_tensor("x1pb", [D, O], BF16, kind="Internal")
    x2of = nc.dram_tensor("x2of", [D, O], F32, kind="Internal")
    x2pb = nc.dram_tensor("x2pb", [D, O], BF16, kind="Internal")
    x3of = nc.dram_tensor("x3of", [D, O], F32, kind="Internal")
    kvs = nc.dram_tensor("kvs", [4096, T], BF16, kind="Internal")

    ctx = ExitStack()
    E = ctx.enter_context
    with ctx:
        rows = E(nc.sbuf_tensor("rows", [1, 4 * T], F32))

        def ROW(i, t0=0, w=T):
            return rows[0:1, i * T + t0: i * T + t0 + w]
        srow = E(nc.sbuf_tensor("srow", [1, 4 * 512], F32))

        def SR(i, w=512):
            return srow[0:1, 512 * i: 512 * i + w]
        onescol = E(nc.sbuf_tensor("onescol", [128, 1], F32))
        onescol_b = E(nc.sbuf_tensor("onescol_b", [128, 1], BF16))
        onesrow = E(nc.sbuf_tensor("onesrow", [1, 128], F32))
        onesrow_b = E(nc.sbuf_tensor("onesrow_b", [1, 128], BF16))
        onesTrow = E(nc.sbuf_tensor("onesTrow", [1, T], BF16))
        idn = E(nc.sbuf_tensor("idn", [128, 128], BF16))
        noti = E(nc.sbuf_tensor("noti", [128, 128], BF16))
        dupm = E(nc.sbuf_tensor("dupm", [32, 64], F32))
        sel64 = E(nc.sbuf_tensor("sel64", [3, 32, 64], BF16))
        basis = E(nc.sbuf_tensor("basis", [1, 32, 32], BF16))
        smalls = E(nc.sbuf_tensor("smalls", [128, 16], F32))
        bdca = E(nc.sbuf_tensor("bdca", [128, 32], F32))
        w2t = E(nc.sbuf_tensor("w2t", [128, 2, 64], BF16))
        w3s = E(nc.sbuf_tensor("w3s", [128, 64, 128], BF16))
        wot = E(nc.sbuf_tensor("wot", [64, D], BF16))
        gpk = E(nc.sbuf_tensor("gpk", [128, KD], F32))
        bpkf = E(nc.sbuf_tensor("bpkf", [128, KD], F32))
        bpk16 = E(nc.sbuf_tensor("bpk16", [128, KD], BF16))
        outpk = E(nc.sbuf_tensor("outpk", [128, KD], F32))

        block = E(nc.Block())

        CHAINED = ("ve", "sc", "gp")
        NOCHAIN = {"dma_start", "wait_ge", "dma_start_transpose"}
        chain_counts = {"ve": 0, "sc": 0, "gp": 0}

        class InstHandle:
            __slots__ = ("incs", "engine", "chain_val", "mark")

            def __init__(self, engine, chain_val):
                self.incs = []
                self.engine = engine
                self.chain_val = chain_val
                self.mark = None

            def then_inc(self, sem, k=1):
                self.incs.append((sem, k))
                return self

        class EngQ:
            def __init__(self, name):
                self.name = name
                self.q = []

            def __getattr__(self, attr):
                def call(*a, **kw):
                    cv = None
                    if self.name in CHAINED and attr not in NOCHAIN:
                        chain_counts[self.name] += 1
                        cv = chain_counts[self.name]
                    h = InstHandle(self.name, cv)
                    self.q.append((attr, a, kw, h))
                    return h
                return call

        sy, gp, ve, sc, te = EngQ("sy"), EngQ("gp"), EngQ("ve"), EngQ("sc"), EngQ("te")
        ALLE = (sy, gp, ve, sc, te)
        chain_sems = {n: E(nc.semaphore("ch_" + n)) for n in CHAINED}

        def flush():
            regs = ((sy, block.sync), (gp, block.gpsimd), (ve, block.vector),
                    (sc, block.scalar), (te, block.tensor))
            for proxy, reg in regs:
                q = proxy.q
                csem = chain_sems.get(proxy.name)

                def runner(e, q=q, csem=csem):
                    for attr, a, kw, h in q:
                        if CHAIN_WAITS and h.chain_val is not None and h.chain_val > 1:
                            e.wait_ge(csem, h.chain_val - 1)
                        inst = getattr(e, attr)(*a, **kw)
                        if h.chain_val is not None:
                            inst.then_inc(csem, 1)
                        for sem, k in h.incs:
                            inst.then_inc(sem, k)
                reg(runner)
                proxy.q = []

        sem_pool = [E(nc.semaphore(f"sp{i}")) for i in range(88)]
        pool_state = {}
        pool_i = [0]

        class Ctr:
            def __init__(self):
                assert pool_i[0] < len(sem_pool), "semaphore pool exhausted"
                self.sem = sem_pool[pool_i[0]]
                pool_i[0] += 1
                self.base = pool_state.get(id(self.sem), 0)
                self.n = self.base
                self.mk = {}

            def inc(self, h, k=1):
                self.n += k
                if getattr(h, "chain_val", None) is not None:
                    self.mk[self.n] = (chain_sems[h.engine], h.chain_val)
                else:
                    h.then_inc(self.sem, k)
                    h.mark = (self.sem, self.n)
                    pool_state[id(self.sem)] = max(pool_state.get(id(self.sem), 0), self.n)
                return self.n

            def tinc(self, k=1):
                assert te.q, "tinc with empty tensor queue"
                h = te.q[-1][3]
                if h.mark is not None:
                    return self.n
                return self.inc(h, k)

            def dma(self, h):
                return self.inc(h, 16)

            def wait(self, eng, n=None):
                n = self.n if n is None else n
                if n <= self.base:
                    return
                if n in self.mk:
                    sem, val = self.mk[n]
                    eng.wait_ge(sem, val)
                else:
                    eng.wait_ge(self.sem, n)

        sB = Ctr()
        bar_gen = [0]
        bar_set = {}
        bar_seen = {id(e): -1 for e in ALLE}

        sTbar = Ctr()

        def barrier(*workers):
            for e in workers:
                if e is te:
                    if not te.q:
                        continue
                    h = te.q[-1][3]
                    if h.mark is not None:
                        bar_set[("te",)] = h.mark
                    else:
                        v = sTbar.inc(h)
                        bar_set[("te",)] = (sTbar.sem, v)
                    continue
                h = e.nop()
                if h.chain_val is not None:
                    sem = chain_sems[h.engine]
                    bar_set[id(sem)] = (sem, h.chain_val)
                else:
                    h.then_inc(sB.sem, 1)
                    sB.n += 1
                    bar_set[id(sB.sem)] = (sB.sem, sB.n)
            bar_gen[0] += 1

        def bwait(*engs):
            for e in engs:
                if bar_seen[id(e)] < bar_gen[0]:
                    for sem, val in bar_set.values():
                        e.wait_ge(sem, val)
                    bar_seen[id(e)] = bar_gen[0]

        def emit_zero_output_and_stop():
            sE = Ctr()
            bwait(sy, ve)
            with nc.sbuf_tensor("zz", [128, O], F32) as zz:
                sVz = Ctr()
                sVz.inc(ve.memset(zz[:, :], 0.0))
                sVz.wait(sy)
                for k in range(KD):
                    sE.dma(sy.dma_start(out=y[128 * k:128 * k + 128, :], in_=zz[:, :]))
                sE.wait(sy)

        sIn = Ctr(); sCa = Ctr(); sV0 = Ctr(); sT0 = Ctr(); sG0 = Ctr(); sS0 = Ctr()
        ve.memset(smalls[:, :], 0.0)
        ve.memset(rows[:, :], 0.0)
        ve.memset(srow[:, :], 0.0)
        sV0.inc(ve.memset(bdca[:, :], 0.0))
        sV0.wait(sy)
        sIn.dma(sy.dma_start(out=gpk[:, :], in_=agam_pk[:, :]))
        sIn.dma(sy.dma_start(out=bpkf[:, :], in_=abet_pk[:, :]))
        sIn.dma(sy.dma_start(out=outpk[:, :], in_=about_pk[:, :]))
        sIn.dma(sy.dma_start(out=smalls[:, 0:1], in_=ab1[:, :]))
        sIn.dma(sy.dma_start(out=smalls[:, 1:2], in_=ab2d[:, :]))
        sIn.dma(sy.dma_start(out=smalls[:, 2:4], in_=abqkv[:, :]))
        sIn.dma(sy.dma_start(out=smalls[0:1, 4:5], in_=ab3[:, :]))
        sIn.dma(sy.dma_start(out=smalls[0:64, 5:6], in_=aW3c[:, :]))
        sIn.dma(sy.dma_start(out=ROW(0, 0, D), in_=blq1[:, :]))
        sIn.dma(sy.dma_start(out=ROW(1, 0, D), in_=blk1[:, :]))
        sIn.dma(sy.dma_start(out=ROW(2, 0, D), in_=blq2[:, :]))
        sIn.dma(sy.dma_start(out=ROW(3, 0, D), in_=blk2[:, :]))

        NCB = 3
        cstack = ExitStack()
        cbuf = cstack.enter_context(nc.sbuf_tensor("cbuf", [64, NCB, 4096], BF16, side="right"))
        wstack = ExitStack()
        w1t = wstack.enter_context(nc.sbuf_tensor("w1t", [128, KD, 128], BF16, side="right"))
        wqt = wstack.enter_context(nc.sbuf_tensor("wqt", [128, KD, 192], BF16, side="right"))
        for k in range(KD):
            sCa.dma(gp.dma_start(out=w1t[:, k, :], in_=aW1T[128 * k:128 * k + 128, :]))
            sCa.dma(gp.dma_start(out=wqt[:, k, :], in_=aWqkvT[128 * k:128 * k + 128, :]))
        sCa.dma(gp.dma_start(out=w2t[:, 0, :], in_=aW2T[:, :]))
        sCa.dma(gp.dma_start(out=wot[:, :], in_=aWoutT[:, :]))

        ve.memset(smalls[:, 13:14], float(EPS))
        ve.memset(onescol[:, :], 1.0)
        ve.memset(onescol_b[:, :], 1.0)
        ve.memset(onesrow[:, :], 1.0)
        ve.memset(onesrow_b[:, :], 1.0)
        ve.memset(onesTrow[:, :], 1.0)
        sV0.inc(ve.memset(w3s[:, :, :], 0.0))
        MEMSET_DONE = sV0.n
        gp.memset(idn[:, :], 1.0)
        sG0.inc(gp.affine_select(idn[:, :], idn[:, :], [[1, 128]], ALU.is_equal, 0.0,
                                 base=0, channel_multiplier=-1))
        gp.memset(noti[:, :], 1.0)
        sG0.inc(gp.affine_select(noti[:, :], noti[:, :], [[1, 128]], ALU.not_equal, 0.0,
                                 base=0, channel_multiplier=-1))
        gp.memset(dupm[:, :], 1.0)
        gp.affine_select(dupm[:, :], dupm[:, :], [[1, 64]], ALU.is_ge, 0.0,
                         base=0, channel_multiplier=-2)
        sG0.inc(gp.affine_select(dupm[:, :], dupm[:, :], [[-1, 64]], ALU.is_ge, 0.0,
                                 base=1, channel_multiplier=2))
        gp.memset(sel64[:, :, :], 1.0)
        gp.affine_select(sel64[:, :, :], sel64[:, :, :], [[-2, 32], [1, 64]],
                         ALU.is_equal, 0.0, base=1, channel_multiplier=-1)
        sG0.inc(gp.affine_select(sel64[:, :, :], sel64[:, :, :], [[0, 32], [0, 64]],
                                 ALU.is_ge, 0.0, base=-1, channel_multiplier=1))
        gp.memset(basis[:, :, :], 1.0)
        sG0.inc(gp.affine_select(basis[:, :, :], basis[:, :, :], [[-1, 32], [1, 32]],
                                 ALU.is_equal, 0.0, base=0, channel_multiplier=0))

        if DEBUG_STOP == "ld":
            with nc.sbuf_tensor("zzl", [128, O], F32) as zz:
                sVz = Ctr()
                sVz.inc(ve.memset(zz[:, :], 0.0))
                sVz.wait(sy)
                sIn.wait(sy); sCa.wait(sy)
                sV0.wait(sy); sG0.wait(sy)
                sE = Ctr()
                tg = tap("gpk", [128, 16], F32)
                if tg is not None:
                    sE.dma(sy.dma_start(out=tg[:, :], in_=gpk[:, :]))
                tp = tap("p0", [128, 16], F32)
                if tp is not None:
                    sE.dma(sy.dma_start(out=tp[:, :], in_=smalls[:, :]))
                for k in range(KD):
                    sE.dma(sy.dma_start(out=y[128 * k:128 * k + 128, :], in_=zz[:, :]))
                sE.wait(sy)
            wstack.close()
            cstack.close()
            flush()
            return nc, taps

        sIn.wait(ve)
        lr = ROW(0, 0, D)
        ve.tensor_tensor(lr, lr, ROW(1, 0, D), ALU.mult)
        ve.tensor_reduce(smalls[0:1, 7:8], lr, AX.X, ALU.add)
        ve.tensor_tensor(ROW(2, 0, D), ROW(2, 0, D), ROW(3, 0, D), ALU.mult)
        sV0.inc(ve.tensor_reduce(smalls[0:1, 8:9], ROW(2, 0, D), AX.X, ALU.add))
        sV0.wait(sc)
        sc.activation(smalls[0:1, 9:10], smalls[0:1, 7:8], AF.Exp)
        sS0.inc(sc.activation(smalls[0:1, 10:11], smalls[0:1, 8:9], AF.Exp))
        sS0.wait(ve)
        ve.tensor_tensor(smalls[0:1, 11:12], smalls[0:1, 9:10], smalls[0:1, 10:11],
                         ALU.subtract)
        ve.tensor_scalar(smalls[0:1, 11:12], smalls[0:1, 11:12], float(LAM_INIT),
                         None, ALU.add)
        sV0.inc(ve.tensor_scalar(smalls[0:1, 12:13], smalls[0:1, 11:12], -1.0,
                                 None, ALU.mult))

        with nc.psum_tensor("p0a", [128, 4], F32) as p0a:
            sCa.wait(te); sIn.wait(te)
            sV0.wait(te, MEMSET_DONE)
            sG0.wait(te); sG0.wait(ve); sG0.wait(sc)
            sV0.inc(ve.tensor_copy(bpk16[:, :], bpkf[:, :]))
            sV0.wait(te)
            for k in range(KD):
                te.matmul(p0a[0:128, 0:1], wqt[:, k, 0:128], bpk16[:, k:k + 1],
                          start=(k == 0), stop=(k == KD - 1))
            for k in range(KD):
                te.matmul(p0a[0:64, 1:2], wqt[:, k, 128:192], bpk16[:, k:k + 1],
                          start=(k == 0), stop=(k == KD - 1))
            sV0.wait(te)
            te.matmul(p0a[:, 2:3], onesrow[0:1, :], smalls[0:1, 4:5],
                      start=True, stop=True)
            te.matmul(p0a[0:64, 3:4], onesrow[0:1, 0:64], smalls[0:1, 12:13],
                      start=True, stop=True)
            sT0.tinc()
            sT0.wait(ve)
            ve.tensor_tensor(smalls[:, 2:3], smalls[:, 2:3], p0a[0:128, 0:1], ALU.add)
            ve.tensor_tensor(smalls[0:64, 3:4], smalls[0:64, 3:4], p0a[0:64, 1:2], ALU.add)
            ve.tensor_copy(smalls[:, 4:5], p0a[:, 2:3])
            ve.tensor_copy(smalls[0:64, 6:7], p0a[0:64, 3:4])
            for k in range(KD):
                ve.tensor_scalar(w1t[:, k, :], w1t[:, k, :], gpk[:, k:k + 1], None, ALU.mult)
                ve.tensor_scalar(wqt[:, k, :], wqt[:, k, :], gpk[:, k:k + 1], None, ALU.mult)
            ve.tensor_scalar(w2t[:, 1, :], w2t[:, 0, :], -1.0, None, ALU.mult)
            sV0.wait(ve)
            ve.tensor_copy(w3s[0:64, 0, 0:1], smalls[0:64, 5:6])
            for t in range(64):
                if t > 0:
                    ve.tensor_copy(w3s[0:64, t, 2 * t:2 * t + 1], w3s[0:64, 0, 0:1])
                ve.tensor_copy(w3s[64:128, t, 2 * t + 1:2 * t + 2], w3s[0:64, 0, 0:1])
            sV0.inc(ve.nop())

        if DEBUG_STOP == "p0b":
            with nc.sbuf_tensor("zzl2", [128, O], F32) as zz:
                sVz = Ctr()
                sVz.inc(ve.memset(zz[:, :], 0.0))
                sVz.wait(sy)
                sV0.wait(sy); sT0.wait(sy); sS0.wait(sy)
                sE = Ctr()
                tp = tap("p0", [128, 16], F32)
                if tp is not None:
                    sE.dma(sy.dma_start(out=tp[:, :], in_=smalls[:, :]))
                for k in range(KD):
                    sE.dma(sy.dma_start(out=y[128 * k:128 * k + 128, :], in_=zz[:, :]))
                sE.wait(sy)
            wstack.close()
            cstack.close()
            flush()
            return nc, taps

        sFW = Ctr()
        sFL = Ctr()
        segs = []
        for r in range(0, D, 64):
            for cc in range(4):
                segs.append((fgupT, gup16, r, 4096 * cc, 4096))
        for r in range(0, INTER, 64):
            segs.append((fdnT, dn16, r, 0, 2048))
        marks = []
        for si, (srcd, dstd, r, c0, w) in enumerate(segs):
            bi = si % NCB
            if si >= NCB and si % NCB == 0:
                sFW.wait(gp, marks[si - 1])
            sFL.dma(gp.dma_start(out=cbuf[:, bi, 0:w], in_=srcd[r:r + 64, c0:c0 + w]))
            sFL.wait(gp)
            sFW.dma(gp.dma_start(out=dstd[r:r + 64, c0:c0 + w], in_=cbuf[:, bi, 0:w]))
            marks.append(sFW.n)
        FW_DONE = sFW.n

        barrier(ve, te, sc, sy)

        t_ = tap("p0", [128, 16], F32)
        if t_ is not None:
            s_ = Ctr(); bwait(sy)
            s_.dma(sy.dma_start(out=t_[:, :], in_=smalls[:, :]))
            s_.wait(sy)
        t_ = tap("gpk", [128, 16], F32)
        if t_ is not None:
            s_ = Ctr(); bwait(sy)
            s_.dma(sy.dma_start(out=t_[:, :], in_=gpk[:, :]))
            s_.wait(sy)
        t_ = tap("rows01", [1, 4 * T], F32)
        if t_ is not None:
            s_ = Ctr(); bwait(sy)
            s_.dma(sy.dma_start(out=t_[:, :], in_=rows[:, :]))
            s_.wait(sy)
        if DEBUG_STOP == "p0":
            with nc.sbuf_tensor("zzp", [128, O], F32) as zz:
                sVz = Ctr()
                sVz.inc(ve.memset(zz[:, :], 0.0))
                sVz.wait(sy)
                sE = Ctr()
                for k in range(KD):
                    sE.dma(sy.dma_start(out=y[128 * k:128 * k + 128, :], in_=zz[:, :]))
                sE.wait(sy)
            wstack.close()
            cstack.close()
            flush()
            return nc, taps

        xna = wstack.enter_context(nc.sbuf_tensor("xna", [128, KD, T], BF16, side="right"))
        sl1a = ExitStack()
        with sl1a:
            x0c = sl1a.enter_context(nc.sbuf_tensor("x0c", [128, KD, 512], F32))
            sqc = sl1a.enter_context(nc.sbuf_tensor("sqc", [128, 2, 512], F32))
            psS1 = sl1a.enter_context(nc.psum_tensor("psS1", [1, 512], F32))
            psS2 = sl1a.enter_context(nc.psum_tensor("psS2", [1, 512], F32))
            psBC = sl1a.enter_context(nc.psum_tensor("psBC", [128, 1024], F32))
            sX = Ctr(); sVa = Ctr(); sTa = Ctr(); sSa = Ctr()
            bwait(sy, te, sc, ve)
            for c in range(4):
                src = xT_own if c < 2 else xT_oth
                col0 = (c % 2) * 512
                t0 = 512 * c
                for k in range(KD):
                    sX.dma(sy.dma_start(out=x0c[:, k, :],
                                        in_=src[128 * k:128 * k + 128, col0:col0 + 512]))
                sX.wait(sc); sX.wait(te)
                temark = sTa.n
                for k in range(KD):
                    if k >= 2:
                        sTa.wait(sc, temark + k - 1)
                    sSa.inc(sc.activation(sqc[:, k % 2, :], x0c[:, k, :], AF.Square))
                    te.matmul(psS1[:, :], onescol[:, :], x0c[:, k, :],
                              start=(k == 0), stop=(k == KD - 1))
                    sSa.wait(te)
                    te.matmul(psS2[:, :], onescol[:, :], sqc[:, k % 2, :],
                              start=(k == 0), stop=(k == KD - 1))
                    sTa.tinc()
                sTa.wait(ve)
                m, v, t1, v2 = SR(0), SR(1), SR(2), SR(3)
                ve.tensor_scalar(m, psS1[:, :], 1.0 / D, None, ALU.mult)
                ve.tensor_scalar(v, psS2[:, :], 1.0 / D, None, ALU.mult)
                ve.tensor_tensor(t1, m, m, ALU.mult)
                sVa.inc(ve.tensor_tensor(v, v, t1, ALU.subtract))
                sVa.wait(sc)
                sSa.inc(sc.activation(t1, v, AF.Sqrt, bias=smalls[0:1, 13:14]))
                sSa.wait(ve)
                ve.reciprocal(ROW(2, t0, 512), t1)
                ve.tensor_tensor(ROW(3, t0, 512), m, ROW(2, t0, 512), ALU.mult)
                ve.tensor_tensor(v2, v, ROW(2, t0, 512), ALU.mult)
                sVa.inc(ve.tensor_tensor(v2, v2, ROW(2, t0, 512), ALU.mult))
                sVa.wait(sc)
                sSa.inc(sc.activation(v2, v2, AF.Sqrt, bias=smalls[0:1, 13:14]))
                sSa.wait(ve)
                ve.reciprocal(v2, v2)
                ve.tensor_tensor(ROW(0, t0, 512), ROW(2, t0, 512), v2, ALU.mult)
                ve.tensor_tensor(ROW(1, t0, 512), ROW(3, t0, 512), v2, ALU.mult)
                sVa.inc(ve.nop())
                sVa.wait(te)
                te.matmul(psBC[:, 0:512], onesrow[:, :], ROW(0, t0, 512),
                          start=True, stop=True)
                te.matmul(psBC[:, 512:1024], onesrow[:, :], ROW(1, t0, 512),
                          start=True, stop=True)
                sTa.tinc()
                sTa.wait(ve)
                for k in range(KD):
                    ve.tensor_tensor(sqc[:, k % 2, :], x0c[:, k, :], psBC[:, 0:512],
                                     ALU.mult)
                    ve.tensor_tensor(xna[:, k, t0:t0 + 512], sqc[:, k % 2, :],
                                     psBC[:, 512:1024], ALU.subtract)
                sVa.inc(ve.nop())
                sVa.wait(sy); sVa.wait(te); sVa.wait(sc)
        barrier(ve, te, sc, sy)

        t_ = tap("xna", [D, T], BF16)
        if t_ is not None:
            s_ = Ctr(); bwait(sy)
            for k in range(KD):
                s_.dma(sy.dma_start(out=t_[128 * k:128 * k + 128, :], in_=xna[:, k, :]))
            s_.wait(sy)
        t_ = tap("rows_a", [1, 4 * T], F32)
        if t_ is not None:
            s_ = Ctr(); bwait(sy)
            s_.dma(sy.dma_start(out=t_[:, :], in_=rows[:, :]))
            s_.wait(sy)
        if DEBUG_STOP == "sl1a":
            emit_zero_output_and_stop()
            wstack.close()
            cstack.close()
            flush()
            return nc, taps

        sl1stack = ExitStack()
        hT = sl1stack.enter_context(nc.sbuf_tensor("hT", [128, T], BF16))
        Hb = sl1stack.enter_context(nc.sbuf_tensor("Hb", [128, T], F32))
        kqv = sl1stack.enter_context(nc.sbuf_tensor("kqv", [128, T], BF16))
        vhat = sl1stack.enter_context(nc.sbuf_tensor("vhat", [128, KD, 96], BF16))
        out_aT = sl1stack.enter_context(nc.sbuf_tensor("out_aT", [64, T], BF16))
        sl1b = ExitStack()
        with sl1b:
            psH = sl1b.enter_context(nc.psum_tensor("psH", [128, 1024], F32))
            psT2 = sl1b.enter_context(nc.psum_tensor("psT2", [128, 128], BF16))
            psMQ = sl1b.enter_context(nc.psum_tensor("psMQ", [128, 128], F32))
            vrows = sl1b.enter_context(nc.sbuf_tensor("vrows", [64, T], BF16))
            sVb = Ctr(); sTb = Ctr(); sCb = Ctr(); sSb = Ctr()
            bwait(te, ve, sc, gp)
            sV0.wait(te)
            for half in range(2):
                cols = slice(1024 * half, 1024 * half + 1024)
                for nn2 in range(2):
                    n0 = 1024 * half + 512 * nn2
                    for k in range(KD):
                        te.matmul(psH[:, 512 * nn2:512 * nn2 + 512], w1t[:, k, :],
                                  xna[:, k, n0:n0 + 512], start=(k == 0), stop=(k == KD - 1))
                sTb.tinc()
                sTb.wait(ve)
                sVb.inc(ve.tensor_copy(hT[:, cols], psH[:, :]))
                sVb.wait(te)
            for half in range(2):
                cols = slice(1024 * half, 1024 * half + 1024)
                for nn2 in range(2):
                    n0 = 1024 * half + 512 * nn2
                    for k in range(KD):
                        te.matmul(psH[:, 512 * nn2:512 * nn2 + 512], wqt[:, k, 0:128],
                                  xna[:, k, n0:n0 + 512], start=(k == 0), stop=(k == KD - 1))
                sTb.tinc()
                sTb.wait(ve)
                sVb.inc(ve.tensor_scalar(kqv[:, cols], psH[:, :], smalls[:, 2:3],
                                         None, ALU.add))
                sVb.wait(te)
                for nn2 in range(2):
                    n0 = 1024 * half + 512 * nn2
                    for k in range(KD):
                        te.matmul(psH[0:64, 512 * nn2:512 * nn2 + 512], wqt[:, k, 128:192],
                                  xna[:, k, n0:n0 + 512], start=(k == 0), stop=(k == KD - 1))
                sTb.tinc()
                sTb.wait(ve)
                sVb.inc(ve.tensor_scalar(vrows[:, cols], psH[0:64, :],
                                         smalls[0:64, 3:4], None, ALU.add))
                sVb.wait(te)
            sVb.inc(ve.tensor_scalar(Hb[:, :], hT[:, :], smalls[:, 0:1], None, ALU.add))
            for jc in range(KD):
                sTb.inc(te.transpose(psT2[:, 0:64], vrows[:, 128 * jc:128 * jc + 128],
                                     idn[0:64, 0:64]))
                sTb.wait(ve)
                dst = vhat[:, jc, :].rearrange("p (h t) -> p h t", t=3)
                ve.tensor_copy(dst[:, :, 1:3],
                               psT2[:, 0:64].rearrange("p (h t) -> p h t", t=2))
                sVb.inc(ve.memset(dst[:, :, 0:1], 1.0))
                sVb.wait(te)
            t_ = tap("kqv", [256, T], BF16)
            if t_ is not None:
                s_ = Ctr()
                sVb.wait(sy)
                s_.dma(sy.dma_start(out=t_[0:128, :], in_=kqv[:, :]))
                s_.dma(sy.dma_start(out=t_[128:192, :], in_=vrows[:, :]))
                s_.wait(sy)
            with nc.sbuf_tensor("mqk", [128, 4], F32) as mqk, \
                 nc.sbuf_tensor("mrow", [1, 128], F32) as mrow, \
                 nc.sbuf_tensor("mtile", [128, 128], F32) as mtile:
                sVb.inc(ve.tensor_reduce(mqk[:, 0:1], kqv[:, :], AX.X, ALU.max,
                                         apply_absolute_value=True))
                sVb.wait(sy)
                sKmv = Ctr()
                sKmv.dma(sy.dma_start(out=mqk[0:64, 2:3], in_=mqk[64:128, 0:1]))
                sKmv.wait(ve)
                ve.tensor_tensor(mqk[0:64, 1:2], mqk[0:64, 0:1], mqk[0:64, 2:3],
                                 ALU.mult)
                ve.memset(mtile[:, :], 0.0)
                sVb.inc(ve.tensor_copy(mtile[0:64, 0:1], mqk[0:64, 1:2]))
                with nc.sbuf_tensor("mtb", [128, 128], BF16) as mtb:
                    sVb.wait(ve)
                    ve.tensor_copy(mtb[:, :], mtile[:, :])
                    sVb.inc(ve.nop())
                    sVb.wait(te)
                    sTb.inc(te.transpose(psT2[:, :], mtb[:, :], idn[:, :]))
                    sTb.wait(ve)
                    ve.tensor_copy(mrow[0:1, :], psT2[0:1, :])
                    sVb.inc(ve.tensor_reduce(
                        SR(0, 32),
                        mrow[0:1, 0:64].rearrange("a (h t) -> a h t", t=2),
                        AX.X, ALU.add))
                    ve.tensor_scalar(SR(0, 32), SR(0, 32), -RSQ2, None, ALU.mult)
                    sVb.inc(ve.nop())
                    sVb.wait(te)
                    te.matmul(psMQ[:, 0:32], onesrow[0:1, :], SR(0, 32),
                              start=True, stop=True)
                    sTb.tinc()
                    sTb.wait(ve)
                    sVb.inc(ve.tensor_copy(bdca[:, :], psMQ[:, 0:32]))
                    sVb.wait(te)
        barrier(ve, te, sc, gp, sy)
        wstack.close()

        if DEBUG_STOP == "sl1b":
            emit_zero_output_and_stop()
            sl1stack.close()
            wstack.close()
            cstack.close()
            flush()
            return nc, taps

        t_ = tap("hT", [128, T], BF16)
        if t_ is not None:
            s_ = Ctr(); bwait(sy)
            s_.dma(sy.dma_start(out=t_[:, :], in_=hT[:, :]))
            s_.wait(sy)

        sl1c = ExitStack()
        with sl1c:
            EC = sl1c.enter_context
            p1a = EC(nc.sbuf_tensor("p1a", [128, 2, T], BF16))
            p1b = EC(nc.sbuf_tensor("p1b", [128, 2, T], BF16))
            rp2 = EC(nc.sbuf_tensor("rp2", [128, 2, 1024], BF16))
            simb4 = EC(nc.sbuf_tensor("simb4", [128, 4, T], F32))
            bst = EC(nc.sbuf_tensor("bst", [128, 8, 4], F32))
            selT = EC(nc.sbuf_tensor("selT", [128, KD, 512], BF16))
            ksl = EC(nc.sbuf_tensor("ksl", [2, 2, T], BF16))
            qsl = EC(nc.sbuf_tensor("qsl", [2, 2, 512], BF16))
            wts = EC(nc.sbuf_tensor("wts", [128, 2, 1024], BF16))
            aou = EC(nc.sbuf_tensor("aou", [64, 512], BF16))
            zinv = EC(nc.sbuf_tensor("zinv", [32, 512], F32))
            aor = aou[0:3, :]
            sSc = Ctr(); sVc = Ctr(); sTc = Ctr(); sKc2 = [Ctr(), Ctr()]
            bwait(te, ve, sc, sy)
            for sb in range(4):
                icols = slice(512 * sb, 512 * sb + 512)
                pstk = ExitStack()
                psPP = pstk.enter_context(nc.psum_tensor(f"psPP{sb}", [128, 1024], F32))
                psSM = pstk.enter_context(nc.psum_tensor(f"psSM{sb}", [128, 2048], F32))
                for bl in range(4):
                    blk = 4 * sb + bl
                    i0 = 128 * blk
                    relu_marks = []
                    for t in range(64):
                        ia, ib = i0 + 2 * t, i0 + 2 * t + 1
                        pa, pb_ = p1a[:, t % 2, :], p1b[:, t % 2, :]
                        if t >= 2:
                            sTc.wait(sc, relu_marks[t - 2][1])
                            sTc.wait(ve, relu_marks[t - 2][1])
                        sSc.inc(sc.activation(pa, hT[:, :], AF.Relu,
                                              bias=Hb[:, ia:ia + 1], scale=-1.0))
                        sVc.inc(ve.tensor_scalar(pb_, hT[:, :], Hb[:, ib:ib + 1],
                                                 0.0, ALU.subtract, ALU.min))
                        sSc.wait(te); sVc.wait(te)
                        w2m = None
                        for jh in range(2):
                            pp = psPP[:, :]
                            j0 = 1024 * jh
                            for nn2 in range(2):
                                js = slice(512 * nn2, 512 * nn2 + 512)
                                jsrc = slice(j0 + 512 * nn2, j0 + 512 * nn2 + 512)
                                te.matmul(pp[0:64, js], w2t[:, 0, :], pa[:, jsrc],
                                          start=True, stop=True, tile_position=(0, 0))
                                te.matmul(pp[64:128, js], w2t[:, 1, :], pb_[:, jsrc],
                                          start=True, stop=True, tile_position=(0, 64))
                            m_ = sTc.tinc()
                            if jh == 0:
                                w2m = m_
                            sTc.wait(sc)
                            sSc.inc(sc.activation(rp2[:, jh, :], pp[:, :], AF.Relu,
                                                  bias=smalls[:, 1:2]))
                            sSc.wait(te)
                            for nn2 in range(2):
                                js = slice(512 * nn2, 512 * nn2 + 512)
                                jsim = slice(j0 + 512 * nn2, j0 + 512 * nn2 + 512)
                                te.matmul(psSM[:, jsim], w3s[:, t, :], rp2[:, jh, js],
                                          start=(t == 0), stop=(t == 63))
                        relu_marks.append((w2m, sTc.tinc()))
                    sTc.wait(ve)
                    simv = simb4[:, bl, :]
                    ve.tensor_scalar(simv, psSM[:, :], smalls[:, 4:5], None, ALU.add)
                    sVc.inc(ve.tensor_tensor(simv[:, i0:i0 + 128], simv[:, i0:i0 + 128],
                                             noti[:, :], ALU.mult))
                    if blk == 0:
                        tsim = tap("sim0", [128, T], F32)
                        if tsim is not None:
                            sVc.wait(sy)
                            s_ = Ctr()
                            s_.dma(sy.dma_start(out=tsim[:, :], in_=simv))
                            s_.wait(ve)
                LO, HI, MID, CNT, G, T5, T6 = range(7)
                def B4(i):
                    return bst[:, i, :]
                for bl in range(4):
                    ve.tensor_reduce(bst[:, HI, bl:bl + 1], simb4[:, bl, :], AX.X, ALU.max)
                    ve.tensor_reduce(bst[:, LO, bl:bl + 1], simb4[:, bl, :], AX.X, ALU.min)
                ve.tensor_scalar(B4(T5), B4(HI), -1.0, None, ALU.mult)
                ve.tensor_tensor(B4(T5), B4(T5), B4(HI), ALU.max)
                ve.tensor_scalar(B4(T5), B4(T5), 1e-5, 1e-20, ALU.mult, ALU.add)
                ve.tensor_tensor(B4(HI), B4(HI), B4(T5), ALU.add)
                jview = wts[:, :, :].rearrange("p a b -> p (a b)")
                for it in range(NBIS):
                    ve.tensor_tensor(B4(MID), B4(LO), B4(HI), ALU.add)
                    ve.tensor_scalar(B4(MID), B4(MID), 0.5, None, ALU.mult)
                    for bl in range(4):
                        ve.tensor_scalar(jview, simb4[:, bl, :], bst[:, MID, bl:bl + 1],
                                         0.0, ALU.is_ge, ALU.add,
                                         accum_out=bst[:, CNT, bl:bl + 1])
                    ve.tensor_scalar(B4(G), B4(CNT), float(KSP) - 0.5, None, ALU.is_ge)
                    ve.tensor_tensor(B4(T5), B4(MID), B4(LO), ALU.subtract)
                    ve.tensor_tensor(B4(T5), B4(T5), B4(G), ALU.mult)
                    ve.tensor_tensor(B4(LO), B4(LO), B4(T5), ALU.add)
                    ve.tensor_tensor(B4(T5), B4(MID), B4(HI), ALU.subtract)
                    ve.tensor_tensor(B4(T6), B4(T5), B4(G), ALU.mult)
                    ve.tensor_tensor(B4(T5), B4(T5), B4(T6), ALU.subtract)
                    ve.tensor_tensor(B4(HI), B4(HI), B4(T5), ALU.add)
                for bl in range(4):
                    ve.tensor_scalar(jview, simb4[:, bl, :], bst[:, LO, bl:bl + 1],
                                     None, ALU.is_ge)
                    sVc.inc(ve.nop())
                    sVc.wait(te)
                    with nc.psum_tensor(f"psSL_{sb}_{bl}", [128, 128], BF16) as psSL:
                        for jc in range(KD):
                            te.transpose(
                                psSL[:, :],
                                wts[:, jc // 8, 128 * (jc % 8):128 * (jc % 8) + 128],
                                idn[:, :])
                            sTc.tinc()
                            sTc.wait(ve)
                            sVc.inc(ve.tensor_copy(selT[:, jc, 128 * bl:128 * bl + 128],
                                                   psSL[:, :]))
                            sVc.wait(te)
                pstk.close()
                if sb == 0:
                    tsel = tap("sel", [128, KD * 512], BF16)
                    if tsel is not None:
                        sVc.wait(sy)
                        s_ = Ctr()
                        s_.dma(sy.dma_start(
                            out=tsel[:, :],
                            in_=selT[:, :, :].rearrange("p k n -> p (k n)")))
                        s_.wait(sy)
                with nc.psum_tensor(f"psSC_{sb}", [128, 4, 512], F32) as psSC, \
                     nc.psum_tensor(f"psAO_{sb}", [128, 512], F32) as psAO, \
                     nc.psum_tensor(f"psZP_{sb}", [32, 512], F32) as psZP, \
                     nc.psum_tensor(f"psOA_{sb}", [64, 512], F32) as psOA, \
                     nc.psum_tensor(f"psZD_{sb}", [64, 512], F32) as psZD:
                    kq_marks = {}
                    def fetch_head(h):
                        bi = h % 2
                        c = sKc2[bi]
                        c.dma(sy.dma_start(out=ksl[0:2, bi, :],
                                           in_=kqv[64 + 2 * h:64 + 2 * h + 2, :]))
                        c.dma(sy.dma_start(out=qsl[0:2, bi, :],
                                           in_=kqv[2 * h:2 * h + 2, icols]))
                        kq_marks[h] = c.n
                    sTc.wait(sy)
                    fetch_head(0)
                    fetch_head(1)
                    for h in range(NHA):
                        bi = h % 2
                        sKc2[bi].wait(te, kq_marks[h])
                        for jt2 in range(KD // 2):
                            par = jt2 % 2
                            for u in range(2):
                                jt = 2 * jt2 + u
                                te.matmul(psSC[:, 2 * par + u, :],
                                          ksl[0:2, bi, 128 * jt:128 * jt + 128],
                                          qsl[0:2, bi, :], start=True, stop=True)
                            sTc.tinc()
                            sTc.wait(sc)
                            sSc.inc(sc.activation(
                                wts[:, par, :],
                                psSC[:, 2 * par:2 * par + 2, :].rearrange("p a b -> p (a b)"),
                                AF.Exp, scale=RSQ2, bias=bdca[:, h:h + 1]))
                            sSc.wait(ve)
                            sVc.inc(ve.tensor_tensor(
                                wts[:, par, :], wts[:, par, :],
                                selT[:, 2 * jt2:2 * jt2 + 2, :].rearrange("p a b -> p (a b)"),
                                ALU.mult))
                            sVc.wait(te)
                            for u in range(2):
                                jt = 2 * jt2 + u
                                te.matmul(psAO[0:3, :], vhat[:, jt, 3 * h:3 * h + 3],
                                          wts[:, par, 512 * u:512 * u + 512],
                                          start=(jt == 0), stop=(jt == KD - 1))
                            sTc.tinc()
                        aomark = sTc.tinc()
                        if h + 2 < NHA:
                            sTc.wait(sy, aomark)
                            fetch_head(h + 2)
                        sTc.wait(ve)
                        sVc.inc(ve.tensor_copy(aor, psAO[0:3, :]))
                        sVc.wait(te)
                        te.matmul(psZP[:, :], basis[0:1, h, :], aor[0:1, :],
                                  start=(h == 0), stop=(h == NHA - 1))
                        te.matmul(psOA[:, :], sel64[:, h, :], aor,
                                  start=(h == 0), stop=(h == NHA - 1))
                        sTc.tinc()
                        sTc.wait(ve)
                    sTc.wait(ve)
                    ve.reciprocal(zinv[:, :], psZP[:, :])
                    ve.tensor_copy(aou[:, :], psOA[:, :])
                    sVc.inc(ve.nop())
                    sVc.wait(te)
                    te.matmul(psZD[:, :], dupm[:, :], zinv[:, :], start=True, stop=True)
                    sTc.tinc()
                    sTc.wait(ve)
                    sVc.inc(ve.tensor_tensor(out_aT[:, icols], aou[:, :], psZD[:, :],
                                             ALU.mult))
                    sVc.wait(te)
        barrier(ve, te, sc, sy)

        if DEBUG_STOP == "sl1c":
            emit_zero_output_and_stop()
            sl1stack.close()
            cstack.close()
            flush()
            return nc, taps

        sl1d = ExitStack()
        with sl1d:
            ED = sl1d.enter_context
            x0c2 = ED(nc.sbuf_tensor("x0c2", [128, 512], F32))
            x1c = ED(nc.sbuf_tensor("x1c", [128, 512], F32))
            x1cb = ED(nc.sbuf_tensor("x1cb", [128, 512], BF16))
            sq2 = ED(nc.sbuf_tensor("sq2", [128, 512], F32))
            psO = ED(nc.psum_tensor("psO", [128, 2, 512], F32))
            psBC2 = ED(nc.psum_tensor("psBC2", [128, 1024], F32))
            psS1b = ED(nc.psum_tensor("psS1b", [1, 512], F32))
            psS2b = ED(nc.psum_tensor("psS2b", [1, 512], F32))
            sVd = Ctr(); sTd = Ctr(); sSd = Ctr(); sXd = Ctr(); sStd = Ctr()
            t_x1 = tap("x1", [D, O], F32)
            bwait(te, ve, sc, sy)
            for tc in range(4):
                t0 = 512 * tc
                own = tc < 2
                src = xT_own if own else xT_oth
                col0 = (tc % 2) * 512
                sVd.wait(te); sSd.wait(te)
                te.matmul(psBC2[:, 0:512], onesrow[:, :], ROW(2, t0, 512),
                          start=True, stop=True)
                te.matmul(psBC2[:, 512:1024], onesrow[:, :], ROW(3, t0, 512),
                          start=True, stop=True)
                sTd.tinc()
                for dc in range(KD):
                    sXd.dma(sy.dma_start(out=x0c2[:, :],
                                         in_=src[128 * dc:128 * dc + 128, col0:col0 + 512]))
                    te.matmul(psO[:, dc % 2, :], wot[:, 128 * dc:128 * dc + 128],
                              out_aT[:, t0:t0 + 512], start=True, stop=True)
                    tdm = sTd.tinc()
                    sXd.wait(ve); sTd.wait(ve, tdm)
                    if dc > 0 or tc > 0:
                        sStd.wait(ve, sStd.n)
                        sSd.wait(ve)
                    ve.tensor_tensor(x1c[:, :], x0c2[:, :], psBC2[:, 0:512], ALU.mult)
                    ve.tensor_tensor(x1c[:, :], x1c[:, :], psBC2[:, 512:1024], ALU.subtract)
                    ve.tensor_tensor(x1c[:, :], x1c[:, :], x0c2[:, :], ALU.add)
                    ve.tensor_tensor(x1c[:, :], x1c[:, :], psO[:, dc % 2, :], ALU.add)
                    ve.tensor_scalar(x1c[:, :], x1c[:, :], outpk[:, dc:dc + 1], None, ALU.add)
                    if not own:
                        ve.tensor_copy(x1cb[:, :], x1c[:, :])
                    sVd.inc(ve.nop())
                    sVd.wait(sy); sVd.wait(sc)
                    if own:
                        sStd.dma(sy.dma_start(out=x1of[128 * dc:128 * dc + 128, col0:col0 + 512],
                                              in_=x1c[:, :]))
                        if t_x1 is not None:
                            sStd.dma(sy.dma_start(
                                out=t_x1[128 * dc:128 * dc + 128, col0:col0 + 512],
                                in_=x1c[:, :]))
                    else:
                        sStd.dma(sy.dma_start(out=x1pb[128 * dc:128 * dc + 128, col0:col0 + 512],
                                              in_=x1cb[:, :]))
                    sSd.inc(sc.activation(sq2[:, :], x1c[:, :], AF.Square))
                    sVd.wait(te)
                    te.matmul(psS1b[:, :], onescol[:, :], x1c[:, :],
                              start=(dc == 0), stop=(dc == KD - 1))
                    sSd.wait(te)
                    te.matmul(psS2b[:, :], onescol[:, :], sq2[:, :],
                              start=(dc == 0), stop=(dc == KD - 1))
                    sTd.tinc()
                sTd.wait(ve)
                m, v, t1 = SR(0), SR(1), SR(2)
                ve.tensor_scalar(m, psS1b[:, :], 1.0 / D, None, ALU.mult)
                ve.tensor_scalar(v, psS2b[:, :], 1.0 / D, None, ALU.mult)
                ve.tensor_tensor(t1, m, m, ALU.mult)
                sVd.inc(ve.tensor_tensor(v, v, t1, ALU.subtract))
                sVd.wait(sc)
                sSd.inc(sc.activation(t1, v, AF.Sqrt, bias=smalls[0:1, 13:14]))
                sSd.wait(ve)
                ve.reciprocal(ROW(0, t0, 512), t1)
                ve.tensor_tensor(ROW(1, t0, 512), m, ROW(0, t0, 512), ALU.mult)
                sVd.inc(ve.nop())
            sStd.wait(sy)
        barrier(ve, te, sc, sy)



        if DEBUG_STOP == "sl1":
            emit_zero_output_and_stop()
            sl1stack.close()
            cstack.close()
            flush()
            return nc, taps

        def emit_ffn(tag, tokens, src_f32, src_b16, rowA, rowB, want_stats,
                     dst_f32, dst_b16, to_output):
            HT = 512
            sTf = Ctr(); sVf = Ctr(); sSf = Ctr(); sWf2 = [Ctr(), Ctr()]; sXf = Ctr()
            sDf2 = [Ctr(), Ctr()]; sOf = Ctr()
            es = ExitStack()
            with es:
                sy.wait_ge(sFW.sem, FW_DONE)
                EF = es.enter_context
                xnf = EF(nc.sbuf_tensor(f"xnf{tag}", [128, KD, HT], BF16))
                hbuf = EF(nc.sbuf_tensor(f"hbuf{tag}", [128, 64, HT], BF16))
                wgs = EF(nc.sbuf_tensor(f"wgs{tag}", [128, 2, KD, 256], BF16))
                wus = EF(nc.sbuf_tensor(f"wus{tag}", [128, 2, KD, 256], BF16))
                xrc = EF(nc.sbuf_tensor(f"xrc{tag}", [128, 512], F32))
                xrcb = EF(nc.sbuf_tensor(f"xrcb{tag}", [128, 512], BF16))
                stmp = EF(nc.sbuf_tensor(f"stmp{tag}", [128, HT], BF16))
                sqf = EF(nc.sbuf_tensor(f"sqf{tag}", [128, 512], F32))
                dts = EF(nc.sbuf_tensor(f"dts{tag}", [128, 2, 512], BF16))
                for hf in range(tokens // HT):
                    tok0 = HT * hf
                    own = (tok0 < O)
                    gcol0 = tok0 % O
                    with nc.psum_tensor(f"psBCf{tag}{hf}", [128, 1024], F32) as psBCf:
                        bwait(te, ve, sy, sc, gp)
                        sVf.wait(te)
                        te.matmul(psBCf[:, 0:512], onesrow[:, :], ROW(rowA, tok0, 512),
                                  start=True, stop=True)
                        te.matmul(psBCf[:, 512:1024], onesrow[:, :], ROW(rowB, tok0, 512),
                                  start=True, stop=True)
                        sTf.tinc()
                        sTf.wait(ve)
                        for k in range(KD):
                            src = src_f32 if own else src_b16
                            sXf.dma(sy.dma_start(
                                out=(xrc[:, :] if own else xrcb[:, :]),
                                in_=src[128 * k:128 * k + 128, gcol0:gcol0 + 512]))
                            sXf.wait(ve)
                            xin = xrc[:, :] if own else xrcb[:, :]
                            ve.tensor_tensor(sqf[:, :], xin, psBCf[:, 0:512], ALU.mult)
                            ve.tensor_tensor(xnf[:, k, :], sqf[:, :],
                                             psBCf[:, 512:1024], ALU.subtract)
                            sVf.inc(ve.nop())
                            sVf.wait(sy)
                        sVf.wait(te)
                    with nc.psum_tensor(f"psG{tag}{hf}", [128, 512], F32) as psG, \
                         nc.psum_tensor(f"psU{tag}{hf}", [128, 512], F32) as psU:
                        gu_marks = []
                        w_marks = {}

                        def prefetch(si, buf):
                            c0 = 256 * si
                            c = sWf2[buf]
                            for k in range(KD):
                                c.dma(sy.dma_start(out=wgs[:, buf, k, :],
                                                   in_=gup16[128 * k:128 * k + 128,
                                                             c0:c0 + 256]))
                                c.dma(sy.dma_start(out=wus[:, buf, k, :],
                                                   in_=gup16[128 * k:128 * k + 128,
                                                             INTER + c0:INTER + c0 + 256]))
                            w_marks[si] = c.n
                        if hf == 0 and tag == "f1":
                            sy.wait_ge(sFW.sem, FW_DONE)
                        prefetch(0, 0)
                        for si in range(32):
                            if si + 1 < 32:
                                if si >= 1:
                                    sTf.wait(sy, gu_marks[si - 1])
                                prefetch(si + 1, (si + 1) % 2)
                            sWf2[si % 2].wait(te, w_marks[si])
                            for fi in range(2):
                                fc = 2 * si + fi
                                fcol = 128 * fi
                                for k in range(KD):
                                    te.matmul(psG[:, :], wgs[:, si % 2, k, fcol:fcol + 128],
                                              xnf[:, k, :], start=(k == 0), stop=(k == KD - 1))
                                for k in range(KD):
                                    te.matmul(psU[:, :], wus[:, si % 2, k, fcol:fcol + 128],
                                              xnf[:, k, :], start=(k == 0), stop=(k == KD - 1))
                                sTf.tinc()
                                sTf.wait(sc)
                                sSf.inc(sc.activation(stmp[:, :], psG[:, :], AF.Silu))
                                sSf.wait(ve)
                                sVf.inc(ve.tensor_tensor(hbuf[:, fc, :], stmp[:, :],
                                                         psU[:, :], ALU.mult))
                                sVf.wait(te)
                            gu_marks.append(sTf.n)
                    for g in range(4):
                        with nc.psum_tensor(f"psD{tag}{hf}{g}", [128, 4, 512], F32) as psD:
                            dmarks = []
                            dl_marks = {}
                            sDf2[0].dma(sy.dma_start(out=dts[:, 0, :],
                                                     in_=dn16[0:128, 512 * g:512 * g + 512]))
                            dl_marks[0] = sDf2[0].n
                            for ic in range(64):
                                if ic + 1 < 64:
                                    if ic >= 1:
                                        sTf.wait(sy, dmarks[ic - 1])
                                    c = sDf2[(ic + 1) % 2]
                                    c.dma(sy.dma_start(
                                        out=dts[:, (ic + 1) % 2, :],
                                        in_=dn16[128 * (ic + 1):128 * (ic + 1) + 128,
                                                 512 * g:512 * g + 512]))
                                    dl_marks[ic + 1] = c.n
                                sDf2[ic % 2].wait(te, dl_marks[ic])
                                for dci in range(4):
                                    te.matmul(psD[:, dci, :],
                                              dts[:, ic % 2, 128 * dci:128 * dci + 128],
                                              hbuf[:, ic, :], start=(ic == 0), stop=(ic == 63))
                                dmarks.append(sTf.tinc())
                            for dci in range(4):
                                dc = 4 * g + dci
                                src = src_f32 if own else src_b16
                                sOf.wait(sy)
                                sXf.dma(sy.dma_start(
                                    out=(xrc[:, :] if own else xrcb[:, :]),
                                    in_=src[128 * dc:128 * dc + 128, gcol0:gcol0 + 512]))
                                sXf.wait(ve)
                                sTf.wait(ve)
                                if dci > 0:
                                    sOf.wait(ve, sOf.n)
                                    sSf.wait(ve)
                                xin = xrc[:, :] if own else xrcb[:, :]
                                ve.tensor_tensor(xrc[:, :], xin, psD[:, dci, :], ALU.add)
                                if (not own) and (dst_b16 is not None):
                                    ve.tensor_copy(xrcb[:, :], xrc[:, :])
                                sVf.inc(ve.nop())
                                sVf.wait(sy); sVf.wait(sc)
                                if to_output:
                                    sOf.dma(sy.dma_start(
                                        out=y[128 * dc:128 * dc + 128, gcol0:gcol0 + 512],
                                        in_=xrc[:, :]))
                                elif own:
                                    sOf.dma(sy.dma_start(
                                        out=dst_f32[128 * dc:128 * dc + 128, gcol0:gcol0 + 512],
                                        in_=xrc[:, :]))
                                else:
                                    sOf.dma(sy.dma_start(
                                        out=dst_b16[128 * dc:128 * dc + 128, gcol0:gcol0 + 512],
                                        in_=xrcb[:, :]))
                    sOf.wait(sy)
                barrier(ve, te, sc, sy)

        sl1stack.close()
        cstack.close()
        emit_ffn("f1", 2048, x1of, x1pb, 0, 1, False, x2of, x2pb, False)

        sl2s = ExitStack()
        with sl2s:
            ES2 = sl2s.enter_context
            xc3 = ES2(nc.sbuf_tensor("xc3", [128, 512], F32))
            xc3b = ES2(nc.sbuf_tensor("xc3b", [128, 512], BF16))
            sq3 = ES2(nc.sbuf_tensor("sq3", [128, 2, 512], F32))
            psS13 = ES2(nc.psum_tensor("psS13", [1, 512], F32))
            psS23 = ES2(nc.psum_tensor("psS23", [1, 512], F32))
            sX3 = Ctr(); sT3 = Ctr(); sS3 = Ctr(); sV3 = Ctr(); sTp = Ctr()
            t_x2 = tap("x2", [D, O], F32)
            bwait(te, ve, sc, sy)
            for tc in range(4):
                t0 = 512 * tc
                own = tc < 2
                src = x2of if own else x2pb
                col0 = (tc % 2) * 512
                tbase = sT3.n
                for k in range(KD):
                    sTp.wait(sy)
                    sX3.dma(sy.dma_start(out=(xc3[:, :] if own else xc3b[:, :]),
                                         in_=src[128 * k:128 * k + 128, col0:col0 + 512]))
                    if own and t_x2 is not None:
                        sX3.wait(sy)
                        sTp.dma(sy.dma_start(
                            out=t_x2[128 * k:128 * k + 128, col0:col0 + 512],
                            in_=xc3[:, :]))
                    sX3.wait(sc); sX3.wait(te)
                    xin = xc3[:, :] if own else xc3b[:, :]
                    if k >= 2:
                        sT3.wait(sc, tbase + k - 1)
                    sS3.inc(sc.activation(sq3[:, k % 2, :], xin, AF.Square))
                    if own:
                        te.matmul(psS13[:, :], onescol[:, :], xin,
                                  start=(k == 0), stop=(k == KD - 1))
                    else:
                        te.matmul(psS13[:, :], onescol_b[:, :], xin,
                                  start=(k == 0), stop=(k == KD - 1))
                    sS3.wait(te)
                    te.matmul(psS23[:, :], onescol[:, :], sq3[:, k % 2, :],
                              start=(k == 0), stop=(k == KD - 1))
                    sT3.tinc()
                    sT3.wait(sy)
                sT3.wait(ve)
                m, v, t1 = SR(0), SR(1), SR(2)
                ve.tensor_scalar(m, psS13[:, :], 1.0 / D, None, ALU.mult)
                ve.tensor_scalar(v, psS23[:, :], 1.0 / D, None, ALU.mult)
                ve.tensor_tensor(t1, m, m, ALU.mult)
                sV3.inc(ve.tensor_tensor(v, v, t1, ALU.subtract))
                sV3.wait(sc)
                sS3.inc(sc.activation(t1, v, AF.Sqrt, bias=smalls[0:1, 13:14]))
                sS3.wait(ve)
                ve.reciprocal(ROW(2, t0, 512), t1)
                ve.tensor_tensor(ROW(3, t0, 512), m, ROW(2, t0, 512), ALU.mult)
                sV3.inc(ve.nop())
                sV3.wait(te)
        barrier(ve, te, sc, sy)


        if DEBUG_STOP == "sl2":
            emit_zero_output_and_stop()
            flush()
            return nc, taps

        sl3stack = ExitStack()
        qb = sl3stack.enter_context(nc.sbuf_tensor("qb", [128, 16, O], BF16))
        sl3a = ExitStack()
        with sl3a:
            E3 = sl3a.enter_context
            xnb = E3(nc.sbuf_tensor("xnb", [128, KD, T], BF16))
            xrc3 = E3(nc.sbuf_tensor("xrc3a", [128, 512], F32))
            xrc3b = E3(nc.sbuf_tensor("xrc3b", [128, 512], BF16))
            was = E3(nc.sbuf_tensor("was", [128, 2, KD, 512], BF16))
            kvst = E3(nc.sbuf_tensor("kvst", [128, T], BF16))
            sTg = Ctr(); sVg = Ctr(); sSg = Ctr(); sXg = Ctr(); sWg2 = [Ctr(), Ctr()]; sKg = Ctr()
            bwait(te, ve, sc, sy, gp)
            with nc.psum_tensor("psBC3", [128, 1024], F32) as psBC3:
                for tc in range(4):
                    t0 = 512 * tc
                    own = tc < 2
                    src = x2of if own else x2pb
                    col0 = (tc % 2) * 512
                    sVg.wait(te)
                    te.matmul(psBC3[:, 0:512], onesrow[:, :], ROW(2, t0, 512),
                              start=True, stop=True)
                    te.matmul(psBC3[:, 512:1024], onesrow[:, :], ROW(3, t0, 512),
                              start=True, stop=True)
                    sTg.tinc()
                    sTg.wait(ve)
                    for k in range(KD):
                        sXg.dma(sy.dma_start(out=(xrc3[:, :] if own else xrc3b[:, :]),
                                             in_=src[128 * k:128 * k + 128, col0:col0 + 512]))
                        sXg.wait(ve)
                        xin = xrc3[:, :] if own else xrc3b[:, :]
                        ve.tensor_tensor(xrc3[:, :], xin, psBC3[:, 0:512], ALU.mult)
                        ve.tensor_tensor(xnb[:, k, t0:t0 + 512], xrc3[:, :],
                                         psBC3[:, 512:1024], ALU.subtract)
                        sVg.inc(ve.nop())
                        sVg.wait(sy)
                    sVg.wait(te)
            with nc.psum_tensor("psQ", [128, 2, 512], F32) as psQ:
                aw_marks = {}
                def prefetch_a(si, buf):
                    c0 = 512 * si
                    c = sWg2[buf]
                    for k in range(KD):
                        c.dma(gp.dma_start(out=was[:, buf, k, :],
                                           in_=bWattn[128 * k:128 * k + 128, c0:c0 + 512]))
                    aw_marks[si] = c.n
                a_marks = []
                prefetch_a(0, 0)
                for si in range(12):
                    if si + 1 < 12:
                        if si >= 1:
                            sTg.wait(gp, a_marks[si - 1])
                        prefetch_a(si + 1, (si + 1) % 2)
                    sWg2[si % 2].wait(te, aw_marks[si])
                    for oi in range(4):
                        oc = 4 * si + oi
                        is_q = oc < 8 or (16 <= oc < 24)
                        ntok = O if is_q else T
                        for q in range(ntok // 512):
                            qs = slice(512 * q, 512 * q + 512)
                            for k in range(KD):
                                te.matmul(psQ[:, q % 2, :],
                                          was[:, si % 2, k, 128 * oi:128 * oi + 128],
                                          xnb[:, k, qs], start=(k == 0), stop=(k == KD - 1))
                            sTg.tinc()
                            sTg.wait(ve)
                            if is_q:
                                qi = oc if oc < 8 else (oc - 8)
                                sVg.inc(ve.tensor_copy(qb[:, qi, qs], psQ[:, q % 2, :]))
                            else:
                                sVg.inc(ve.tensor_copy(kvst[:, qs], psQ[:, q % 2, :]))
                            sVg.wait(te)
                        if not is_q:
                            if 8 <= oc < 16:
                                r0 = 128 * (oc - 8)
                            elif 24 <= oc < 32:
                                r0 = 1024 + 128 * (oc - 24)
                            else:
                                r0 = 2048 + 128 * (oc - 32)
                            sVg.wait(sy)
                            sKg.dma(sy.dma_start(out=kvs[r0:r0 + 128, :], in_=kvst[:, :]))
                            sKg.wait(ve)
                    a_marks.append(sTg.n)
            sKg.wait(sy)
        barrier(ve, te, sc, sy, gp)

        obT = sl3stack.enter_context(nc.sbuf_tensor("obT", [128, KD, O], BF16))
        sl3b = ExitStack()
        with sl3b:
            E4 = sl3b.enter_context
            kh1 = E4(nc.sbuf_tensor("kh1", [65, T], BF16))
            kh2 = E4(nc.sbuf_tensor("kh2", [65, T], BF16))
            qh1 = E4(nc.sbuf_tensor("qh1", [65, O], BF16))
            qh2 = E4(nc.sbuf_tensor("qh2", [65, O], BF16))
            vtok = E4(nc.sbuf_tensor("vtok", [128, KD, 128], BF16))
            wtb = E4(nc.sbuf_tensor("wtb", [128, 2, 1024], BF16))
            aqb = E4(nc.sbuf_tensor("aqb", [64, O], BF16))
            mk64 = E4(nc.sbuf_tensor("mk64", [64, 2], BF16))
            mk64f = E4(nc.sbuf_tensor("mk64f", [64, 2], F32))
            bdr = E4(nc.sbuf_tensor("bdr", [1, O], F32))
            zrow = E4(nc.sbuf_tensor("zrow", [1, O], F32))
            zb = E4(nc.sbuf_tensor("zb", [128, O], F32))
            sTh = Ctr(); sVh = Ctr(); sSh = Ctr(); sKh = Ctr()
            bwait(te, ve, sc, sy, gp)
            for h in range(NHB):
                q0 = 64 * (h % 2)
                sVh.wait(te)
                ve.tensor_copy(qh1[0:64, :], qb[q0:q0 + 64, h // 2, :])
                ve.tensor_scalar(qh2[0:64, :], qb[q0:q0 + 64, 8 + h // 2, :],
                                 smalls[0:64, 6:7], None, ALU.mult)
                ve.memset(qh2[64:65, :], 0.0)
                sVh.inc(ve.nop())
                sVh.wait(sy)
                sKh.dma(sy.dma_start(out=kh1[0:64, :], in_=kvs[64 * h:64 * h + 64, :]))
                sKh.dma(sy.dma_start(out=kh2[0:64, :],
                                     in_=kvs[1024 + 64 * h:1024 + 64 * h + 64, :]))
                for jc in range(KD):
                    sKh.dma(sy.dma_start(out=vtok[:, jc, :],
                                         in_=kvs[2048 + 128 * h:2048 + 128 * h + 128,
                                                 128 * jc:128 * jc + 128],
                                         transpose=True))
                sKh.wait(ve)
                ve.memset(kh1[64:65, :], 1.0)
                ve.memset(kh2[64:65, :], 0.0)
                sVh.inc(ve.nop())
                with nc.psum_tensor(f"psB3_{h}", [1, 512], F32) as psB3:
                    sVh.wait(ve)
                    ve.tensor_reduce(mk64f[:, 0:1], kh1[0:64, :], AX.X, ALU.max,
                                     apply_absolute_value=True)
                    ve.tensor_reduce(mk64f[:, 1:2], kh2[0:64, :], AX.X, ALU.max,
                                     apply_absolute_value=True)
                    sVh.inc(ve.tensor_copy(mk64[:, :], mk64f[:, :]))
                    sVh.wait(sc)
                    sSh.inc(sc.activation(aqb[:, :], qh1[0:64, :], AF.Abs))
                    sSh.wait(te); sVh.wait(te)
                    for q in range(2):
                        te.matmul(psB3[:, :], mk64[:, 0:1], aqb[:, 512 * q:512 * q + 512],
                                  start=True, stop=True)
                        sTh.tinc()
                        sTh.wait(ve)
                        sVh.inc(ve.tensor_copy(bdr[0:1, 512 * q:512 * q + 512], psB3[:, :]))
                        sVh.wait(te)
                    sVh.wait(sc)
                    sSh.inc(sc.activation(aqb[:, :], qh2[0:64, :], AF.Abs))
                    sSh.wait(te)
                    for q in range(2):
                        te.matmul(psB3[:, :], mk64[:, 1:2], aqb[:, 512 * q:512 * q + 512],
                                  start=True, stop=True)
                        sTh.tinc()
                        sTh.wait(ve)
                        ve.tensor_tensor(bdr[0:1, 512 * q:512 * q + 512],
                                         bdr[0:1, 512 * q:512 * q + 512], psB3[:, :], ALU.add)
                        sVh.inc(ve.tensor_scalar(qh1[64:65, 512 * q:512 * q + 512],
                                                 bdr[0:1, 512 * q:512 * q + 512],
                                                 -1.0, None, ALU.mult))
                        sVh.wait(te)
                with nc.psum_tensor(f"psSC3_{h}", [128, 2, 512], F32) as psSC3, \
                     nc.psum_tensor(f"psZ3_{h}", [1, 1024], F32) as psZ3, \
                     nc.psum_tensor(f"psAO3_{h}", [128, 1024], F32) as psAO3, \
                     nc.psum_tensor(f"psZB_{h}", [128, 1024], F32) as psZB:
                    for jt in range(KD):
                        jsl = slice(128 * jt, 128 * jt + 128)
                        for q in range(2):
                            qs = slice(512 * q, 512 * q + 512)
                            te.matmul(psSC3[:, q, :], kh1[:, jsl], qh1[:, qs],
                                      start=True, stop=False)
                            te.matmul(psSC3[:, q, :], kh2[:, jsl], qh2[:, qs],
                                      start=False, stop=True)
                        sTh.tinc()
                        sTh.wait(sc)
                        sSh.inc(sc.activation(wtb[:, jt % 2, 0:512], psSC3[:, 0, :],
                                              AF.Exp, scale=0.125))
                        sSh.inc(sc.activation(wtb[:, jt % 2, 512:1024], psSC3[:, 1, :],
                                              AF.Exp, scale=0.125))
                        sSh.wait(te)
                        for q in range(2):
                            qs = slice(512 * q, 512 * q + 512)
                            te.matmul(psZ3[:, qs], onescol_b[:, :], wtb[:, jt % 2, qs],
                                      start=(jt == 0), stop=(jt == KD - 1))
                        for q in range(2):
                            qs = slice(512 * q, 512 * q + 512)
                            te.matmul(psAO3[:, qs], vtok[:, jt, :], wtb[:, jt % 2, qs],
                                      start=(jt == 0), stop=(jt == KD - 1))
                        sTh.tinc()
                    sTh.wait(ve)
                    ve.tensor_copy(zrow[:, :], psZ3[:, :])
                    sVh.inc(ve.tensor_scalar(zrow[:, :], zrow[:, :], 1.25, None, ALU.mult))
                    sVh.wait(te)
                    for q in range(2):
                        qs = slice(512 * q, 512 * q + 512)
                        te.matmul(psZB[:, qs], onesrow[:, :], zrow[0:1, qs],
                                  start=True, stop=True)
                    sTh.tinc()
                    sTh.wait(ve)
                    ve.reciprocal(zb[:, :], psZB[:, :])
                    sVh.inc(ve.tensor_tensor(obT[:, h, :], psAO3[:, :], zb[:, :], ALU.mult))
                    sVh.wait(te)
        barrier(ve, te, sc, sy, gp)

        t_ = tap("ob", [D, O], BF16)
        if t_ is not None:
            s_ = Ctr(); bwait(sy)
            for k in range(KD):
                s_.dma(sy.dma_start(out=t_[128 * k:128 * k + 128, :], in_=obT[:, k, :]))
            s_.wait(sy)

        sl3c = ExitStack()
        with sl3c:
            E5 = sl3c.enter_context
            wps = E5(nc.sbuf_tensor("wps", [128, 2, KD, 512], BF16))
            xrc4 = E5(nc.sbuf_tensor("xrc4", [128, 512], F32))
            sq4 = E5(nc.sbuf_tensor("sq4", [128, 512], F32))
            x3c = E5(nc.sbuf_tensor("x3c", [128, 512], F32))
            psP = E5(nc.psum_tensor("psP", [128, 2, 512], F32))
            psS1g = E5(nc.psum_tensor("psS1g", [1, 1024], F32))
            psS2g = E5(nc.psum_tensor("psS2g", [1, 1024], F32))
            sTi = Ctr(); sVi = Ctr(); sSi = Ctr(); sXi = Ctr(); sWi2 = [Ctr(), Ctr()]; sSti = Ctr()
            t_x3 = tap("x3", [D, O], F32)
            bwait(te, ve, sc, sy, gp)
            p_marks = []
            pw_marks = {}

            def prefetch_p(si, buf):
                c0 = 512 * si
                c = sWi2[buf]
                for k in range(KD):
                    c.dma(gp.dma_start(out=wps[:, buf, k, :],
                                       in_=bWproj[128 * k:128 * k + 128, c0:c0 + 512]))
                pw_marks[si] = c.n
            prefetch_p(0, 0)
            for si in range(4):
                if si + 1 < 4:
                    if si >= 1:
                        sTi.wait(gp, p_marks[si - 1])
                    prefetch_p(si + 1, (si + 1) % 2)
                sWi2[si % 2].wait(te, pw_marks[si])
                for di in range(4):
                    dc = 4 * si + di
                    for q in range(2):
                        qs = slice(512 * q, 512 * q + 512)
                        for k in range(KD):
                            te.matmul(psP[:, q % 2, :],
                                      wps[:, si % 2, k, 128 * di:128 * di + 128],
                                      obT[:, k, qs], start=(k == 0), stop=(k == KD - 1))
                        tm = sTi.tinc()
                        sXi.dma(sy.dma_start(out=xrc4[:, :],
                                             in_=x2of[128 * dc:128 * dc + 128, qs]))
                        sXi.wait(ve); sTi.wait(ve, tm)
                        if not (si == 0 and di == 0 and q == 0):
                            sSti.wait(ve, sSti.n)
                            sSi.wait(ve)
                        ve.tensor_tensor(x3c[:, :], xrc4[:, :], psP[:, q % 2, :], ALU.add)
                        sVi.inc(ve.nop())
                        sVi.wait(sy); sVi.wait(sc)
                        sSti.dma(sy.dma_start(out=x3of[128 * dc:128 * dc + 128, qs],
                                              in_=x3c[:, :]))
                        if t_x3 is not None:
                            sSti.dma(sy.dma_start(out=t_x3[128 * dc:128 * dc + 128, qs],
                                                  in_=x3c[:, :]))
                        sSi.inc(sc.activation(sq4[:, :], x3c[:, :], AF.Square))
                        sVi.wait(te)
                        te.matmul(psS1g[:, qs], onescol[:, :], x3c[:, :],
                                  start=(dc == 0), stop=(dc == KD - 1))
                        sSi.wait(te)
                        te.matmul(psS2g[:, qs], onescol[:, :], sq4[:, :],
                                  start=(dc == 0), stop=(dc == KD - 1))
                        sTi.tinc()
                p_marks.append(sTi.n)
            sTi.wait(ve)
            for q in range(2):
                qs = slice(512 * q, 512 * q + 512)
                t0 = 512 * q
                m, v, t1 = SR(0), SR(1), SR(2)
                ve.tensor_scalar(m, psS1g[:, qs], 1.0 / D, None, ALU.mult)
                ve.tensor_scalar(v, psS2g[:, qs], 1.0 / D, None, ALU.mult)
                ve.tensor_tensor(t1, m, m, ALU.mult)
                sVi.inc(ve.tensor_tensor(v, v, t1, ALU.subtract))
                sVi.wait(sc)
                sSi.inc(sc.activation(t1, v, AF.Sqrt, bias=smalls[0:1, 13:14]))
                sSi.wait(ve)
                ve.reciprocal(ROW(0, t0, 512), t1)
                ve.tensor_tensor(ROW(1, t0, 512), m, ROW(0, t0, 512), ALU.mult)
                sVi.inc(ve.nop())
            sSti.wait(sy)
        barrier(ve, te, sc, sy, gp)


        if DEBUG_STOP == "sl3":
            emit_zero_output_and_stop()
            sl3stack.close()
            flush()
            return nc, taps

        sl3stack.close()
        emit_ffn("f2", 1024, x3of, None, 0, 1, False, None, None, True)
        flush()

    return nc, taps


_NC_CACHE = {}


def _host_maps(inputs):
    f32 = np.float32
    T_ = np.ascontiguousarray
    gup = T_(inputs["L0_ffn_w_gate_up"].astype(f32, copy=False).T)
    dnT = T_(inputs["L0_ffn_w_down"].astype(f32, copy=False).T)
    W1T = T_(inputs["L0_attn_a_W1"].astype(f32, copy=False).T)
    W2T = T_(inputs["L0_attn_a_W2"].astype(f32, copy=False).T)
    W3c = T_(inputs["L0_attn_a_W3"].astype(f32, copy=False).reshape(64, 1))
    WqT = T_(inputs["L0_attn_a_Wqkv"].astype(f32, copy=False).T)
    WoT = T_(inputs["L0_attn_a_Wout"].astype(f32, copy=False).T)
    gam = inputs["L0_attn_a_gamma"].astype(f32, copy=False)
    bet = inputs["L0_attn_a_beta"].astype(f32, copy=False)
    bout = inputs["L0_attn_a_bout"].astype(f32, copy=False)
    b1 = T_(inputs["L0_attn_a_b1"].astype(f32, copy=False).reshape(128, 1))
    b2 = inputs["L0_attn_a_b2"].astype(f32, copy=False)
    b2d = T_(np.concatenate([b2, b2]).reshape(128, 1))
    b3 = inputs["L0_attn_a_b3"].astype(f32, copy=False).reshape(1, 1)
    bqkv = inputs["L0_attn_a_bqkv"].astype(f32, copy=False)
    bqkv_p = np.zeros((128, 2), f32)
    bqkv_p[:, 0] = bqkv[0:128]
    bqkv_p[0:64, 1] = bqkv[128:192]

    def pk(vec):
        return T_(vec.reshape(KD, 128).T)
    x = inputs["L0_attn_a_x"].astype(f32, copy=False)
    com = {
        "aW1T": W1T, "ab1": b1, "aW2T": W2T, "ab2d": b2d, "aW3c": W3c, "ab3": b3,
        "aWqkvT": WqT, "abqkv": bqkv_p, "aWoutT": WoT,
        "about_pk": pk(bout), "agam_pk": pk(gam), "abet_pk": pk(bet),
        "bWattn": T_(inputs["L0_attn_b_W_attn"].astype(f32, copy=False)),
        "bWproj": T_(inputs["L0_attn_b_W_proj"].astype(f32, copy=False)),
        "blq1": inputs["L0_attn_b_lambda_q1"].astype(f32, copy=False).reshape(1, D),
        "blk1": inputs["L0_attn_b_lambda_k1"].astype(f32, copy=False).reshape(1, D),
        "blq2": inputs["L0_attn_b_lambda_q2"].astype(f32, copy=False).reshape(1, D),
        "blk2": inputs["L0_attn_b_lambda_k2"].astype(f32, copy=False).reshape(1, D),
        "fgupT": gup, "fdnT": dnT,
    }
    in_maps = []
    for c in range(8):
        b, half = c // 2, c % 2
        xb = x[b]
        m = dict(com)
        m["xT_own"] = T_(xb[1024 * half:1024 * half + 1024].T)
        m["xT_oth"] = T_(xb[1024 * (1 - half):1024 * (1 - half) + 1024].T)
        in_maps.append(m)
    return in_maps


def kernel(**inputs):
    if "std" not in _NC_CACHE:
        _NC_CACHE["std"] = build_program()
    nc, taps = _NC_CACHE["std"]
    in_maps = _host_maps(inputs)
    res = run_bass_kernel_spmd(nc, in_maps, core_ids=list(range(8)))
    out = np.empty((4, 2048, 2048), np.float32)
    for c in range(8):
        b, half = c // 2, c % 2
        out[b, 1024 * half:1024 * half + 1024, :] = res.results[c]["y"].T
    return out


# version 04
# speedup vs baseline: 1.1712x; 1.0789x over previous
import numpy as np
import ml_dtypes
import concourse.bass as bass
import concourse.mybir as mybir
from concourse.bass_utils import run_bass_kernel_spmd
from contextlib import ExitStack

F32 = mybir.dt.float32
BF16 = mybir.dt.bfloat16
AF = mybir.ActivationFunctionType
ALU = mybir.AluOpType
AX = mybir.AxisListType

D = 2048
T = 2048
O = 1024
KD = 16
EPS = 1e-5
KSP = 204
NHA = 32
NHB = 16
INTER = 8192
LAM_INIT = 0.2
NBIS = 20
RSQ2 = float(1.0 / np.sqrt(2.0))

DEBUG_STOP = None
DEBUG_TAPS = ()
CHAIN_WAITS = True


def build_program():
    nc = bass.Bass("TRN2", target_bir_lowering=False)

    def I(name, shape, dtype=F32):
        return nc.dram_tensor(name, list(shape), dtype, kind="ExternalInput")

    xT_own = I("xT_own", [D, O])
    xT_oth = I("xT_oth", [D, O])
    aW1T = I("aW1T", [D, 128])
    ab1 = I("ab1", [128, 1])
    aW2T = I("aW2T", [128, 64])
    ab2d = I("ab2d", [128, 1])
    aW3c = I("aW3c", [64, 1])
    ab3 = I("ab3", [1, 1])
    aWqkvT = I("aWqkvT", [D, 192])
    abqkv = I("abqkv", [128, 2])
    aWoutT = I("aWoutT", [64, D])
    about_pk = I("about_pk", [128, KD])
    agam_pk = I("agam_pk", [128, KD])
    abet_pk = I("abet_pk", [128, KD])
    bWattn = I("bWattn", [D, 3 * D])
    bWproj = I("bWproj", [D, D])
    blq1 = I("blq1", [1, D]); blk1 = I("blk1", [1, D])
    blq2 = I("blq2", [1, D]); blk2 = I("blk2", [1, D])
    fgupT = I("fgupT", [D, 2 * INTER])
    fdnT = I("fdnT", [INTER, D])

    y = nc.dram_tensor("y", [D, O], F32, kind="ExternalOutput")

    taps = {}

    def tap(name, shape, dtype=F32):
        if name in DEBUG_TAPS:
            taps[name] = nc.dram_tensor("tap_" + name, list(shape), dtype,
                                        kind="ExternalOutput")
        return taps.get(name)

    gup16 = nc.dram_tensor("gup16", [D, 2 * INTER], BF16, kind="Internal")
    dn16 = nc.dram_tensor("dn16", [INTER, D], BF16, kind="Internal")
    x1of = nc.dram_tensor("x1of", [D, O], F32, kind="Internal")
    x1pb = nc.dram_tensor("x1pb", [D, O], BF16, kind="Internal")
    x2of = nc.dram_tensor("x2of", [D, O], F32, kind="Internal")
    x2pb = nc.dram_tensor("x2pb", [D, O], BF16, kind="Internal")
    x3of = nc.dram_tensor("x3of", [D, O], F32, kind="Internal")
    kvs = nc.dram_tensor("kvs", [4096, T], BF16, kind="Internal")

    ctx = ExitStack()
    E = ctx.enter_context
    with ctx:
        rows = E(nc.sbuf_tensor("rows", [1, 4 * T], F32))

        def ROW(i, t0=0, w=T):
            return rows[0:1, i * T + t0: i * T + t0 + w]
        srow = E(nc.sbuf_tensor("srow", [1, 4 * 512], F32))

        def SR(i, w=512):
            return srow[0:1, 512 * i: 512 * i + w]
        onescol = E(nc.sbuf_tensor("onescol", [128, 1], F32))
        onescol_b = E(nc.sbuf_tensor("onescol_b", [128, 1], BF16))
        onesrow = E(nc.sbuf_tensor("onesrow", [1, 128], F32))
        onesrow_b = E(nc.sbuf_tensor("onesrow_b", [1, 128], BF16))
        onesTrow = E(nc.sbuf_tensor("onesTrow", [1, T], BF16))
        idn = E(nc.sbuf_tensor("idn", [128, 128], BF16))
        noti = E(nc.sbuf_tensor("noti", [128, 128], BF16))
        dupm = E(nc.sbuf_tensor("dupm", [32, 64], F32))
        sel64 = E(nc.sbuf_tensor("sel64", [3, 32, 64], BF16))
        basis = E(nc.sbuf_tensor("basis", [1, 32, 32], BF16))
        smalls = E(nc.sbuf_tensor("smalls", [128, 16], F32))
        bdca = E(nc.sbuf_tensor("bdca", [128, 32], F32))
        w2t = E(nc.sbuf_tensor("w2t", [128, 2, 64], BF16))
        w3s = E(nc.sbuf_tensor("w3s", [128, 64, 128], BF16))
        wot = E(nc.sbuf_tensor("wot", [64, D], BF16))
        gpk = E(nc.sbuf_tensor("gpk", [128, KD], F32))
        bpkf = E(nc.sbuf_tensor("bpkf", [128, KD], F32))
        bpk16 = E(nc.sbuf_tensor("bpk16", [128, KD], BF16))
        outpk = E(nc.sbuf_tensor("outpk", [128, KD], F32))

        block = E(nc.Block())

        CHAINED = ("ve", "sc", "gp")
        NOCHAIN = {"dma_start", "wait_ge", "dma_start_transpose"}
        chain_counts = {"ve": 0, "sc": 0, "gp": 0}

        class InstHandle:
            __slots__ = ("incs", "engine", "chain_val", "mark")

            def __init__(self, engine, chain_val):
                self.incs = []
                self.engine = engine
                self.chain_val = chain_val
                self.mark = None

            def then_inc(self, sem, k=1):
                self.incs.append((sem, k))
                return self

        class EngQ:
            def __init__(self, name):
                self.name = name
                self.q = []

            def __getattr__(self, attr):
                def call(*a, **kw):
                    cv = None
                    if self.name in CHAINED and attr not in NOCHAIN:
                        chain_counts[self.name] += 1
                        cv = chain_counts[self.name]
                    h = InstHandle(self.name, cv)
                    self.q.append((attr, a, kw, h))
                    return h
                return call

        sy, gp, ve, sc, te = EngQ("sy"), EngQ("gp"), EngQ("ve"), EngQ("sc"), EngQ("te")
        ALLE = (sy, gp, ve, sc, te)
        chain_sems = {n: E(nc.semaphore("ch_" + n)) for n in CHAINED}

        def flush():
            regs = ((sy, block.sync), (gp, block.gpsimd), (ve, block.vector),
                    (sc, block.scalar), (te, block.tensor))
            for proxy, reg in regs:
                q = proxy.q
                csem = chain_sems.get(proxy.name)

                def runner(e, q=q, csem=csem, pname=proxy.name):
                    for attr, a, kw, h in q:
                        if (CHAIN_WAITS or pname == "gp") and \
                                h.chain_val is not None and h.chain_val > 1:
                            e.wait_ge(csem, h.chain_val - 1)
                        inst = getattr(e, attr)(*a, **kw)
                        if h.chain_val is not None:
                            inst.then_inc(csem, 1)
                        for sem, k in h.incs:
                            inst.then_inc(sem, k)
                reg(runner)
                proxy.q = []

        sem_pool = [E(nc.semaphore(f"sp{i}")) for i in range(88)]
        pool_state = {}
        pool_i = [0]

        class Ctr:
            def __init__(self):
                assert pool_i[0] < len(sem_pool), "semaphore pool exhausted"
                self.sem = sem_pool[pool_i[0]]
                pool_i[0] += 1
                self.base = pool_state.get(id(self.sem), 0)
                self.n = self.base
                self.mk = {}

            def inc(self, h, k=1):
                self.n += k
                if getattr(h, "chain_val", None) is not None:
                    self.mk[self.n] = (chain_sems[h.engine], h.chain_val)
                else:
                    h.then_inc(self.sem, k)
                    h.mark = (self.sem, self.n)
                    pool_state[id(self.sem)] = max(pool_state.get(id(self.sem), 0), self.n)
                return self.n

            def tinc(self, k=1):
                assert te.q, "tinc with empty tensor queue"
                h = te.q[-1][3]
                if h.mark is not None:
                    return self.n
                return self.inc(h, k)

            def dma(self, h):
                return self.inc(h, 16)

            def wait(self, eng, n=None):
                n = self.n if n is None else n
                if n <= self.base:
                    return
                if n in self.mk:
                    sem, val = self.mk[n]
                    eng.wait_ge(sem, val)
                else:
                    eng.wait_ge(self.sem, n)

        sB = Ctr()
        bar_gen = [0]
        bar_set = {}
        bar_seen = {id(e): -1 for e in ALLE}

        sTbar = Ctr()

        def barrier(*workers):
            for e in workers:
                if e is te:
                    if not te.q:
                        continue
                    h = te.q[-1][3]
                    if h.mark is not None:
                        bar_set[("te",)] = h.mark
                    else:
                        v = sTbar.inc(h)
                        bar_set[("te",)] = (sTbar.sem, v)
                    continue
                h = e.nop()
                if h.chain_val is not None:
                    sem = chain_sems[h.engine]
                    bar_set[id(sem)] = (sem, h.chain_val)
                else:
                    h.then_inc(sB.sem, 1)
                    sB.n += 1
                    bar_set[id(sB.sem)] = (sB.sem, sB.n)
            bar_gen[0] += 1

        def bwait(*engs):
            for e in engs:
                if bar_seen[id(e)] < bar_gen[0]:
                    for sem, val in bar_set.values():
                        e.wait_ge(sem, val)
                    bar_seen[id(e)] = bar_gen[0]

        def emit_zero_output_and_stop():
            sE = Ctr()
            bwait(sy, ve)
            with nc.sbuf_tensor("zz", [128, O], F32) as zz:
                sVz = Ctr()
                sVz.inc(ve.memset(zz[:, :], 0.0))
                sVz.wait(sy)
                for k in range(KD):
                    sE.dma(sy.dma_start(out=y[128 * k:128 * k + 128, :], in_=zz[:, :]))
                sE.wait(sy)

        sIn = Ctr(); sCa = Ctr(); sV0 = Ctr(); sT0 = Ctr(); sG0 = Ctr(); sS0 = Ctr()
        ve.memset(smalls[:, :], 0.0)
        ve.memset(rows[:, :], 0.0)
        ve.memset(srow[:, :], 0.0)
        sV0.inc(ve.memset(bdca[:, :], 0.0))
        sV0.wait(sy)
        sIn.dma(sy.dma_start(out=gpk[:, :], in_=agam_pk[:, :]))
        sIn.dma(sy.dma_start(out=bpkf[:, :], in_=abet_pk[:, :]))
        sIn.dma(sy.dma_start(out=outpk[:, :], in_=about_pk[:, :]))
        sIn.dma(sy.dma_start(out=smalls[:, 0:1], in_=ab1[:, :]))
        sIn.dma(sy.dma_start(out=smalls[:, 1:2], in_=ab2d[:, :]))
        sIn.dma(sy.dma_start(out=smalls[:, 2:4], in_=abqkv[:, :]))
        sIn.dma(sy.dma_start(out=smalls[0:1, 4:5], in_=ab3[:, :]))
        sIn.dma(sy.dma_start(out=smalls[0:64, 5:6], in_=aW3c[:, :]))
        sIn.dma(sy.dma_start(out=ROW(0, 0, D), in_=blq1[:, :]))
        sIn.dma(sy.dma_start(out=ROW(1, 0, D), in_=blk1[:, :]))
        sIn.dma(sy.dma_start(out=ROW(2, 0, D), in_=blq2[:, :]))
        sIn.dma(sy.dma_start(out=ROW(3, 0, D), in_=blk2[:, :]))

        NCB = 3
        cstack = ExitStack()
        cbuf = cstack.enter_context(nc.sbuf_tensor("cbuf", [64, NCB, 4096], BF16, side="right"))
        wstack = ExitStack()
        w1t = wstack.enter_context(nc.sbuf_tensor("w1t", [128, KD, 128], BF16, side="right"))
        wqt = wstack.enter_context(nc.sbuf_tensor("wqt", [128, KD, 192], BF16, side="right"))
        for k in range(KD):
            sCa.dma(gp.dma_start(out=w1t[:, k, :], in_=aW1T[128 * k:128 * k + 128, :]))
            sCa.dma(gp.dma_start(out=wqt[:, k, :], in_=aWqkvT[128 * k:128 * k + 128, :]))
        sCa.dma(gp.dma_start(out=w2t[:, 0, :], in_=aW2T[:, :]))
        sCa.dma(gp.dma_start(out=wot[:, :], in_=aWoutT[:, :]))

        ve.memset(smalls[:, 13:14], float(EPS))
        ve.memset(onescol[:, :], 1.0)
        ve.memset(onescol_b[:, :], 1.0)
        ve.memset(onesrow[:, :], 1.0)
        ve.memset(onesrow_b[:, :], 1.0)
        ve.memset(onesTrow[:, :], 1.0)
        sV0.inc(ve.memset(w3s[:, :, :], 0.0))
        MEMSET_DONE = sV0.n
        gp.memset(idn[:, :], 1.0)
        sG0.inc(gp.affine_select(idn[:, :], idn[:, :], [[1, 128]], ALU.is_equal, 0.0,
                                 base=0, channel_multiplier=-1))
        gp.memset(noti[:, :], 1.0)
        sG0.inc(gp.affine_select(noti[:, :], noti[:, :], [[1, 128]], ALU.not_equal, 0.0,
                                 base=0, channel_multiplier=-1))
        gp.memset(dupm[:, :], 1.0)
        gp.affine_select(dupm[:, :], dupm[:, :], [[1, 64]], ALU.is_ge, 0.0,
                         base=0, channel_multiplier=-2)
        sG0.inc(gp.affine_select(dupm[:, :], dupm[:, :], [[-1, 64]], ALU.is_ge, 0.0,
                                 base=1, channel_multiplier=2))
        gp.memset(sel64[:, :, :], 1.0)
        gp.affine_select(sel64[:, :, :], sel64[:, :, :], [[-2, 32], [1, 64]],
                         ALU.is_equal, 0.0, base=1, channel_multiplier=-1)
        sG0.inc(gp.affine_select(sel64[:, :, :], sel64[:, :, :], [[0, 32], [0, 64]],
                                 ALU.is_ge, 0.0, base=-1, channel_multiplier=1))
        gp.memset(basis[:, :, :], 1.0)
        sG0.inc(gp.affine_select(basis[:, :, :], basis[:, :, :], [[-1, 32], [1, 32]],
                                 ALU.is_equal, 0.0, base=0, channel_multiplier=0))

        if DEBUG_STOP == "ld":
            with nc.sbuf_tensor("zzl", [128, O], F32) as zz:
                sVz = Ctr()
                sVz.inc(ve.memset(zz[:, :], 0.0))
                sVz.wait(sy)
                sIn.wait(sy); sCa.wait(sy)
                sV0.wait(sy); sG0.wait(sy)
                sE = Ctr()
                tg = tap("gpk", [128, 16], F32)
                if tg is not None:
                    sE.dma(sy.dma_start(out=tg[:, :], in_=gpk[:, :]))
                tp = tap("p0", [128, 16], F32)
                if tp is not None:
                    sE.dma(sy.dma_start(out=tp[:, :], in_=smalls[:, :]))
                for k in range(KD):
                    sE.dma(sy.dma_start(out=y[128 * k:128 * k + 128, :], in_=zz[:, :]))
                sE.wait(sy)
            wstack.close()
            cstack.close()
            flush()
            return nc, taps

        sIn.wait(ve)
        lr = ROW(0, 0, D)
        ve.tensor_tensor(lr, lr, ROW(1, 0, D), ALU.mult)
        ve.tensor_reduce(smalls[0:1, 7:8], lr, AX.X, ALU.add)
        ve.tensor_tensor(ROW(2, 0, D), ROW(2, 0, D), ROW(3, 0, D), ALU.mult)
        sV0.inc(ve.tensor_reduce(smalls[0:1, 8:9], ROW(2, 0, D), AX.X, ALU.add))
        sV0.wait(sc)
        sc.activation(smalls[0:1, 9:10], smalls[0:1, 7:8], AF.Exp)
        sS0.inc(sc.activation(smalls[0:1, 10:11], smalls[0:1, 8:9], AF.Exp))
        sS0.wait(ve)
        ve.tensor_tensor(smalls[0:1, 11:12], smalls[0:1, 9:10], smalls[0:1, 10:11],
                         ALU.subtract)
        ve.tensor_scalar(smalls[0:1, 11:12], smalls[0:1, 11:12], float(LAM_INIT),
                         None, ALU.add)
        sV0.inc(ve.tensor_scalar(smalls[0:1, 12:13], smalls[0:1, 11:12], -1.0,
                                 None, ALU.mult))

        with nc.psum_tensor("p0a", [128, 4], F32) as p0a:
            sCa.wait(te); sIn.wait(te)
            sV0.wait(te, MEMSET_DONE)
            sG0.wait(te); sG0.wait(ve); sG0.wait(sc)
            sV0.inc(ve.tensor_copy(bpk16[:, :], bpkf[:, :]))
            sV0.wait(te)
            for k in range(KD):
                te.matmul(p0a[0:128, 0:1], wqt[:, k, 0:128], bpk16[:, k:k + 1],
                          start=(k == 0), stop=(k == KD - 1))
            for k in range(KD):
                te.matmul(p0a[0:64, 1:2], wqt[:, k, 128:192], bpk16[:, k:k + 1],
                          start=(k == 0), stop=(k == KD - 1))
            sV0.wait(te)
            te.matmul(p0a[:, 2:3], onesrow[0:1, :], smalls[0:1, 4:5],
                      start=True, stop=True)
            te.matmul(p0a[0:64, 3:4], onesrow[0:1, 0:64], smalls[0:1, 12:13],
                      start=True, stop=True)
            sT0.tinc()
            sT0.wait(ve)
            ve.tensor_tensor(smalls[:, 2:3], smalls[:, 2:3], p0a[0:128, 0:1], ALU.add)
            ve.tensor_tensor(smalls[0:64, 3:4], smalls[0:64, 3:4], p0a[0:64, 1:2], ALU.add)
            ve.tensor_copy(smalls[:, 4:5], p0a[:, 2:3])
            ve.tensor_copy(smalls[0:64, 6:7], p0a[0:64, 3:4])
            for k in range(KD):
                ve.tensor_scalar(w1t[:, k, :], w1t[:, k, :], gpk[:, k:k + 1], None, ALU.mult)
                ve.tensor_scalar(wqt[:, k, :], wqt[:, k, :], gpk[:, k:k + 1], None, ALU.mult)
            ve.tensor_scalar(w2t[:, 1, :], w2t[:, 0, :], -1.0, None, ALU.mult)
            sV0.wait(ve)
            ve.tensor_copy(w3s[0:64, 0, 0:1], smalls[0:64, 5:6])
            for t in range(64):
                if t > 0:
                    ve.tensor_copy(w3s[0:64, t, 2 * t:2 * t + 1], w3s[0:64, 0, 0:1])
                ve.tensor_copy(w3s[64:128, t, 2 * t + 1:2 * t + 2], w3s[0:64, 0, 0:1])
            sV0.inc(ve.nop())

        if DEBUG_STOP == "p0b":
            with nc.sbuf_tensor("zzl2", [128, O], F32) as zz:
                sVz = Ctr()
                sVz.inc(ve.memset(zz[:, :], 0.0))
                sVz.wait(sy)
                sV0.wait(sy); sT0.wait(sy); sS0.wait(sy)
                sE = Ctr()
                tp = tap("p0", [128, 16], F32)
                if tp is not None:
                    sE.dma(sy.dma_start(out=tp[:, :], in_=smalls[:, :]))
                for k in range(KD):
                    sE.dma(sy.dma_start(out=y[128 * k:128 * k + 128, :], in_=zz[:, :]))
                sE.wait(sy)
            wstack.close()
            cstack.close()
            flush()
            return nc, taps

        sFW = Ctr()
        sFL = Ctr()
        segs = []
        for r in range(0, D, 64):
            for cc in range(4):
                segs.append((fgupT, gup16, r, 4096 * cc, 4096))
        for r in range(0, INTER, 64):
            segs.append((fdnT, dn16, r, 0, 2048))
        marks = []
        for si, (srcd, dstd, r, c0, w) in enumerate(segs):
            bi = si % NCB
            if si >= NCB and si % NCB == 0:
                sFW.wait(gp, marks[si - 1])
            sFL.dma(gp.dma_start(out=cbuf[:, bi, 0:w], in_=srcd[r:r + 64, c0:c0 + w]))
            sFL.wait(gp)
            sFW.dma(gp.dma_start(out=dstd[r:r + 64, c0:c0 + w], in_=cbuf[:, bi, 0:w]))
            marks.append(sFW.n)
        FW_DONE = sFW.n

        barrier(ve, te, sc, sy)

        t_ = tap("p0", [128, 16], F32)
        if t_ is not None:
            s_ = Ctr(); bwait(sy)
            s_.dma(sy.dma_start(out=t_[:, :], in_=smalls[:, :]))
            s_.wait(sy)
        t_ = tap("gpk", [128, 16], F32)
        if t_ is not None:
            s_ = Ctr(); bwait(sy)
            s_.dma(sy.dma_start(out=t_[:, :], in_=gpk[:, :]))
            s_.wait(sy)
        t_ = tap("rows01", [1, 4 * T], F32)
        if t_ is not None:
            s_ = Ctr(); bwait(sy)
            s_.dma(sy.dma_start(out=t_[:, :], in_=rows[:, :]))
            s_.wait(sy)
        if DEBUG_STOP == "p0":
            with nc.sbuf_tensor("zzp", [128, O], F32) as zz:
                sVz = Ctr()
                sVz.inc(ve.memset(zz[:, :], 0.0))
                sVz.wait(sy)
                sE = Ctr()
                for k in range(KD):
                    sE.dma(sy.dma_start(out=y[128 * k:128 * k + 128, :], in_=zz[:, :]))
                sE.wait(sy)
            wstack.close()
            cstack.close()
            flush()
            return nc, taps

        xna = wstack.enter_context(nc.sbuf_tensor("xna", [128, KD, T], BF16, side="right"))
        sl1a = ExitStack()
        with sl1a:
            x0c = sl1a.enter_context(nc.sbuf_tensor("x0c", [128, KD, 512], F32))
            sqc = sl1a.enter_context(nc.sbuf_tensor("sqc", [128, 2, 512], F32))
            psS1 = sl1a.enter_context(nc.psum_tensor("psS1", [1, 512], F32))
            psS2 = sl1a.enter_context(nc.psum_tensor("psS2", [1, 512], F32))
            psBC = sl1a.enter_context(nc.psum_tensor("psBC", [128, 1024], F32))
            sX = Ctr(); sVa = Ctr(); sTa = Ctr(); sSa = Ctr()
            bwait(sy, te, sc, ve)
            for c in range(4):
                src = xT_own if c < 2 else xT_oth
                col0 = (c % 2) * 512
                t0 = 512 * c
                for k in range(KD):
                    sX.dma(sy.dma_start(out=x0c[:, k, :],
                                        in_=src[128 * k:128 * k + 128, col0:col0 + 512]))
                sX.wait(sc); sX.wait(te)
                temark = sTa.n
                for k in range(KD):
                    if k >= 2:
                        sTa.wait(sc, temark + k - 1)
                    sSa.inc(sc.activation(sqc[:, k % 2, :], x0c[:, k, :], AF.Square))
                    te.matmul(psS1[:, :], onescol[:, :], x0c[:, k, :],
                              start=(k == 0), stop=(k == KD - 1))
                    sSa.wait(te)
                    te.matmul(psS2[:, :], onescol[:, :], sqc[:, k % 2, :],
                              start=(k == 0), stop=(k == KD - 1))
                    sTa.tinc()
                sTa.wait(ve)
                m, v, t1, v2 = SR(0), SR(1), SR(2), SR(3)
                ve.tensor_scalar(m, psS1[:, :], 1.0 / D, None, ALU.mult)
                ve.tensor_scalar(v, psS2[:, :], 1.0 / D, None, ALU.mult)
                ve.tensor_tensor(t1, m, m, ALU.mult)
                sVa.inc(ve.tensor_tensor(v, v, t1, ALU.subtract))
                sVa.wait(sc)
                sSa.inc(sc.activation(t1, v, AF.Sqrt, bias=smalls[0:1, 13:14]))
                sSa.wait(ve)
                ve.reciprocal(ROW(2, t0, 512), t1)
                ve.tensor_tensor(ROW(3, t0, 512), m, ROW(2, t0, 512), ALU.mult)
                ve.tensor_tensor(v2, v, ROW(2, t0, 512), ALU.mult)
                sVa.inc(ve.tensor_tensor(v2, v2, ROW(2, t0, 512), ALU.mult))
                sVa.wait(sc)
                sSa.inc(sc.activation(v2, v2, AF.Sqrt, bias=smalls[0:1, 13:14]))
                sSa.wait(ve)
                ve.reciprocal(v2, v2)
                ve.tensor_tensor(ROW(0, t0, 512), ROW(2, t0, 512), v2, ALU.mult)
                ve.tensor_tensor(ROW(1, t0, 512), ROW(3, t0, 512), v2, ALU.mult)
                sVa.inc(ve.nop())
                sVa.wait(te)
                te.matmul(psBC[:, 0:512], onesrow[:, :], ROW(0, t0, 512),
                          start=True, stop=True)
                te.matmul(psBC[:, 512:1024], onesrow[:, :], ROW(1, t0, 512),
                          start=True, stop=True)
                sTa.tinc()
                sTa.wait(ve)
                for k in range(KD):
                    ve.tensor_tensor(sqc[:, k % 2, :], x0c[:, k, :], psBC[:, 0:512],
                                     ALU.mult)
                    ve.tensor_tensor(xna[:, k, t0:t0 + 512], sqc[:, k % 2, :],
                                     psBC[:, 512:1024], ALU.subtract)
                sVa.inc(ve.nop())
                sVa.wait(sy); sVa.wait(te); sVa.wait(sc)
        barrier(ve, te, sc, sy)

        t_ = tap("xna", [D, T], BF16)
        if t_ is not None:
            s_ = Ctr(); bwait(sy)
            for k in range(KD):
                s_.dma(sy.dma_start(out=t_[128 * k:128 * k + 128, :], in_=xna[:, k, :]))
            s_.wait(sy)
        t_ = tap("rows_a", [1, 4 * T], F32)
        if t_ is not None:
            s_ = Ctr(); bwait(sy)
            s_.dma(sy.dma_start(out=t_[:, :], in_=rows[:, :]))
            s_.wait(sy)
        if DEBUG_STOP == "sl1a":
            emit_zero_output_and_stop()
            wstack.close()
            cstack.close()
            flush()
            return nc, taps

        sl1stack = ExitStack()
        hT = sl1stack.enter_context(nc.sbuf_tensor("hT", [128, T], BF16))
        Hb = sl1stack.enter_context(nc.sbuf_tensor("Hb", [128, T], F32))
        kqv = sl1stack.enter_context(nc.sbuf_tensor("kqv", [128, T], BF16))
        vhat = sl1stack.enter_context(nc.sbuf_tensor("vhat", [128, KD, 96], BF16))
        out_aT = sl1stack.enter_context(nc.sbuf_tensor("out_aT", [64, T], BF16))
        sl1b = ExitStack()
        with sl1b:
            psH = sl1b.enter_context(nc.psum_tensor("psH", [128, 1024], F32))
            psT2 = sl1b.enter_context(nc.psum_tensor("psT2", [128, 128], BF16))
            psMQ = sl1b.enter_context(nc.psum_tensor("psMQ", [128, 128], F32))
            vrows = sl1b.enter_context(nc.sbuf_tensor("vrows", [64, T], BF16))
            sVb = Ctr(); sTb = Ctr(); sCb = Ctr(); sSb = Ctr()
            bwait(te, ve, sc, gp)
            sV0.wait(te)
            for half in range(2):
                cols = slice(1024 * half, 1024 * half + 1024)
                for nn2 in range(2):
                    n0 = 1024 * half + 512 * nn2
                    for k in range(KD):
                        te.matmul(psH[:, 512 * nn2:512 * nn2 + 512], w1t[:, k, :],
                                  xna[:, k, n0:n0 + 512], start=(k == 0), stop=(k == KD - 1))
                sTb.tinc()
                sTb.wait(ve)
                sVb.inc(ve.tensor_copy(hT[:, cols], psH[:, :]))
                sVb.wait(te)
            for half in range(2):
                cols = slice(1024 * half, 1024 * half + 1024)
                for nn2 in range(2):
                    n0 = 1024 * half + 512 * nn2
                    for k in range(KD):
                        te.matmul(psH[:, 512 * nn2:512 * nn2 + 512], wqt[:, k, 0:128],
                                  xna[:, k, n0:n0 + 512], start=(k == 0), stop=(k == KD - 1))
                sTb.tinc()
                sTb.wait(ve)
                sVb.inc(ve.tensor_scalar(kqv[:, cols], psH[:, :], smalls[:, 2:3],
                                         None, ALU.add))
                sVb.wait(te)
                for nn2 in range(2):
                    n0 = 1024 * half + 512 * nn2
                    for k in range(KD):
                        te.matmul(psH[0:64, 512 * nn2:512 * nn2 + 512], wqt[:, k, 128:192],
                                  xna[:, k, n0:n0 + 512], start=(k == 0), stop=(k == KD - 1))
                sTb.tinc()
                sTb.wait(ve)
                sVb.inc(ve.tensor_scalar(vrows[:, cols], psH[0:64, :],
                                         smalls[0:64, 3:4], None, ALU.add))
                sVb.wait(te)
            sVb.inc(ve.tensor_scalar(Hb[:, :], hT[:, :], smalls[:, 0:1], None, ALU.add))
            for jc in range(KD):
                sTb.inc(te.transpose(psT2[:, 0:64], vrows[:, 128 * jc:128 * jc + 128],
                                     idn[0:64, 0:64]))
                sTb.wait(ve)
                dst = vhat[:, jc, :].rearrange("p (h t) -> p h t", t=3)
                ve.tensor_copy(dst[:, :, 1:3],
                               psT2[:, 0:64].rearrange("p (h t) -> p h t", t=2))
                sVb.inc(ve.memset(dst[:, :, 0:1], 1.0))
                sVb.wait(te)
            t_ = tap("kqv", [256, T], BF16)
            if t_ is not None:
                s_ = Ctr()
                sVb.wait(sy)
                s_.dma(sy.dma_start(out=t_[0:128, :], in_=kqv[:, :]))
                s_.dma(sy.dma_start(out=t_[128:192, :], in_=vrows[:, :]))
                s_.wait(sy)
            with nc.sbuf_tensor("mqk", [128, 4], F32) as mqk, \
                 nc.sbuf_tensor("mrow", [1, 128], F32) as mrow, \
                 nc.sbuf_tensor("mtile", [128, 128], F32) as mtile:
                sVb.inc(ve.tensor_reduce(mqk[:, 0:1], kqv[:, :], AX.X, ALU.max,
                                         apply_absolute_value=True))
                sVb.wait(sy)
                sKmv = Ctr()
                sKmv.dma(sy.dma_start(out=mqk[0:64, 2:3], in_=mqk[64:128, 0:1]))
                sKmv.wait(ve)
                ve.tensor_tensor(mqk[0:64, 1:2], mqk[0:64, 0:1], mqk[0:64, 2:3],
                                 ALU.mult)
                ve.memset(mtile[:, :], 0.0)
                sVb.inc(ve.tensor_copy(mtile[0:64, 0:1], mqk[0:64, 1:2]))
                with nc.sbuf_tensor("mtb", [128, 128], BF16) as mtb:
                    sVb.wait(ve)
                    ve.tensor_copy(mtb[:, :], mtile[:, :])
                    sVb.inc(ve.nop())
                    sVb.wait(te)
                    sTb.inc(te.transpose(psT2[:, :], mtb[:, :], idn[:, :]))
                    sTb.wait(ve)
                    ve.tensor_copy(mrow[0:1, :], psT2[0:1, :])
                    sVb.inc(ve.tensor_reduce(
                        SR(0, 32),
                        mrow[0:1, 0:64].rearrange("a (h t) -> a h t", t=2),
                        AX.X, ALU.add))
                    ve.tensor_scalar(SR(0, 32), SR(0, 32), -RSQ2, None, ALU.mult)
                    sVb.inc(ve.nop())
                    sVb.wait(te)
                    te.matmul(psMQ[:, 0:32], onesrow[0:1, :], SR(0, 32),
                              start=True, stop=True)
                    sTb.tinc()
                    sTb.wait(ve)
                    sVb.inc(ve.tensor_copy(bdca[:, :], psMQ[:, 0:32]))
                    sVb.wait(te)
        barrier(ve, te, sc, gp, sy)
        wstack.close()

        if DEBUG_STOP == "sl1b":
            emit_zero_output_and_stop()
            sl1stack.close()
            wstack.close()
            cstack.close()
            flush()
            return nc, taps

        t_ = tap("hT", [128, T], BF16)
        if t_ is not None:
            s_ = Ctr(); bwait(sy)
            s_.dma(sy.dma_start(out=t_[:, :], in_=hT[:, :]))
            s_.wait(sy)

        sl1c = ExitStack()
        with sl1c:
            EC = sl1c.enter_context
            p1a = EC(nc.sbuf_tensor("p1a", [128, 2, T], BF16))
            p1b = EC(nc.sbuf_tensor("p1b", [128, 2, T], BF16))
            rp2 = EC(nc.sbuf_tensor("rp2", [128, 2, 1024], BF16))
            simb4 = EC(nc.sbuf_tensor("simb4", [128, 4, T], F32))
            bst = EC(nc.sbuf_tensor("bst", [128, 8, 4], F32))
            selT = EC(nc.sbuf_tensor("selT", [128, KD, 512], BF16))
            ksl = EC(nc.sbuf_tensor("ksl", [2, 2, T], BF16))
            qsl = EC(nc.sbuf_tensor("qsl", [2, 2, 512], BF16))
            wts = EC(nc.sbuf_tensor("wts", [128, 2, 1024], BF16))
            aou = EC(nc.sbuf_tensor("aou", [64, 512], BF16))
            zinv = EC(nc.sbuf_tensor("zinv", [32, 512], F32))
            aor = aou[0:3, :]
            sSc = Ctr(); sVc = Ctr(); sTc = Ctr(); sKc2 = [Ctr(), Ctr()]
            bwait(te, ve, sc, sy)
            for sb in range(4):
                icols = slice(512 * sb, 512 * sb + 512)
                pstk = ExitStack()
                psSM = pstk.enter_context(nc.psum_tensor(f"psSM{sb}", [128, 2048], F32))
                ppstk = ExitStack()
                psPP = ppstk.enter_context(nc.psum_tensor(f"psPP{sb}", [128, 2, 1024], F32))
                for bl in range(4):
                    blk = 4 * sb + bl
                    i0 = 128 * blk
                    relu_marks = []
                    for t in range(64):
                        ia, ib = i0 + 2 * t, i0 + 2 * t + 1
                        pa, pb_ = p1a[:, t % 2, :], p1b[:, t % 2, :]
                        if t >= 2:
                            sTc.wait(sc, relu_marks[t - 2])
                            sTc.wait(ve, relu_marks[t - 2])
                        sSc.inc(sc.activation(pa, hT[:, :], AF.Relu,
                                              bias=Hb[:, ia:ia + 1], scale=-1.0))
                        sVc.inc(ve.tensor_scalar(pb_, hT[:, :], Hb[:, ib:ib + 1],
                                                 0.0, ALU.subtract, ALU.min))
                        sSc.wait(te); sVc.wait(te)
                        jh_marks = []
                        for jh in range(2):
                            pp = psPP[:, jh, :]
                            j0 = 1024 * jh
                            for nn2 in range(2):
                                js = slice(512 * nn2, 512 * nn2 + 512)
                                jsrc = slice(j0 + 512 * nn2, j0 + 512 * nn2 + 512)
                                te.matmul(pp[0:64, js], w2t[:, 0, :], pa[:, jsrc],
                                          start=True, stop=True, tile_position=(0, 0))
                                te.matmul(pp[64:128, js], w2t[:, 1, :], pb_[:, jsrc],
                                          start=True, stop=True, tile_position=(0, 64))
                            jh_marks.append(sTc.tinc())
                        for jh in range(2):
                            j0 = 1024 * jh
                            sTc.wait(sc, jh_marks[jh])
                            sSc.inc(sc.activation(rp2[:, jh, :], psPP[:, jh, :],
                                                  AF.Relu, bias=smalls[:, 1:2]))
                            sSc.wait(te)
                            for nn2 in range(2):
                                js = slice(512 * nn2, 512 * nn2 + 512)
                                jsim = slice(j0 + 512 * nn2, j0 + 512 * nn2 + 512)
                                te.matmul(psSM[:, jsim], w3s[:, t, :], rp2[:, jh, js],
                                          start=(t == 0), stop=(t == 63))
                        sTc.tinc()
                        relu_marks.append(jh_marks[1])
                    sTc.wait(ve)
                    simv = simb4[:, bl, :]
                    ve.tensor_scalar(simv, psSM[:, :], smalls[:, 4:5], None, ALU.add)
                    sVc.inc(ve.tensor_tensor(simv[:, i0:i0 + 128], simv[:, i0:i0 + 128],
                                             noti[:, :], ALU.mult))
                    if blk == 0:
                        tsim = tap("sim0", [128, T], F32)
                        if tsim is not None:
                            sVc.wait(sy)
                            s_ = Ctr()
                            s_.dma(sy.dma_start(out=tsim[:, :], in_=simv))
                            s_.wait(ve)
                ppstk.close()
                LO, HI, MID, CNT, G, T5, T6 = range(7)
                def B4(i):
                    return bst[:, i, :]
                for bl in range(4):
                    ve.tensor_reduce(bst[:, HI, bl:bl + 1], simb4[:, bl, :], AX.X, ALU.max)
                    ve.tensor_reduce(bst[:, LO, bl:bl + 1], simb4[:, bl, :], AX.X, ALU.min)
                ve.tensor_scalar(B4(T5), B4(HI), -1.0, None, ALU.mult)
                ve.tensor_tensor(B4(T5), B4(T5), B4(HI), ALU.max)
                ve.tensor_scalar(B4(T5), B4(T5), 1e-5, 1e-20, ALU.mult, ALU.add)
                ve.tensor_tensor(B4(HI), B4(HI), B4(T5), ALU.add)
                jview = wts[:, :, :].rearrange("p a b -> p (a b)")
                for it in range(NBIS):
                    ve.tensor_tensor(B4(MID), B4(LO), B4(HI), ALU.add)
                    ve.tensor_scalar(B4(MID), B4(MID), 0.5, None, ALU.mult)
                    for bl in range(4):
                        ve.tensor_scalar(jview, simb4[:, bl, :], bst[:, MID, bl:bl + 1],
                                         0.0, ALU.is_ge, ALU.add,
                                         accum_out=bst[:, CNT, bl:bl + 1])
                    ve.tensor_scalar(B4(G), B4(CNT), float(KSP) - 0.5, None, ALU.is_ge)
                    ve.tensor_tensor(B4(T5), B4(MID), B4(LO), ALU.subtract)
                    ve.tensor_tensor(B4(T5), B4(T5), B4(G), ALU.mult)
                    ve.tensor_tensor(B4(LO), B4(LO), B4(T5), ALU.add)
                    ve.tensor_tensor(B4(T5), B4(MID), B4(HI), ALU.subtract)
                    ve.tensor_tensor(B4(T6), B4(T5), B4(G), ALU.mult)
                    ve.tensor_tensor(B4(T5), B4(T5), B4(T6), ALU.subtract)
                    ve.tensor_tensor(B4(HI), B4(HI), B4(T5), ALU.add)
                for bl in range(4):
                    ve.tensor_scalar(jview, simb4[:, bl, :], bst[:, LO, bl:bl + 1],
                                     None, ALU.is_ge)
                    sVc.inc(ve.nop())
                    sVc.wait(te)
                    with nc.psum_tensor(f"psSL_{sb}_{bl}", [128, 128], BF16) as psSL:
                        for jc in range(KD):
                            te.transpose(
                                psSL[:, :],
                                wts[:, jc // 8, 128 * (jc % 8):128 * (jc % 8) + 128],
                                idn[:, :])
                            sTc.tinc()
                            sTc.wait(ve)
                            sVc.inc(ve.tensor_copy(selT[:, jc, 128 * bl:128 * bl + 128],
                                                   psSL[:, :]))
                            sVc.wait(te)
                pstk.close()
                if sb == 0:
                    tsel = tap("sel", [128, KD * 512], BF16)
                    if tsel is not None:
                        sVc.wait(sy)
                        s_ = Ctr()
                        s_.dma(sy.dma_start(
                            out=tsel[:, :],
                            in_=selT[:, :, :].rearrange("p k n -> p (k n)")))
                        s_.wait(sy)
                with nc.psum_tensor(f"psSC_{sb}", [128, 4, 512], F32) as psSC, \
                     nc.psum_tensor(f"psAO_{sb}", [128, 512], F32) as psAO, \
                     nc.psum_tensor(f"psZP_{sb}", [32, 512], F32) as psZP, \
                     nc.psum_tensor(f"psOA_{sb}", [64, 512], F32) as psOA, \
                     nc.psum_tensor(f"psZD_{sb}", [64, 512], F32) as psZD:
                    kq_marks = {}
                    def fetch_head(h):
                        bi = h % 2
                        c = sKc2[bi]
                        c.dma(sy.dma_start(out=ksl[0:2, bi, :],
                                           in_=kqv[64 + 2 * h:64 + 2 * h + 2, :]))
                        c.dma(sy.dma_start(out=qsl[0:2, bi, :],
                                           in_=kqv[2 * h:2 * h + 2, icols]))
                        kq_marks[h] = c.n
                    sTc.wait(sy)
                    fetch_head(0)
                    fetch_head(1)
                    for h in range(NHA):
                        bi = h % 2
                        sKc2[bi].wait(te, kq_marks[h])
                        for jt2 in range(KD // 2):
                            par = jt2 % 2
                            for u in range(2):
                                jt = 2 * jt2 + u
                                te.matmul(psSC[:, 2 * par + u, :],
                                          ksl[0:2, bi, 128 * jt:128 * jt + 128],
                                          qsl[0:2, bi, :], start=True, stop=True)
                            sTc.tinc()
                            sTc.wait(sc)
                            sSc.inc(sc.activation(
                                wts[:, par, :],
                                psSC[:, 2 * par:2 * par + 2, :].rearrange("p a b -> p (a b)"),
                                AF.Exp, scale=RSQ2, bias=bdca[:, h:h + 1]))
                            sSc.wait(ve)
                            sVc.inc(ve.tensor_tensor(
                                wts[:, par, :], wts[:, par, :],
                                selT[:, 2 * jt2:2 * jt2 + 2, :].rearrange("p a b -> p (a b)"),
                                ALU.mult))
                            sVc.wait(te)
                            for u in range(2):
                                jt = 2 * jt2 + u
                                te.matmul(psAO[0:3, :], vhat[:, jt, 3 * h:3 * h + 3],
                                          wts[:, par, 512 * u:512 * u + 512],
                                          start=(jt == 0), stop=(jt == KD - 1))
                            sTc.tinc()
                        aomark = sTc.tinc()
                        if h + 2 < NHA:
                            sTc.wait(sy, aomark)
                            fetch_head(h + 2)
                        sTc.wait(ve)
                        sVc.inc(ve.tensor_copy(aor, psAO[0:3, :]))
                        sVc.wait(te)
                        te.matmul(psZP[:, :], basis[0:1, h, :], aor[0:1, :],
                                  start=(h == 0), stop=(h == NHA - 1))
                        te.matmul(psOA[:, :], sel64[:, h, :], aor,
                                  start=(h == 0), stop=(h == NHA - 1))
                        sTc.tinc()
                        sTc.wait(ve)
                    sTc.wait(ve)
                    ve.reciprocal(zinv[:, :], psZP[:, :])
                    ve.tensor_copy(aou[:, :], psOA[:, :])
                    sVc.inc(ve.nop())
                    sVc.wait(te)
                    te.matmul(psZD[:, :], dupm[:, :], zinv[:, :], start=True, stop=True)
                    sTc.tinc()
                    sTc.wait(ve)
                    sVc.inc(ve.tensor_tensor(out_aT[:, icols], aou[:, :], psZD[:, :],
                                             ALU.mult))
                    sVc.wait(te)
        barrier(ve, te, sc, sy)

        if DEBUG_STOP == "sl1c":
            emit_zero_output_and_stop()
            sl1stack.close()
            cstack.close()
            flush()
            return nc, taps

        sl1d = ExitStack()
        with sl1d:
            ED = sl1d.enter_context
            x0c2 = ED(nc.sbuf_tensor("x0c2", [128, 512], F32))
            x1c = ED(nc.sbuf_tensor("x1c", [128, 512], F32))
            x1cb = ED(nc.sbuf_tensor("x1cb", [128, 512], BF16))
            sq2 = ED(nc.sbuf_tensor("sq2", [128, 512], F32))
            psO = ED(nc.psum_tensor("psO", [128, 2, 512], F32))
            psBC2 = ED(nc.psum_tensor("psBC2", [128, 1024], F32))
            psS1b = ED(nc.psum_tensor("psS1b", [1, 512], F32))
            psS2b = ED(nc.psum_tensor("psS2b", [1, 512], F32))
            sVd = Ctr(); sTd = Ctr(); sSd = Ctr(); sXd = Ctr(); sStd = Ctr()
            t_x1 = tap("x1", [D, O], F32)
            bwait(te, ve, sc, sy)
            for tc in range(4):
                t0 = 512 * tc
                own = tc < 2
                src = xT_own if own else xT_oth
                col0 = (tc % 2) * 512
                sVd.wait(te); sSd.wait(te)
                te.matmul(psBC2[:, 0:512], onesrow[:, :], ROW(2, t0, 512),
                          start=True, stop=True)
                te.matmul(psBC2[:, 512:1024], onesrow[:, :], ROW(3, t0, 512),
                          start=True, stop=True)
                sTd.tinc()
                for dc in range(KD):
                    sXd.dma(sy.dma_start(out=x0c2[:, :],
                                         in_=src[128 * dc:128 * dc + 128, col0:col0 + 512]))
                    te.matmul(psO[:, dc % 2, :], wot[:, 128 * dc:128 * dc + 128],
                              out_aT[:, t0:t0 + 512], start=True, stop=True)
                    tdm = sTd.tinc()
                    sXd.wait(ve); sTd.wait(ve, tdm)
                    if dc > 0 or tc > 0:
                        sStd.wait(ve, sStd.n)
                        sSd.wait(ve)
                    ve.tensor_tensor(x1c[:, :], x0c2[:, :], psBC2[:, 0:512], ALU.mult)
                    ve.tensor_tensor(x1c[:, :], x1c[:, :], psBC2[:, 512:1024], ALU.subtract)
                    ve.tensor_tensor(x1c[:, :], x1c[:, :], x0c2[:, :], ALU.add)
                    ve.tensor_tensor(x1c[:, :], x1c[:, :], psO[:, dc % 2, :], ALU.add)
                    ve.tensor_scalar(x1c[:, :], x1c[:, :], outpk[:, dc:dc + 1], None, ALU.add)
                    if not own:
                        ve.tensor_copy(x1cb[:, :], x1c[:, :])
                    sVd.inc(ve.nop())
                    sVd.wait(sy); sVd.wait(sc)
                    if own:
                        sStd.dma(sy.dma_start(out=x1of[128 * dc:128 * dc + 128, col0:col0 + 512],
                                              in_=x1c[:, :]))
                        if t_x1 is not None:
                            sStd.dma(sy.dma_start(
                                out=t_x1[128 * dc:128 * dc + 128, col0:col0 + 512],
                                in_=x1c[:, :]))
                    else:
                        sStd.dma(sy.dma_start(out=x1pb[128 * dc:128 * dc + 128, col0:col0 + 512],
                                              in_=x1cb[:, :]))
                    sSd.inc(sc.activation(sq2[:, :], x1c[:, :], AF.Square))
                    sVd.wait(te)
                    te.matmul(psS1b[:, :], onescol[:, :], x1c[:, :],
                              start=(dc == 0), stop=(dc == KD - 1))
                    sSd.wait(te)
                    te.matmul(psS2b[:, :], onescol[:, :], sq2[:, :],
                              start=(dc == 0), stop=(dc == KD - 1))
                    sTd.tinc()
                sTd.wait(ve)
                m, v, t1 = SR(0), SR(1), SR(2)
                ve.tensor_scalar(m, psS1b[:, :], 1.0 / D, None, ALU.mult)
                ve.tensor_scalar(v, psS2b[:, :], 1.0 / D, None, ALU.mult)
                ve.tensor_tensor(t1, m, m, ALU.mult)
                sVd.inc(ve.tensor_tensor(v, v, t1, ALU.subtract))
                sVd.wait(sc)
                sSd.inc(sc.activation(t1, v, AF.Sqrt, bias=smalls[0:1, 13:14]))
                sSd.wait(ve)
                ve.reciprocal(ROW(0, t0, 512), t1)
                ve.tensor_tensor(ROW(1, t0, 512), m, ROW(0, t0, 512), ALU.mult)
                sVd.inc(ve.nop())
            sStd.wait(sy)
        barrier(ve, te, sc, sy)



        if DEBUG_STOP == "sl1":
            emit_zero_output_and_stop()
            sl1stack.close()
            cstack.close()
            flush()
            return nc, taps

        def emit_ffn(tag, tokens, src_f32, src_b16, rowA, rowB, want_stats,
                     dst_f32, dst_b16, to_output):
            HT = 512
            sTf = Ctr(); sVf = Ctr(); sSf = Ctr(); sWf2 = [Ctr(), Ctr()]; sXf = Ctr()
            sDf2 = [Ctr(), Ctr()]; sOf = Ctr()
            es = ExitStack()
            with es:
                sy.wait_ge(sFW.sem, FW_DONE)
                EF = es.enter_context
                xnf = EF(nc.sbuf_tensor(f"xnf{tag}", [128, KD, HT], BF16))
                hbuf = EF(nc.sbuf_tensor(f"hbuf{tag}", [128, 64, HT], BF16))
                wgs = EF(nc.sbuf_tensor(f"wgs{tag}", [128, 2, KD, 256], BF16))
                wus = EF(nc.sbuf_tensor(f"wus{tag}", [128, 2, KD, 256], BF16))
                xrc = EF(nc.sbuf_tensor(f"xrc{tag}", [128, 512], F32))
                xrcb = EF(nc.sbuf_tensor(f"xrcb{tag}", [128, 512], BF16))
                stmp = EF(nc.sbuf_tensor(f"stmp{tag}", [128, HT], BF16))
                sqf = EF(nc.sbuf_tensor(f"sqf{tag}", [128, 512], F32))
                dts = EF(nc.sbuf_tensor(f"dts{tag}", [128, 2, 512], BF16))
                for hf in range(tokens // HT):
                    tok0 = HT * hf
                    own = (tok0 < O)
                    gcol0 = tok0 % O
                    with nc.psum_tensor(f"psBCf{tag}{hf}", [128, 1024], F32) as psBCf:
                        bwait(te, ve, sy, sc, gp)
                        sVf.wait(te)
                        te.matmul(psBCf[:, 0:512], onesrow[:, :], ROW(rowA, tok0, 512),
                                  start=True, stop=True)
                        te.matmul(psBCf[:, 512:1024], onesrow[:, :], ROW(rowB, tok0, 512),
                                  start=True, stop=True)
                        sTf.tinc()
                        sTf.wait(ve)
                        for k in range(KD):
                            src = src_f32 if own else src_b16
                            sXf.dma(sy.dma_start(
                                out=(xrc[:, :] if own else xrcb[:, :]),
                                in_=src[128 * k:128 * k + 128, gcol0:gcol0 + 512]))
                            sXf.wait(ve)
                            xin = xrc[:, :] if own else xrcb[:, :]
                            ve.tensor_tensor(sqf[:, :], xin, psBCf[:, 0:512], ALU.mult)
                            ve.tensor_tensor(xnf[:, k, :], sqf[:, :],
                                             psBCf[:, 512:1024], ALU.subtract)
                            sVf.inc(ve.nop())
                            sVf.wait(sy)
                        sVf.wait(te)
                    with nc.psum_tensor(f"psG{tag}{hf}", [128, 512], F32) as psG, \
                         nc.psum_tensor(f"psU{tag}{hf}", [128, 512], F32) as psU:
                        gu_marks = []
                        w_marks = {}

                        def prefetch(si, buf):
                            c0 = 256 * si
                            c = sWf2[buf]
                            for k in range(KD):
                                c.dma(sy.dma_start(out=wgs[:, buf, k, :],
                                                   in_=gup16[128 * k:128 * k + 128,
                                                             c0:c0 + 256]))
                                c.dma(sy.dma_start(out=wus[:, buf, k, :],
                                                   in_=gup16[128 * k:128 * k + 128,
                                                             INTER + c0:INTER + c0 + 256]))
                            w_marks[si] = c.n
                        if hf == 0 and tag == "f1":
                            sy.wait_ge(sFW.sem, FW_DONE)
                        prefetch(0, 0)
                        for si in range(32):
                            if si + 1 < 32:
                                if si >= 1:
                                    sTf.wait(sy, gu_marks[si - 1])
                                prefetch(si + 1, (si + 1) % 2)
                            sWf2[si % 2].wait(te, w_marks[si])
                            for fi in range(2):
                                fc = 2 * si + fi
                                fcol = 128 * fi
                                for k in range(KD):
                                    te.matmul(psG[:, :], wgs[:, si % 2, k, fcol:fcol + 128],
                                              xnf[:, k, :], start=(k == 0), stop=(k == KD - 1))
                                for k in range(KD):
                                    te.matmul(psU[:, :], wus[:, si % 2, k, fcol:fcol + 128],
                                              xnf[:, k, :], start=(k == 0), stop=(k == KD - 1))
                                sTf.tinc()
                                sTf.wait(sc)
                                sSf.inc(sc.activation(stmp[:, :], psG[:, :], AF.Silu))
                                sSf.wait(ve)
                                sVf.inc(ve.tensor_tensor(hbuf[:, fc, :], stmp[:, :],
                                                         psU[:, :], ALU.mult))
                                sVf.wait(te)
                            gu_marks.append(sTf.n)
                    for g in range(4):
                        with nc.psum_tensor(f"psD{tag}{hf}{g}", [128, 4, 512], F32) as psD:
                            dmarks = []
                            dl_marks = {}
                            sDf2[0].dma(sy.dma_start(out=dts[:, 0, :],
                                                     in_=dn16[0:128, 512 * g:512 * g + 512]))
                            dl_marks[0] = sDf2[0].n
                            for ic in range(64):
                                if ic + 1 < 64:
                                    if ic >= 1:
                                        sTf.wait(sy, dmarks[ic - 1])
                                    c = sDf2[(ic + 1) % 2]
                                    c.dma(sy.dma_start(
                                        out=dts[:, (ic + 1) % 2, :],
                                        in_=dn16[128 * (ic + 1):128 * (ic + 1) + 128,
                                                 512 * g:512 * g + 512]))
                                    dl_marks[ic + 1] = c.n
                                sDf2[ic % 2].wait(te, dl_marks[ic])
                                for dci in range(4):
                                    te.matmul(psD[:, dci, :],
                                              dts[:, ic % 2, 128 * dci:128 * dci + 128],
                                              hbuf[:, ic, :], start=(ic == 0), stop=(ic == 63))
                                dmarks.append(sTf.tinc())
                            for dci in range(4):
                                dc = 4 * g + dci
                                src = src_f32 if own else src_b16
                                sOf.wait(sy)
                                sXf.dma(sy.dma_start(
                                    out=(xrc[:, :] if own else xrcb[:, :]),
                                    in_=src[128 * dc:128 * dc + 128, gcol0:gcol0 + 512]))
                                sXf.wait(ve)
                                sTf.wait(ve)
                                if dci > 0:
                                    sOf.wait(ve, sOf.n)
                                    sSf.wait(ve)
                                xin = xrc[:, :] if own else xrcb[:, :]
                                ve.tensor_tensor(xrc[:, :], xin, psD[:, dci, :], ALU.add)
                                if (not own) and (dst_b16 is not None):
                                    ve.tensor_copy(xrcb[:, :], xrc[:, :])
                                sVf.inc(ve.nop())
                                sVf.wait(sy); sVf.wait(sc)
                                if to_output:
                                    sOf.dma(sy.dma_start(
                                        out=y[128 * dc:128 * dc + 128, gcol0:gcol0 + 512],
                                        in_=xrc[:, :]))
                                elif own:
                                    sOf.dma(sy.dma_start(
                                        out=dst_f32[128 * dc:128 * dc + 128, gcol0:gcol0 + 512],
                                        in_=xrc[:, :]))
                                else:
                                    sOf.dma(sy.dma_start(
                                        out=dst_b16[128 * dc:128 * dc + 128, gcol0:gcol0 + 512],
                                        in_=xrcb[:, :]))
                    sOf.wait(sy)
                barrier(ve, te, sc, sy)

        sl1stack.close()
        cstack.close()
        emit_ffn("f1", 2048, x1of, x1pb, 0, 1, False, x2of, x2pb, False)

        sl2s = ExitStack()
        with sl2s:
            ES2 = sl2s.enter_context
            xc3 = ES2(nc.sbuf_tensor("xc3", [128, 512], F32))
            xc3b = ES2(nc.sbuf_tensor("xc3b", [128, 512], BF16))
            sq3 = ES2(nc.sbuf_tensor("sq3", [128, 2, 512], F32))
            psS13 = ES2(nc.psum_tensor("psS13", [1, 512], F32))
            psS23 = ES2(nc.psum_tensor("psS23", [1, 512], F32))
            sX3 = Ctr(); sT3 = Ctr(); sS3 = Ctr(); sV3 = Ctr(); sTp = Ctr()
            t_x2 = tap("x2", [D, O], F32)
            bwait(te, ve, sc, sy)
            for tc in range(4):
                t0 = 512 * tc
                own = tc < 2
                src = x2of if own else x2pb
                col0 = (tc % 2) * 512
                tbase = sT3.n
                for k in range(KD):
                    sTp.wait(sy)
                    sX3.dma(sy.dma_start(out=(xc3[:, :] if own else xc3b[:, :]),
                                         in_=src[128 * k:128 * k + 128, col0:col0 + 512]))
                    if own and t_x2 is not None:
                        sX3.wait(sy)
                        sTp.dma(sy.dma_start(
                            out=t_x2[128 * k:128 * k + 128, col0:col0 + 512],
                            in_=xc3[:, :]))
                    sX3.wait(sc); sX3.wait(te)
                    xin = xc3[:, :] if own else xc3b[:, :]
                    if k >= 2:
                        sT3.wait(sc, tbase + k - 1)
                    sS3.inc(sc.activation(sq3[:, k % 2, :], xin, AF.Square))
                    if own:
                        te.matmul(psS13[:, :], onescol[:, :], xin,
                                  start=(k == 0), stop=(k == KD - 1))
                    else:
                        te.matmul(psS13[:, :], onescol_b[:, :], xin,
                                  start=(k == 0), stop=(k == KD - 1))
                    sS3.wait(te)
                    te.matmul(psS23[:, :], onescol[:, :], sq3[:, k % 2, :],
                              start=(k == 0), stop=(k == KD - 1))
                    sT3.tinc()
                    sT3.wait(sy)
                sT3.wait(ve)
                m, v, t1 = SR(0), SR(1), SR(2)
                ve.tensor_scalar(m, psS13[:, :], 1.0 / D, None, ALU.mult)
                ve.tensor_scalar(v, psS23[:, :], 1.0 / D, None, ALU.mult)
                ve.tensor_tensor(t1, m, m, ALU.mult)
                sV3.inc(ve.tensor_tensor(v, v, t1, ALU.subtract))
                sV3.wait(sc)
                sS3.inc(sc.activation(t1, v, AF.Sqrt, bias=smalls[0:1, 13:14]))
                sS3.wait(ve)
                ve.reciprocal(ROW(2, t0, 512), t1)
                ve.tensor_tensor(ROW(3, t0, 512), m, ROW(2, t0, 512), ALU.mult)
                sV3.inc(ve.nop())
                sV3.wait(te)
        barrier(ve, te, sc, sy)


        if DEBUG_STOP == "sl2":
            emit_zero_output_and_stop()
            flush()
            return nc, taps

        sl3stack = ExitStack()
        qb = sl3stack.enter_context(nc.sbuf_tensor("qb", [128, 16, O], BF16))
        sl3a = ExitStack()
        with sl3a:
            E3 = sl3a.enter_context
            xnb = E3(nc.sbuf_tensor("xnb", [128, KD, T], BF16))
            xrc3 = E3(nc.sbuf_tensor("xrc3a", [128, 512], F32))
            xrc3b = E3(nc.sbuf_tensor("xrc3b", [128, 512], BF16))
            was = E3(nc.sbuf_tensor("was", [128, 2, KD, 512], BF16))
            kvst = E3(nc.sbuf_tensor("kvst", [128, T], BF16))
            sTg = Ctr(); sVg = Ctr(); sSg = Ctr(); sXg = Ctr(); sWg2 = [Ctr(), Ctr()]; sKg = Ctr()
            bwait(te, ve, sc, sy, gp)
            with nc.psum_tensor("psBC3", [128, 1024], F32) as psBC3:
                for tc in range(4):
                    t0 = 512 * tc
                    own = tc < 2
                    src = x2of if own else x2pb
                    col0 = (tc % 2) * 512
                    sVg.wait(te)
                    te.matmul(psBC3[:, 0:512], onesrow[:, :], ROW(2, t0, 512),
                              start=True, stop=True)
                    te.matmul(psBC3[:, 512:1024], onesrow[:, :], ROW(3, t0, 512),
                              start=True, stop=True)
                    sTg.tinc()
                    sTg.wait(ve)
                    for k in range(KD):
                        sXg.dma(sy.dma_start(out=(xrc3[:, :] if own else xrc3b[:, :]),
                                             in_=src[128 * k:128 * k + 128, col0:col0 + 512]))
                        sXg.wait(ve)
                        xin = xrc3[:, :] if own else xrc3b[:, :]
                        ve.tensor_tensor(xrc3[:, :], xin, psBC3[:, 0:512], ALU.mult)
                        ve.tensor_tensor(xnb[:, k, t0:t0 + 512], xrc3[:, :],
                                         psBC3[:, 512:1024], ALU.subtract)
                        sVg.inc(ve.nop())
                        sVg.wait(sy)
                    sVg.wait(te)
            with nc.psum_tensor("psQ", [128, 2, 512], F32) as psQ:
                aw_marks = {}
                def prefetch_a(si, buf):
                    c0 = 512 * si
                    c = sWg2[buf]
                    for k in range(KD):
                        c.dma(gp.dma_start(out=was[:, buf, k, :],
                                           in_=bWattn[128 * k:128 * k + 128, c0:c0 + 512]))
                    aw_marks[si] = c.n
                a_marks = []
                prefetch_a(0, 0)
                for si in range(12):
                    if si + 1 < 12:
                        if si >= 1:
                            sTg.wait(gp, a_marks[si - 1])
                        prefetch_a(si + 1, (si + 1) % 2)
                    sWg2[si % 2].wait(te, aw_marks[si])
                    for oi in range(4):
                        oc = 4 * si + oi
                        is_q = oc < 8 or (16 <= oc < 24)
                        ntok = O if is_q else T
                        for q in range(ntok // 512):
                            qs = slice(512 * q, 512 * q + 512)
                            for k in range(KD):
                                te.matmul(psQ[:, q % 2, :],
                                          was[:, si % 2, k, 128 * oi:128 * oi + 128],
                                          xnb[:, k, qs], start=(k == 0), stop=(k == KD - 1))
                            sTg.tinc()
                            sTg.wait(ve)
                            if is_q:
                                qi = oc if oc < 8 else (oc - 8)
                                sVg.inc(ve.tensor_copy(qb[:, qi, qs], psQ[:, q % 2, :]))
                            else:
                                sVg.inc(ve.tensor_copy(kvst[:, qs], psQ[:, q % 2, :]))
                            sVg.wait(te)
                        if not is_q:
                            if 8 <= oc < 16:
                                r0 = 128 * (oc - 8)
                            elif 24 <= oc < 32:
                                r0 = 1024 + 128 * (oc - 24)
                            else:
                                r0 = 2048 + 128 * (oc - 32)
                            sVg.wait(sy)
                            sKg.dma(sy.dma_start(out=kvs[r0:r0 + 128, :], in_=kvst[:, :]))
                            sKg.wait(ve)
                    a_marks.append(sTg.n)
            sKg.wait(sy)
        barrier(ve, te, sc, sy, gp)

        obT = sl3stack.enter_context(nc.sbuf_tensor("obT", [128, KD, O], BF16))
        sl3b = ExitStack()
        with sl3b:
            E4 = sl3b.enter_context
            kh1 = E4(nc.sbuf_tensor("kh1", [65, T], BF16))
            kh2 = E4(nc.sbuf_tensor("kh2", [65, T], BF16))
            qh1 = E4(nc.sbuf_tensor("qh1", [65, O], BF16))
            qh2 = E4(nc.sbuf_tensor("qh2", [65, O], BF16))
            vtok = E4(nc.sbuf_tensor("vtok", [128, KD, 128], BF16))
            wtb = E4(nc.sbuf_tensor("wtb", [128, 2, 1024], BF16))
            aqb = E4(nc.sbuf_tensor("aqb", [64, O], BF16))
            mk64 = E4(nc.sbuf_tensor("mk64", [64, 2], BF16))
            mk64f = E4(nc.sbuf_tensor("mk64f", [64, 2], F32))
            bdr = E4(nc.sbuf_tensor("bdr", [1, O], F32))
            zrow = E4(nc.sbuf_tensor("zrow", [1, O], F32))
            zb = E4(nc.sbuf_tensor("zb", [128, O], F32))
            sTh = Ctr(); sVh = Ctr(); sSh = Ctr(); sKh = Ctr()
            bwait(te, ve, sc, sy, gp)
            for h in range(NHB):
                q0 = 64 * (h % 2)
                sVh.wait(te)
                ve.tensor_copy(qh1[0:64, :], qb[q0:q0 + 64, h // 2, :])
                ve.tensor_scalar(qh2[0:64, :], qb[q0:q0 + 64, 8 + h // 2, :],
                                 smalls[0:64, 6:7], None, ALU.mult)
                ve.memset(qh2[64:65, :], 0.0)
                sVh.inc(ve.nop())
                sVh.wait(sy)
                sKh.dma(sy.dma_start(out=kh1[0:64, :], in_=kvs[64 * h:64 * h + 64, :]))
                sKh.dma(sy.dma_start(out=kh2[0:64, :],
                                     in_=kvs[1024 + 64 * h:1024 + 64 * h + 64, :]))
                for jc in range(KD):
                    sKh.dma(sy.dma_start(out=vtok[:, jc, :],
                                         in_=kvs[2048 + 128 * h:2048 + 128 * h + 128,
                                                 128 * jc:128 * jc + 128],
                                         transpose=True))
                sKh.wait(ve)
                ve.memset(kh1[64:65, :], 1.0)
                ve.memset(kh2[64:65, :], 0.0)
                sVh.inc(ve.nop())
                with nc.psum_tensor(f"psB3_{h}", [1, 512], F32) as psB3:
                    sVh.wait(ve)
                    ve.tensor_reduce(mk64f[:, 0:1], kh1[0:64, :], AX.X, ALU.max,
                                     apply_absolute_value=True)
                    ve.tensor_reduce(mk64f[:, 1:2], kh2[0:64, :], AX.X, ALU.max,
                                     apply_absolute_value=True)
                    sVh.inc(ve.tensor_copy(mk64[:, :], mk64f[:, :]))
                    sVh.wait(sc)
                    sSh.inc(sc.activation(aqb[:, :], qh1[0:64, :], AF.Abs))
                    sSh.wait(te); sVh.wait(te)
                    for q in range(2):
                        te.matmul(psB3[:, :], mk64[:, 0:1], aqb[:, 512 * q:512 * q + 512],
                                  start=True, stop=True)
                        sTh.tinc()
                        sTh.wait(ve)
                        sVh.inc(ve.tensor_copy(bdr[0:1, 512 * q:512 * q + 512], psB3[:, :]))
                        sVh.wait(te)
                    sVh.wait(sc)
                    sSh.inc(sc.activation(aqb[:, :], qh2[0:64, :], AF.Abs))
                    sSh.wait(te)
                    for q in range(2):
                        te.matmul(psB3[:, :], mk64[:, 1:2], aqb[:, 512 * q:512 * q + 512],
                                  start=True, stop=True)
                        sTh.tinc()
                        sTh.wait(ve)
                        ve.tensor_tensor(bdr[0:1, 512 * q:512 * q + 512],
                                         bdr[0:1, 512 * q:512 * q + 512], psB3[:, :], ALU.add)
                        sVh.inc(ve.tensor_scalar(qh1[64:65, 512 * q:512 * q + 512],
                                                 bdr[0:1, 512 * q:512 * q + 512],
                                                 -1.0, None, ALU.mult))
                        sVh.wait(te)
                with nc.psum_tensor(f"psSC3_{h}", [128, 2, 512], F32) as psSC3, \
                     nc.psum_tensor(f"psZ3_{h}", [1, 1024], F32) as psZ3, \
                     nc.psum_tensor(f"psAO3_{h}", [128, 1024], F32) as psAO3, \
                     nc.psum_tensor(f"psZB_{h}", [128, 1024], F32) as psZB:
                    for jt in range(KD):
                        jsl = slice(128 * jt, 128 * jt + 128)
                        for q in range(2):
                            qs = slice(512 * q, 512 * q + 512)
                            te.matmul(psSC3[:, q, :], kh1[:, jsl], qh1[:, qs],
                                      start=True, stop=False)
                            te.matmul(psSC3[:, q, :], kh2[:, jsl], qh2[:, qs],
                                      start=False, stop=True)
                        sTh.tinc()
                        sTh.wait(sc)
                        sSh.inc(sc.activation(wtb[:, jt % 2, 0:512], psSC3[:, 0, :],
                                              AF.Exp, scale=0.125))
                        sSh.inc(sc.activation(wtb[:, jt % 2, 512:1024], psSC3[:, 1, :],
                                              AF.Exp, scale=0.125))
                        sSh.wait(te)
                        for q in range(2):
                            qs = slice(512 * q, 512 * q + 512)
                            te.matmul(psZ3[:, qs], onescol_b[:, :], wtb[:, jt % 2, qs],
                                      start=(jt == 0), stop=(jt == KD - 1))
                        for q in range(2):
                            qs = slice(512 * q, 512 * q + 512)
                            te.matmul(psAO3[:, qs], vtok[:, jt, :], wtb[:, jt % 2, qs],
                                      start=(jt == 0), stop=(jt == KD - 1))
                        sTh.tinc()
                    sTh.wait(ve)
                    ve.tensor_copy(zrow[:, :], psZ3[:, :])
                    sVh.inc(ve.tensor_scalar(zrow[:, :], zrow[:, :], 1.25, None, ALU.mult))
                    sVh.wait(te)
                    for q in range(2):
                        qs = slice(512 * q, 512 * q + 512)
                        te.matmul(psZB[:, qs], onesrow[:, :], zrow[0:1, qs],
                                  start=True, stop=True)
                    sTh.tinc()
                    sTh.wait(ve)
                    ve.reciprocal(zb[:, :], psZB[:, :])
                    sVh.inc(ve.tensor_tensor(obT[:, h, :], psAO3[:, :], zb[:, :], ALU.mult))
                    sVh.wait(te)
        barrier(ve, te, sc, sy, gp)

        t_ = tap("ob", [D, O], BF16)
        if t_ is not None:
            s_ = Ctr(); bwait(sy)
            for k in range(KD):
                s_.dma(sy.dma_start(out=t_[128 * k:128 * k + 128, :], in_=obT[:, k, :]))
            s_.wait(sy)

        sl3c = ExitStack()
        with sl3c:
            E5 = sl3c.enter_context
            wps = E5(nc.sbuf_tensor("wps", [128, 2, KD, 512], BF16))
            xrc4 = E5(nc.sbuf_tensor("xrc4", [128, 512], F32))
            sq4 = E5(nc.sbuf_tensor("sq4", [128, 512], F32))
            x3c = E5(nc.sbuf_tensor("x3c", [128, 512], F32))
            psP = E5(nc.psum_tensor("psP", [128, 2, 512], F32))
            psS1g = E5(nc.psum_tensor("psS1g", [1, 1024], F32))
            psS2g = E5(nc.psum_tensor("psS2g", [1, 1024], F32))
            sTi = Ctr(); sVi = Ctr(); sSi = Ctr(); sXi = Ctr(); sWi2 = [Ctr(), Ctr()]; sSti = Ctr()
            t_x3 = tap("x3", [D, O], F32)
            bwait(te, ve, sc, sy, gp)
            p_marks = []
            pw_marks = {}

            def prefetch_p(si, buf):
                c0 = 512 * si
                c = sWi2[buf]
                for k in range(KD):
                    c.dma(gp.dma_start(out=wps[:, buf, k, :],
                                       in_=bWproj[128 * k:128 * k + 128, c0:c0 + 512]))
                pw_marks[si] = c.n
            prefetch_p(0, 0)
            for si in range(4):
                if si + 1 < 4:
                    if si >= 1:
                        sTi.wait(gp, p_marks[si - 1])
                    prefetch_p(si + 1, (si + 1) % 2)
                sWi2[si % 2].wait(te, pw_marks[si])
                for di in range(4):
                    dc = 4 * si + di
                    for q in range(2):
                        qs = slice(512 * q, 512 * q + 512)
                        for k in range(KD):
                            te.matmul(psP[:, q % 2, :],
                                      wps[:, si % 2, k, 128 * di:128 * di + 128],
                                      obT[:, k, qs], start=(k == 0), stop=(k == KD - 1))
                        tm = sTi.tinc()
                        sXi.dma(sy.dma_start(out=xrc4[:, :],
                                             in_=x2of[128 * dc:128 * dc + 128, qs]))
                        sXi.wait(ve); sTi.wait(ve, tm)
                        if not (si == 0 and di == 0 and q == 0):
                            sSti.wait(ve, sSti.n)
                            sSi.wait(ve)
                        ve.tensor_tensor(x3c[:, :], xrc4[:, :], psP[:, q % 2, :], ALU.add)
                        sVi.inc(ve.nop())
                        sVi.wait(sy); sVi.wait(sc)
                        sSti.dma(sy.dma_start(out=x3of[128 * dc:128 * dc + 128, qs],
                                              in_=x3c[:, :]))
                        if t_x3 is not None:
                            sSti.dma(sy.dma_start(out=t_x3[128 * dc:128 * dc + 128, qs],
                                                  in_=x3c[:, :]))
                        sSi.inc(sc.activation(sq4[:, :], x3c[:, :], AF.Square))
                        sVi.wait(te)
                        te.matmul(psS1g[:, qs], onescol[:, :], x3c[:, :],
                                  start=(dc == 0), stop=(dc == KD - 1))
                        sSi.wait(te)
                        te.matmul(psS2g[:, qs], onescol[:, :], sq4[:, :],
                                  start=(dc == 0), stop=(dc == KD - 1))
                        sTi.tinc()
                p_marks.append(sTi.n)
            sTi.wait(ve)
            for q in range(2):
                qs = slice(512 * q, 512 * q + 512)
                t0 = 512 * q
                m, v, t1 = SR(0), SR(1), SR(2)
                ve.tensor_scalar(m, psS1g[:, qs], 1.0 / D, None, ALU.mult)
                ve.tensor_scalar(v, psS2g[:, qs], 1.0 / D, None, ALU.mult)
                ve.tensor_tensor(t1, m, m, ALU.mult)
                sVi.inc(ve.tensor_tensor(v, v, t1, ALU.subtract))
                sVi.wait(sc)
                sSi.inc(sc.activation(t1, v, AF.Sqrt, bias=smalls[0:1, 13:14]))
                sSi.wait(ve)
                ve.reciprocal(ROW(0, t0, 512), t1)
                ve.tensor_tensor(ROW(1, t0, 512), m, ROW(0, t0, 512), ALU.mult)
                sVi.inc(ve.nop())
            sSti.wait(sy)
        barrier(ve, te, sc, sy, gp)


        if DEBUG_STOP == "sl3":
            emit_zero_output_and_stop()
            sl3stack.close()
            flush()
            return nc, taps

        sl3stack.close()
        emit_ffn("f2", 1024, x3of, None, 0, 1, False, None, None, True)
        flush()

    return nc, taps


_NC_CACHE = {}


def _host_maps(inputs):
    f32 = np.float32
    T_ = np.ascontiguousarray
    gup = T_(inputs["L0_ffn_w_gate_up"].astype(f32, copy=False).T)
    dnT = T_(inputs["L0_ffn_w_down"].astype(f32, copy=False).T)
    W1T = T_(inputs["L0_attn_a_W1"].astype(f32, copy=False).T)
    W2T = T_(inputs["L0_attn_a_W2"].astype(f32, copy=False).T)
    W3c = T_(inputs["L0_attn_a_W3"].astype(f32, copy=False).reshape(64, 1))
    WqT = T_(inputs["L0_attn_a_Wqkv"].astype(f32, copy=False).T)
    WoT = T_(inputs["L0_attn_a_Wout"].astype(f32, copy=False).T)
    gam = inputs["L0_attn_a_gamma"].astype(f32, copy=False)
    bet = inputs["L0_attn_a_beta"].astype(f32, copy=False)
    bout = inputs["L0_attn_a_bout"].astype(f32, copy=False)
    b1 = T_(inputs["L0_attn_a_b1"].astype(f32, copy=False).reshape(128, 1))
    b2 = inputs["L0_attn_a_b2"].astype(f32, copy=False)
    b2d = T_(np.concatenate([b2, b2]).reshape(128, 1))
    b3 = inputs["L0_attn_a_b3"].astype(f32, copy=False).reshape(1, 1)
    bqkv = inputs["L0_attn_a_bqkv"].astype(f32, copy=False)
    bqkv_p = np.zeros((128, 2), f32)
    bqkv_p[:, 0] = bqkv[0:128]
    bqkv_p[0:64, 1] = bqkv[128:192]

    def pk(vec):
        return T_(vec.reshape(KD, 128).T)
    x = inputs["L0_attn_a_x"].astype(f32, copy=False)
    com = {
        "aW1T": W1T, "ab1": b1, "aW2T": W2T, "ab2d": b2d, "aW3c": W3c, "ab3": b3,
        "aWqkvT": WqT, "abqkv": bqkv_p, "aWoutT": WoT,
        "about_pk": pk(bout), "agam_pk": pk(gam), "abet_pk": pk(bet),
        "bWattn": T_(inputs["L0_attn_b_W_attn"].astype(f32, copy=False)),
        "bWproj": T_(inputs["L0_attn_b_W_proj"].astype(f32, copy=False)),
        "blq1": inputs["L0_attn_b_lambda_q1"].astype(f32, copy=False).reshape(1, D),
        "blk1": inputs["L0_attn_b_lambda_k1"].astype(f32, copy=False).reshape(1, D),
        "blq2": inputs["L0_attn_b_lambda_q2"].astype(f32, copy=False).reshape(1, D),
        "blk2": inputs["L0_attn_b_lambda_k2"].astype(f32, copy=False).reshape(1, D),
        "fgupT": gup, "fdnT": dnT,
    }
    in_maps = []
    for c in range(8):
        b, half = c // 2, c % 2
        xb = x[b]
        m = dict(com)
        m["xT_own"] = T_(xb[1024 * half:1024 * half + 1024].T)
        m["xT_oth"] = T_(xb[1024 * (1 - half):1024 * (1 - half) + 1024].T)
        in_maps.append(m)
    return in_maps


def kernel(**inputs):
    if "std" not in _NC_CACHE:
        _NC_CACHE["std"] = build_program()
    nc, taps = _NC_CACHE["std"]
    in_maps = _host_maps(inputs)
    res = run_bass_kernel_spmd(nc, in_maps, core_ids=list(range(8)))
    out = np.empty((4, 2048, 2048), np.float32)
    for c in range(8):
        b, half = c // 2, c % 2
        out[b, 1024 * half:1024 * half + 1024, :] = res.results[c]["y"].T
    return out
